# version 02
# baseline (speedup 1.0000x reference)
import contextlib
import math

import numpy as np

import concourse.bass as bass
import concourse.mybir as mybir
from concourse.bass_core_rust import IndirectOffsetOnAxis
from concourse.bass_utils import run_bass_kernel_spmd

F32 = mybir.dt.float32
BF16 = mybir.dt.bfloat16
U32 = mybir.dt.uint32
AF = mybir.ActivationFunctionType
ALU = mybir.AluOpType
AX = mybir.AxisListType.X

D = 2048
T = 2048
NT = 16
T4 = 4
DC = 16
H = 16
RD = 64
ND = 128
VD = 128
QHD = ND + RD
QL = 512
KVL = 512
HF = 2816
HC = 22
E = 8
CAP = 2048
NCORE = 8
SCALE_A = 1.0 / math.sqrt(float(QHD))
SCALE_B = 1.0 / math.sqrt(float(VD))
LN_EPS = 1e-5

N_SUB = 4
DBG = False

ENGINES = ("pe", "dve", "act", "pool", "sync")
_ENG_ATTR = {"pe": "tensor", "dve": "vector", "act": "scalar", "pool": "gpsimd",
             "sync": "sync"}


class Sem:
    __slots__ = ("h", "n")

    def __init__(self, h):
        self.h = h
        self.n = 0


class Plan:

    def __init__(self):
        self.items = {e: [] for e in ENGINES}

    def add(self, eng, fn, waits=(), inc=None, by=1):
        if inc is not None:
            inc.n += by
        self.items[eng].append((tuple(waits), fn, inc, by))

    @staticmethod
    def at(*sems):
        return tuple((s, s.n) for s in sems)


def _flush(nc, plan):
    if not any(plan.items[e] for e in ENGINES):
        return
    with nc.Block() as blk:
        for eng in ENGINES:
            items = plan.items[eng]
            if not items:
                continue

            def mk(items=items):
                def body(E):
                    lastw = {}
                    for waits, fn, inc, by in items:
                        for (s, v) in waits:
                            if v > 0 and lastw.get(id(s), -1) < v:
                                E.wait_ge(s.h, v)
                                lastw[id(s)] = v
                        ins = fn(E)
                        if inc is not None:
                            ins.then_inc(inc.h, by)
                return body

            getattr(blk, _ENG_ATTR[eng])(mk())



def _rope_tables():
    inv_freq = 10000.0 ** (-np.arange(0, RD, 2, dtype=np.float64) / RD)
    pos = np.arange(T, dtype=np.float64)
    ang = pos[:, None] * inv_freq[None, :]
    cos = np.cos(ang).astype(np.float32)
    sin = np.sin(ang).astype(np.float32)
    C = np.empty((RD, T), np.float32)
    S = np.empty((RD, T), np.float32)
    C[0::2, :] = cos.T
    C[1::2, :] = cos.T
    S[0::2, :] = -sin.T
    S[1::2, :] = sin.T
    return C, S


def _mask_tiles():
    masks = np.zeros((4, 128, 512), np.float32)
    for j in range(4):
        for p in range(128):
            masks[j, p, 128 * j + p:] = 1.0
    return masks.reshape(4 * 128, 512)


def _swap64():
    P = np.zeros((RD, RD), np.float32)
    for p in range(RD):
        P[p, p ^ 1] = 1.0
    return P



def build_nc(n_sub=N_SUB):
    nc = bass.Bass("TRN2", target_bir_lowering=False)
    st = contextlib.ExitStack()

    xfm_in = nc.dram_tensor("xfm", [D, T], F32, kind="ExternalInput")
    wqa = nc.dram_tensor("wqa", [D, QL], F32, kind="ExternalInput")
    wqb = nc.dram_tensor("wqb", [QL, H * QHD], F32, kind="ExternalInput")
    wkva = nc.dram_tensor("wkva", [D, KVL + RD], F32, kind="ExternalInput")
    wkvb = nc.dram_tensor("wkvb", [KVL, H * (ND + VD)], F32, kind="ExternalInput")
    woa = nc.dram_tensor("woa", [D, D], F32, kind="ExternalInput")
    wq_b = nc.dram_tensor("wq", [D, D], F32, kind="ExternalInput")
    wk_b = nc.dram_tensor("wk", [D, D], F32, kind="ExternalInput")
    wv_b = nc.dram_tensor("wv", [D, D], F32, kind="ExternalInput")
    wob = nc.dram_tensor("wob", [D, D], F32, kind="ExternalInput")
    wg = nc.dram_tensor("wg", [D, E], F32, kind="ExternalInput")
    w1 = nc.dram_tensor("w1", [D, HF], F32, kind="ExternalInput")
    w3 = nc.dram_tensor("w3", [D, HF], F32, kind="ExternalInput")
    w2 = nc.dram_tensor("w2", [HF, D], F32, kind="ExternalInput")
    sw1 = nc.dram_tensor("sw1", [D, HF], F32, kind="ExternalInput")
    sw3 = nc.dram_tensor("sw3", [D, HF], F32, kind="ExternalInput")
    sw2 = nc.dram_tensor("sw2", [HF, D], F32, kind="ExternalInput")
    rankmsk = nc.dram_tensor("rankmsk", [64, E], F32, kind="ExternalInput")
    tokid = nc.dram_tensor("tokid", [128, NT], F32, kind="ExternalInput")
    eslot = nc.dram_tensor("eslot", [128, NT], U32, kind="ExternalInput")

    yfm = nc.dram_tensor("yfm", [D, T], F32, kind="ExternalOutput")
    if DBG:
        dbg_partial = nc.dram_tensor("dbg_partial", [D, T], F32, kind="ExternalOutput")
        dbg_slot = nc.dram_tensor("dbg_slot", [16512, 1], F32, kind="ExternalOutput")
        dbg_xd = nc.dram_tensor("dbg_xd", [CAP, D], BF16, kind="ExternalOutput")
        dbg_oe = nc.dram_tensor("dbg_oe", [D, CAP], BF16, kind="ExternalOutput")
        dbg_ret = nc.dram_tensor("dbg_ret", [T, D], BF16, kind="ExternalOutput")
        dbg_xn = nc.dram_tensor("dbg_xn", [D, T], BF16, kind="ExternalOutput")
        dbg2_oh = nc.dram_tensor("dbg2_oh", [128, E * NT], F32, kind="ExternalOutput")
        dbg2_ohT = nc.dram_tensor("dbg2_ohT", [E, T], F32, kind="ExternalOutput")
        dbg2_incl = nc.dram_tensor("dbg2_incl", [E, T], F32, kind="ExternalOutput")
        dbg2_cnt = nc.dram_tensor("dbg2_cnt", [64, 1], F32, kind="ExternalOutput")
        dbg2_offs = nc.dram_tensor("dbg2_offs", [8, 1], F32, kind="ExternalOutput")
        dbg2_sidx = nc.dram_tensor("dbg2_sidx", [128, NT], U32, kind="ExternalOutput")
        dbg2_retg = nc.dram_tensor("dbg2_retg", [128, NT], U32, kind="ExternalOutput")
        dbg2_gk = nc.dram_tensor("dbg2_gk", [128, NT], F32, kind="ExternalOutput")
        dbg2_lg = nc.dram_tensor("dbg2_lg", [128, E], F32, kind="ExternalOutput")
        dbg2_pexp = nc.dram_tensor("dbg2_pexp", [128, E], F32, kind="ExternalOutput")
        dbg2_rcol = nc.dram_tensor("dbg2_rcol", [128, 8], F32, kind="ExternalOutput")

    xres = [nc.dram_tensor(f"xres{i}", [D, T], F32, kind="Internal")
            for i in range(2)]
    xnfm_d = nc.dram_tensor("xnfm_d", [D, T], BF16, kind="Internal")
    ag_x_in = nc.dram_tensor("ag_x_in", [T, D], BF16, kind="Internal")
    xall = nc.dram_tensor("xall", [NCORE * T, D], BF16, kind="Internal",
                          addr_space="Shared")
    cnt_in = nc.dram_tensor("cnt_in", [E, 1], F32, kind="Internal")
    cnt_all = nc.dram_tensor("cnt_all", [NCORE * E, 1], F32, kind="Internal",
                             addr_space="Shared")
    slot_loc = nc.dram_tensor("slot_loc", [16512, 1], F32, kind="Internal")
    slot_all = nc.dram_tensor("slot_all", [16512, 1], F32, kind="Internal",
                              addr_space="Shared")
    xd_tm_d = nc.dram_tensor("xd_tm_d", [CAP, D], BF16, kind="Internal")
    ag_oe_in = nc.dram_tensor("ag_oe_in", [CAP, D], BF16, kind="Internal")
    oe_all = nc.dram_tensor("oe_all", [NCORE * CAP, D], BF16, kind="Internal",
                            addr_space="Shared")
    partial_d = nc.dram_tensor("partial_d", [D, T], F32, kind="Internal")
    oe_fm_d = nc.dram_tensor("oe_fm_d", [D, CAP], BF16, kind="Internal")
    ret_tm_d = nc.dram_tensor("ret_tm_d", [T, D], BF16, kind="Internal")

    ident_bf = nc.inline_tensor(np.eye(128, dtype=np.float32)
                                .astype(np.float32), name="identf")
    ropeC_h, ropeS_h = _rope_tables()
    ropeC_d = nc.inline_tensor(ropeC_h, name="ropeC")
    ropeS_d = nc.inline_tensor(ropeS_h, name="ropeS")
    masks_d = nc.inline_tensor(_mask_tiles(), name="masks")
    swap_d = nc.inline_tensor(_swap64(), name="swap64")
    ones128_d = nc.inline_tensor(np.ones((128, 1), np.float32), name="ones128")
    onesrow_d = nc.inline_tensor(np.ones((1, 128), np.float32), name="onesrow")

    LD = Sem(st.enter_context(nc.semaphore("LD")))
    GD = Sem(st.enter_context(nc.semaphore("GD")))
    PE = Sem(st.enter_context(nc.semaphore("PE")))
    DV = Sem(st.enter_context(nc.semaphore("DV")))
    AC = Sem(st.enter_context(nc.semaphore("AC")))
    CC = Sem(st.enter_context(nc.semaphore("CC")))

    _uid = [0]

    def sb(name, shape, dtype, **kw):
        _uid[0] += 1
        return nc.sbuf_tensor(f"{name}_{_uid[0]}", shape, dtype, **kw)
    c_ident_bf = st.enter_context(sb("c_ident_bf", [128, 128], BF16))
    c_ident_f = st.enter_context(sb("c_ident_f", [128, 128], F32))
    c_swap = st.enter_context(sb("c_swap", [RD, RD], BF16))
    c_ones_bf = st.enter_context(sb("c_ones_bf", [128, 1], BF16))
    c_onesrow_f = st.enter_context(sb("c_onesrow_f", [1, 128], F32))
    c_tokid = st.enter_context(sb("c_tokid", [128, NT], F32))
    c_eslot = st.enter_context(sb("c_eslot", [128, NT], U32))
    c_rankmsk = st.enter_context(sb("c_rankmsk", [64, E], F32))
    c_iota8 = st.enter_context(sb("c_iota8", [128, E], F32))
    rows = st.enter_context(sb("rows", [1, 3 * T + 512], F32))
    R_SUM, R_SQ, R_MTS = (rows[:, i * T:(i + 1) * T] for i in range(3))
    R_EXTRA = rows[:, 3 * T:3 * T + 512]

    PS_BIG = st.enter_context(nc.psum_tensor("PS_BIG", [128, 2048], F32))
    PS_A = st.enter_context(nc.psum_tensor("PS_A", [128, 512], F32))
    PS_B = st.enter_context(nc.psum_tensor("PS_B", [128, 512], F32))
    PS_C = st.enter_context(nc.psum_tensor("PS_C", [128, 512], F32))
    PS_D = st.enter_context(nc.psum_tensor("PS_D", [128, 512], F32))

    RB_SLOT = nc.alloc_register(mybir.EngineType.Pool, "rb_slot")
    RB_SLOTALL = nc.alloc_register(mybir.EngineType.Pool, "rb_sall")
    RB_TOK = nc.alloc_register(mybir.EngineType.Pool, "rb_tok")

    p = Plan()
    p.add("pool", lambda g: g.reg_mov(RB_SLOT, 16384) or g.engine_nop())
    p.add("pool", lambda g: g.reg_mov(RB_SLOTALL, 16511) or g.engine_nop())
    p.add("pool", lambda g: g.reg_mov(RB_TOK, 16383) or g.engine_nop())
    p.add("pool", lambda g: g.dma_start(out=c_ident_f[:, :], in_=ident_bf.ap()), inc=GD, by=16)
    p.add("pool", lambda g: g.dma_start(out=c_ident_bf[:, :], in_=ident_bf.ap()), inc=GD, by=16)
    p.add("pool", lambda g: g.dma_start(out=c_swap[:, :], in_=swap_d.ap()), inc=GD, by=16)
    p.add("pool", lambda g: g.dma_start(out=c_ones_bf[:, :], in_=ones128_d.ap()), inc=GD, by=16)
    p.add("pool", lambda g: g.dma_start(out=c_onesrow_f[:, :], in_=onesrow_d.ap()), inc=GD, by=16)
    p.add("pool", lambda g: g.dma_start(out=c_tokid[:, :], in_=tokid.ap()), inc=GD, by=16)
    p.add("pool", lambda g: g.dma_start(out=c_eslot[:, :], in_=eslot.ap()), inc=GD, by=16)
    p.add("pool", lambda g: g.dma_start(out=c_rankmsk[:, :], in_=rankmsk.ap()), inc=GD, by=16)
    p.add("pool", lambda g: g.iota(c_iota8[:, :], [[1, E]], base=0,
                                    channel_multiplier=0,
                                    allow_small_or_imprecise_dtypes=True))
    p.add("pool", lambda g: g.wait_ge(GD.h, GD.n) or g.engine_nop())
    _flush(nc, p)


    def ln_stats(src_dram):
        p = Plan()
        with contextlib.ExitStack() as ph:
            xc = [ph.enter_context(sb(f"lnst_x{i}", [128, T], F32)) for i in range(2)]
            xb = [ph.enter_context(sb(f"lnst_xb{i}", [128, T], BF16)) for i in range(2)]
            sq = [ph.enter_context(sb(f"lnst_sq{i}", [128, T], BF16)) for i in range(2)]
            for c in range(DC):
                b = c % 2
                if c >= 2:
                    p.add("sync", lambda s, a=xc[b], cc=c: s.dma_start(
                        out=a[:, :], in_=src_dram.ap()[cc * 128:(cc + 1) * 128, :]),
                        waits=Plan.at(PE), inc=LD, by=16)
                else:
                    p.add("sync", lambda s, a=xc[b], cc=c: s.dma_start(
                        out=a[:, :], in_=src_dram.ap()[cc * 128:(cc + 1) * 128, :]),
                        inc=LD, by=16)
                ldv = Plan.at(LD)
                wv = list(ldv) + (list(Plan.at(PE)) if c >= 2 else [])
                p.add("act", lambda s, i=b: s.copy(xb[i][:, :], xc[i][:, :]),
                      waits=tuple(wv), inc=AC)
                p.add("act", lambda s, i=b: s.activation(sq[i][:, :], xc[i][:, :], AF.Square),
                      inc=AC)
                acv = Plan.at(AC)
                for n in range(T4):
                    p.add("pe", lambda t, i=b, nn=n, cc=c: t.matmul(
                        PS_BIG[0:1, nn * 512:(nn + 1) * 512],
                        c_ones_bf[:, :], xb[i][:, nn * 512:(nn + 1) * 512],
                        start=(cc == 0), stop=(cc == DC - 1)),
                        waits=acv if n == 0 else ())
                for n in range(T4):
                    p.add("pe", lambda t, i=b, nn=n, cc=c: t.matmul(
                        PS_A[0:1, :] if nn == 0 else (PS_B[0:1, :] if nn == 1 else
                        (PS_C[0:1, :] if nn == 2 else PS_D[0:1, :])),
                        c_ones_bf[:, :], sq[i][:, nn * 512:(nn + 1) * 512],
                        start=(cc == 0), stop=(cc == DC - 1)),
                        inc=PE if (n == T4 - 1) else None)
            pev = Plan.at(PE)
            p.add("dve", lambda v: v.tensor_copy(R_SUM[:, :], PS_BIG[0:1, 0:2048]),
                  waits=pev)
            p.add("dve", lambda v: v.tensor_copy(R_SQ[:, 0:512], PS_A[0:1, :]))
            p.add("dve", lambda v: v.tensor_copy(R_SQ[:, 512:1024], PS_B[0:1, :]))
            p.add("dve", lambda v: v.tensor_copy(R_SQ[:, 1024:1536], PS_C[0:1, :]))
            p.add("dve", lambda v: v.tensor_copy(R_SQ[:, 1536:2048], PS_D[0:1, :]),
                  inc=DV)
            _flush(nc, p)

    def ln_rows_and_bcast(bc_rstd, bc_mts):
        p = Plan()
        inv = 1.0 / float(D)
        p.add("dve", lambda v: v.tensor_scalar_mul(R_SUM[:, :], R_SUM[:, :], inv))
        p.add("dve", lambda v: v.tensor_scalar_mul(R_SQ[:, :], R_SQ[:, :], inv))
        p.add("dve", lambda v: v.tensor_mul(R_MTS[:, :], R_SUM[:, :], R_SUM[:, :]))
        p.add("dve", lambda v: v.tensor_sub(R_SQ[:, :], R_SQ[:, :], R_MTS[:, :]),
              inc=DV)
        p.add("dve", lambda v: v.tensor_scalar_add(R_SQ[:, :], R_SQ[:, :], LN_EPS),
              inc=DV)
        dvv = Plan.at(DV)
        p.add("act", lambda s: s.activation(R_SQ[:, :], R_SQ[:, :], AF.Ln),
              waits=dvv)
        p.add("act", lambda s: s.activation(R_SQ[:, :], R_SQ[:, :], AF.Exp,
                                            scale=-0.5), inc=AC)
        acv = Plan.at(AC)
        p.add("dve", lambda v: v.tensor_mul(R_MTS[:, :], R_SUM[:, :], R_SQ[:, :]),
              waits=acv, inc=DV)
        dv2 = Plan.at(DV)
        for n in range(T4):
            p.add("pe", lambda t, nn=n: t.matmul(
                PS_BIG[:, nn * 512:(nn + 1) * 512], c_onesrow_f[:, :],
                R_SQ[:, nn * 512:(nn + 1) * 512], start=True, stop=True),
                waits=dv2 if n == 0 else (), inc=PE if n == T4 - 1 else None)
        pev = Plan.at(PE)
        p.add("dve", lambda v: v.tensor_copy(bc_rstd[:, :], PS_BIG[:, :]),
              waits=pev, inc=DV)
        dv3 = Plan.at(DV)
        for n in range(T4):
            p.add("pe", lambda t, nn=n: t.matmul(
                PS_BIG[:, nn * 512:(nn + 1) * 512], c_onesrow_f[:, :],
                R_MTS[:, nn * 512:(nn + 1) * 512], start=True, stop=True),
                waits=dv3 if n == 0 else (), inc=PE if n == T4 - 1 else None)
        pe2 = Plan.at(PE)
        p.add("dve", lambda v: v.tensor_copy(bc_mts[:, :], PS_BIG[:, :]),
              waits=pe2, inc=DV)
        _flush(nc, p)

    def ln_normalize(src_dram, xn_slab, bc_rstd, bc_mts, write_fm_dram=None):
        p = Plan()
        with contextlib.ExitStack() as ph:
            xc = [ph.enter_context(sb(f"lnn_x{i}", [128, T], F32)) for i in range(2)]
            for c in range(DC):
                b = c % 2
                w = Plan.at(DV) if c >= 2 else ()
                p.add("sync", lambda s, i=b, cc=c: s.dma_start(
                    out=xc[i][:, :], in_=src_dram.ap()[cc * 128:(cc + 1) * 128, :]),
                    waits=w, inc=LD, by=16)
                ldv = Plan.at(LD)
                p.add("dve", lambda v, i=b: v.tensor_sub(xc[i][:, :], xc[i][:, :],
                                                         bc_mts[:, :]), waits=ldv)
                p.add("dve", lambda v, i=b, cc=c: v.tensor_mul(
                    xn_slab[:, cc * T:(cc + 1) * T], xc[i][:, :], bc_rstd[:, :]),
                    inc=DV)
                if write_fm_dram is not None:
                    dvv = Plan.at(DV)
                    p.add("sync", lambda s, cc=c: s.dma_start(
                        out=write_fm_dram.ap()[cc * 128:(cc + 1) * 128, :],
                        in_=xn_slab[:, cc * T:(cc + 1) * T]), waits=dvv, inc=LD, by=16)
            _flush(nc, p)

    def load_w(p, dst_sb_ap, src_ap, waits=()):
        p.add("pool", lambda g, d=dst_sb_ap, s=src_ap: g.dma_start(out=d, in_=s),
              waits=waits, inc=GD, by=16)


    def attn_a(src_dram, dst_dram):
        ln_stats(src_dram)
        with contextlib.ExitStack() as ph:
            c_ropeC = ph.enter_context(sb("c_ropeC", [RD, T], BF16))
            c_ropeS = ph.enter_context(sb("c_ropeS", [RD, T], BF16))
            c_masks = ph.enter_context(sb("c_masks", [128, 4 * 512], BF16))
            bc_rstd = ph.enter_context(sb("bc_rstd_a", [128, T], F32))
            bc_mts = ph.enter_context(sb("bc_mts_a", [128, T], F32))
            pld = Plan()
            pld.add("pool", lambda g: g.dma_start(out=c_ropeC[:, :],
                                                  in_=ropeC_d.ap()), inc=GD, by=16)
            pld.add("pool", lambda g: g.dma_start(out=c_ropeS[:, :],
                                                  in_=ropeS_d.ap()), inc=GD, by=16)
            pld.add("pool", lambda g: g.dma_start(
                out=c_masks[:, :].rearrange("p (j f) -> p j f", j=4),
                in_=masks_d.ap().rearrange("(j p) f -> p j f", p=128)),
                inc=GD, by=16)
            pld.add("pool", lambda g: g.wait_ge(GD.h, GD.n) or g.engine_nop())
            _flush(nc, pld)
            ln_rows_and_bcast(bc_rstd, bc_mts)
            cq = ph.enter_context(sb("cq", [128, 4 * T], BF16))
            comp = ph.enter_context(sb("comp", [128, 4 * T], BF16))
            kpe_rot = ph.enter_context(sb("kpe_rot", [RD, T], BF16))
            A_QSQ, A_KSQ, A_KPESQ, A_RECIP = R_SUM, R_SQ, R_MTS, R_EXTRA

            with contextlib.ExitStack() as pp:
                xn = pp.enter_context(sb("xn_a", [128, DC * T], BF16, side="right"))
                ln_normalize(src_dram, xn, bc_rstd, bc_mts)
                wqa_sb = pp.enter_context(sb("wqa_sb", [128, DC * QL], BF16,
                                             side="right"))
                wkva_sb = pp.enter_context(sb("wkva_sb", [128, DC * 576], BF16,
                                              side="right"))
                kpe_raw = pp.enter_context(sb("kpe_raw", [RD, T], BF16,
                                              side="right"))
                rtmp1 = pp.enter_context(sb("rtmp1", [RD, 512], F32, side="right"))
                rtmp2 = pp.enter_context(sb("rtmp2", [RD, 512], F32, side="right"))
                p = Plan()
                load_w(p, wqa_sb[:, :].rearrange("p (k m) -> p k m", k=DC),
                       wqa.ap().rearrange("(k p) m -> p k m", p=128))
                load_w(p, wkva_sb[:, :].rearrange("p (k m) -> p k m", k=DC),
                       wkva.ap().rearrange("(k p) m -> p k m", p=128))
                gdv = Plan.at(GD)
                pss = [PS_A, PS_B]
                g = 0
                for mc in range(4):
                    for tt in range(T4):
                        ps = pss[g % 2]
                        wl = list(gdv)
                        if g >= 2:
                            wl += list(Plan.at(DV))
                        for kc in range(DC):
                            p.add("pe", lambda t, ps=ps, kc=kc, mc=mc, tt=tt: t.matmul(
                                ps[:, :],
                                wqa_sb[:, kc * QL + mc * 128: kc * QL + (mc + 1) * 128],
                                xn[:, kc * T + tt * 512: kc * T + (tt + 1) * 512],
                                start=(kc == 0), stop=(kc == DC - 1)),
                                waits=tuple(wl) if kc == 0 else (),
                                inc=PE if kc == DC - 1 else None)
                        pev = Plan.at(PE)
                        p.add("dve", lambda v, ps=ps, mc=mc, tt=tt: v.tensor_copy(
                            cq[:, mc * T + tt * 512: mc * T + (tt + 1) * 512],
                            ps[:, :]), waits=pev, inc=DV)
                        g += 1
                for mc in range(5):
                    m0 = mc * 128
                    mw = 128 if mc < 4 else RD
                    for tt in range(T4):
                        ps = pss[g % 2]
                        wl = Plan.at(DV) if g >= 2 else ()
                        for kc in range(DC):
                            p.add("pe", lambda t, ps=ps, kc=kc, m0=m0, mw=mw, tt=tt:
                                  t.matmul(
                                      ps[0:mw, :],
                                      wkva_sb[:, kc * 576 + m0: kc * 576 + m0 + mw],
                                      xn[:, kc * T + tt * 512: kc * T + (tt + 1) * 512],
                                      start=(kc == 0), stop=(kc == DC - 1)),
                                  waits=wl if kc == 0 else (),
                                  inc=PE if kc == DC - 1 else None)
                        pev = Plan.at(PE)
                        if mc < 4:
                            p.add("dve", lambda v, ps=ps, mc=mc, tt=tt: v.tensor_copy(
                                comp[:, mc * T + tt * 512: mc * T + (tt + 1) * 512],
                                ps[:, :]), waits=pev, inc=DV)
                        else:
                            p.add("dve", lambda v, ps=ps, tt=tt: v.tensor_copy(
                                kpe_raw[:, tt * 512:(tt + 1) * 512], ps[0:RD, :]),
                                waits=pev, inc=DV)
                        g += 1
                for tt in range(T4):
                    dvv = Plan.at(DV)
                    ps = pss[g % 2]
                    p.add("pe", lambda t, ps=ps, tt=tt: t.matmul(
                        ps[0:RD, :], c_swap[:, :],
                        kpe_raw[:, tt * 512:(tt + 1) * 512], start=True, stop=True),
                        waits=dvv, inc=PE)
                    pev = Plan.at(PE)
                    p.add("dve", lambda v, ps=ps, tt=tt: v.tensor_mul(
                        rtmp1[:, :], ps[0:RD, :], c_ropeS[:, tt * 512:(tt + 1) * 512]),
                        waits=pev, inc=DV)
                    p.add("dve", lambda v, tt=tt: v.tensor_mul(
                        rtmp2[:, :], kpe_raw[:, tt * 512:(tt + 1) * 512],
                        c_ropeC[:, tt * 512:(tt + 1) * 512]))
                    p.add("dve", lambda v, tt=tt: v.tensor_add(
                        kpe_rot[:, tt * 512:(tt + 1) * 512], rtmp2[:, :],
                        rtmp1[:, :]), inc=DV)
                    g += 1
                dvv = Plan.at(DV)
                p.add("act", lambda s: s.activation(kpe_raw[:, :], kpe_rot[:, :],
                                                    AF.Square), waits=dvv, inc=AC)
                acv = Plan.at(AC)
                for tt in range(T4):
                    p.add("pe", lambda t, tt=tt: t.matmul(
                        PS_BIG[0:1, tt * 512:(tt + 1) * 512], c_ones_bf[0:RD, :],
                        kpe_raw[:, tt * 512:(tt + 1) * 512], start=True, stop=True),
                        waits=acv if tt == 0 else (), inc=PE if tt == T4 - 1 else None)
                pev = Plan.at(PE)
                p.add("dve", lambda v: v.tensor_copy(A_KPESQ[:, :], PS_BIG[0:1, :]),
                      waits=pev, inc=DV)
                _flush(nc, p)
            with contextlib.ExitStack() as hw:
                attn_sl = hw.enter_context(sb("attn_sl", [128, DC * T], BF16))
                with contextlib.ExitStack() as hh:
                    wqbh = [hh.enter_context(sb(f"wqbh{i}", [128, 4 * QHD], BF16))
                            for i in range(2)]
                    wkvbh = [hh.enter_context(sb(f"wkvbh{i}", [128, 4 * 256], BF16))
                             for i in range(2)]
                    qn = hh.enter_context(sb("qn", [128, T], BF16))
                    qpe_raw = hh.enter_context(sb("qpe_raw", [RD, T], BF16))
                    qpe = hh.enter_context(sb("qpe", [RD, T], BF16))
                    kn = hh.enter_context(sb("kn", [128, T], BF16))
                    v_tm = hh.enter_context(sb("v_tm", [128, T], BF16))
                    sqt = hh.enter_context(sb("sqt", [128, T], BF16))
                    probs = hh.enter_context(sb("probs", [128, 4 * 512], BF16))
                    rtmp1 = hh.enter_context(sb("h_rt1", [RD, 512], F32))
                    rtmp2 = hh.enter_context(sb("h_rt2", [RD, 512], F32))
                    bias_c = hh.enter_context(sb("bias_c", [128, 1], F32))
                    brow = hh.enter_context(sb("brow", [1, 4], F32))
                    rb = hh.enter_context(sb("rb", [128, 512], F32))
                    p = Plan()
                    wqb_r = wqb.ap().rearrange("(k p) m -> p k m", p=128)
                    wkvb_r = wkvb.ap().rearrange("(k p) m -> p k m", p=128)

                    def load_head(hx, waits=()):
                        load_w(p, wqbh[hx % 2][:, :].rearrange("p (k m) -> p k m", k=4),
                               wqb_r[:, :, hx * QHD:(hx + 1) * QHD], waits=waits)
                        load_w(p, wkvbh[hx % 2][:, :].rearrange("p (k m) -> p k m", k=4),
                               wkvb_r[:, :, hx * 256:(hx + 1) * 256], waits=waits)

                    gd_marks = []
                    load_head(0)
                    gd_marks.append(Plan.at(GD))
                    load_head(1)
                    gd_marks.append(Plan.at(GD))
                    for h in range(H):
                        wsel = wqbh[h % 2]
                        ksel = wkvbh[h % 2]
                        for tt in range(T4):
                            ps = [PS_A, PS_B][tt % 2]
                            wl = list(gd_marks[h]) + list(Plan.at(AC, DV))
                            for kc in range(4):
                                p.add("pe", lambda t, ps=ps, kc=kc, tt=tt, w=wsel:
                                      t.matmul(
                                          ps[:, :],
                                          w[:, kc * QHD: kc * QHD + ND],
                                          cq[:, kc * T + tt * 512:
                                             kc * T + (tt + 1) * 512],
                                          start=(kc == 0), stop=(kc == 3)),
                                      waits=tuple(wl) if kc == 0 else (),
                                      inc=PE if kc == 3 else None)
                            pev = Plan.at(PE)
                            p.add("dve", lambda v, ps=ps, tt=tt: v.tensor_copy(
                                qn[:, tt * 512:(tt + 1) * 512], ps[:, :]),
                                waits=pev, inc=DV)
                        for tt in range(T4):
                            ps = [PS_A, PS_B][tt % 2]
                            wl = Plan.at(DV)
                            for kc in range(4):
                                p.add("pe", lambda t, ps=ps, kc=kc, tt=tt, w=wsel:
                                      t.matmul(
                                          ps[0:RD, :],
                                          w[:, kc * QHD + ND: kc * QHD + QHD],
                                          cq[:, kc * T + tt * 512:
                                             kc * T + (tt + 1) * 512],
                                          start=(kc == 0), stop=(kc == 3)),
                                      waits=wl if kc == 0 else (),
                                      inc=PE if kc == 3 else None)
                            pev = Plan.at(PE)
                            p.add("dve", lambda v, ps=ps, tt=tt: v.tensor_copy(
                                qpe_raw[:, tt * 512:(tt + 1) * 512], ps[0:RD, :]),
                                waits=pev, inc=DV)
                        for tt in range(T4):
                            dvv = Plan.at(DV)
                            ps = [PS_A, PS_B][tt % 2]
                            p.add("pe", lambda t, ps=ps, tt=tt: t.matmul(
                                ps[0:RD, :], c_swap[:, :],
                                qpe_raw[:, tt * 512:(tt + 1) * 512],
                                start=True, stop=True), waits=dvv, inc=PE)
                            pev = Plan.at(PE)
                            p.add("dve", lambda v, ps=ps, tt=tt: v.tensor_mul(
                                rtmp1[:, :], ps[0:RD, :],
                                c_ropeS[:, tt * 512:(tt + 1) * 512]), waits=pev)
                            p.add("dve", lambda v, tt=tt: v.tensor_mul(
                                rtmp2[:, :], qpe_raw[:, tt * 512:(tt + 1) * 512],
                                c_ropeC[:, tt * 512:(tt + 1) * 512]))
                            p.add("dve", lambda v, tt=tt: v.tensor_add(
                                qpe[:, tt * 512:(tt + 1) * 512], rtmp2[:, :],
                                rtmp1[:, :]), inc=DV)
                        for tt in range(T4):
                            ps = [PS_A, PS_B][tt % 2]
                            wl = Plan.at(DV)
                            for kc in range(4):
                                p.add("pe", lambda t, ps=ps, kc=kc, tt=tt, w=ksel:
                                      t.matmul(
                                          ps[:, :],
                                          w[:, kc * 256: kc * 256 + ND],
                                          comp[:, kc * T + tt * 512:
                                               kc * T + (tt + 1) * 512],
                                          start=(kc == 0), stop=(kc == 3)),
                                      waits=wl if kc == 0 else (),
                                      inc=PE if kc == 3 else None)
                            pev = Plan.at(PE)
                            p.add("dve", lambda v, ps=ps, tt=tt: v.tensor_copy(
                                kn[:, tt * 512:(tt + 1) * 512], ps[:, :]),
                                waits=pev, inc=DV)
                        for tti in range(NT):
                            ps = [PS_C, PS_D][tti % 2]
                            wl = Plan.at(DV) if tti >= 2 else ()
                            for kc in range(4):
                                p.add("pe", lambda t, ps=ps, kc=kc, tti=tti, w=ksel:
                                      t.matmul(
                                          ps[:, 0:128],
                                          comp[:, kc * T + tti * 128:
                                               kc * T + (tti + 1) * 128],
                                          w[:, kc * 256 + ND: kc * 256 + 256],
                                          start=(kc == 0), stop=(kc == 3)),
                                      waits=wl if kc == 0 else (),
                                      inc=PE if kc == 3 else None)
                            pev = Plan.at(PE)
                            p.add("dve", lambda v, ps=ps, tti=tti: v.tensor_copy(
                                v_tm[:, tti * 128:(tti + 1) * 128], ps[:, 0:128]),
                                waits=pev, inc=DV)
                        if h + 2 < H:
                            load_head(h + 2, waits=Plan.at(PE))
                            gd_marks.append(Plan.at(GD))
                        dvv = Plan.at(DV)
                        p.add("act", lambda s: s.activation(sqt[:, :], qn[:, :],
                                                            AF.Square),
                              waits=dvv, inc=AC)
                        acv = Plan.at(AC)
                        for tt in range(T4):
                            p.add("pe", lambda t, tt=tt: t.matmul(
                                PS_BIG[0:1, tt * 512:(tt + 1) * 512], c_ones_bf[:, :],
                                sqt[:, tt * 512:(tt + 1) * 512],
                                start=True, stop=False),
                                waits=acv if tt == 0 else (),
                                inc=PE if tt == T4 - 1 else None)
                        p.add("act", lambda s: s.activation(sqt[0:RD, 0:T], qpe[:, :],
                                                            AF.Square),
                              waits=Plan.at(PE), inc=AC)
                        acv = Plan.at(AC)
                        for tt in range(T4):
                            p.add("pe", lambda t, tt=tt: t.matmul(
                                PS_BIG[0:1, tt * 512:(tt + 1) * 512],
                                c_ones_bf[0:RD, :],
                                sqt[0:RD, tt * 512:(tt + 1) * 512],
                                start=False, stop=True),
                                waits=acv if tt == 0 else (),
                                inc=PE if tt == T4 - 1 else None)
                        pev = Plan.at(PE)
                        p.add("dve", lambda v: v.tensor_copy(A_QSQ[:, :],
                                                             PS_BIG[0:1, :]),
                              waits=pev, inc=DV)
                        dv2 = Plan.at(DV)
                        p.add("act", lambda s: s.activation(sqt[:, :], kn[:, :],
                                                            AF.Square),
                              waits=tuple(list(dv2) + list(Plan.at(PE))), inc=AC)
                        acv = Plan.at(AC)
                        for tt in range(T4):
                            p.add("pe", lambda t, tt=tt: t.matmul(
                                PS_BIG[0:1, tt * 512:(tt + 1) * 512], c_ones_bf[:, :],
                                sqt[:, tt * 512:(tt + 1) * 512],
                                start=True, stop=True),
                                waits=acv if tt == 0 else (),
                                inc=PE if tt == T4 - 1 else None)
                        pev = Plan.at(PE)
                        p.add("dve", lambda v: v.tensor_copy(A_KSQ[:, :],
                                                             PS_BIG[0:1, :]),
                              waits=pev)
                        p.add("dve", lambda v: v.tensor_add(A_KSQ[:, :], A_KSQ[:, :],
                                                            A_KPESQ[:, :]))
                        p.add("dve", lambda v: v.reduce_max(brow[:, 0:1], A_QSQ[:, :],
                                                            axis=AX), inc=DV)
                        p.add("dve", lambda v: v.reduce_max(brow[:, 1:2], A_KSQ[:, :],
                                                            axis=AX),
                              waits=Plan.at(DV), inc=DV)
                        p.add("dve", lambda v: v.tensor_tensor(
                            brow[:, 2:3], brow[:, 0:1], brow[:, 1:2], ALU.add),
                              waits=Plan.at(DV), inc=DV)
                        p.add("dve", lambda v: v.tensor_scalar_mul(
                            brow[:, 3:4], brow[:, 2:3], -0.5 * SCALE_A),
                              waits=Plan.at(DV), inc=DV)
                        dv3 = Plan.at(DV)
                        p.add("pe", lambda t: t.matmul(
                            PS_C[:, 0:1], c_onesrow_f[:, :], brow[:, 3:4],
                            start=True, stop=True), waits=dv3, inc=PE)
                        pev = Plan.at(PE)
                        p.add("dve", lambda v: v.tensor_copy(bias_c[:, :],
                                                             PS_C[:, 0:1]),
                              waits=pev, inc=DV)
                        bias_ready = Plan.at(DV)
                        exp_marks = {}
                        for qt in range(T4):
                            nkt = 4 * qt + 4
                            for kt in range(nkt):
                                sb_off = (kt % 4) * 512
                                wl = []
                                if kt >= 4:
                                    wl += list(exp_marks[(qt, kt - 4)])
                                if kt == 0:
                                    wl += list(bias_ready)
                                p.add("pe", lambda t, kt=kt, qt=qt, o=sb_off:
                                      t.matmul(
                                          PS_BIG[:, o:o + 512],
                                          kn[:, kt * 128:(kt + 1) * 128],
                                          qn[:, qt * 512:(qt + 1) * 512],
                                          start=True, stop=False),
                                      waits=tuple(wl) if wl else ())
                                p.add("pe", lambda t, kt=kt, qt=qt, o=sb_off:
                                      t.matmul(
                                          PS_BIG[:, o:o + 512],
                                          kpe_rot[:, kt * 128:(kt + 1) * 128],
                                          qpe[:, qt * 512:(qt + 1) * 512],
                                          start=False, stop=True), inc=PE)
                                pev = Plan.at(PE)
                                p.add("act", lambda s, o=sb_off: s.activation(
                                    probs[:, o:o + 512], PS_BIG[:, o:o + 512], AF.Exp,
                                    bias=bias_c[:, :], scale=SCALE_A),
                                    waits=pev, inc=AC)
                                if kt >= 4 * qt:
                                    acv = Plan.at(AC)
                                    j = kt - 4 * qt
                                    p.add("dve", lambda v, o=sb_off, j=j: v.tensor_mul(
                                        probs[:, o:o + 512], probs[:, o:o + 512],
                                        c_masks[:, j * 512:(j + 1) * 512]),
                                        waits=acv, inc=DV)
                                    pr_ready = Plan.at(DV)
                                else:
                                    pr_ready = Plan.at(AC)
                                exp_marks[(qt, kt)] = pr_ready
                                p.add("pe", lambda t, o=sb_off, kt=kt, nkt=nkt:
                                      t.matmul(
                                          PS_C[0:1, :], c_ones_bf[:, :],
                                          probs[:, o:o + 512],
                                          start=(kt == 0), stop=(kt == nkt - 1)),
                                      waits=pr_ready)
                                p.add("pe", lambda t, o=sb_off, kt=kt, nkt=nkt:
                                      t.matmul(
                                          PS_D[:, :], v_tm[:, kt * 128:(kt + 1) * 128],
                                          probs[:, o:o + 512],
                                          start=(kt == 0), stop=(kt == nkt - 1)),
                                      inc=PE if kt == nkt - 1 else None)
                            pev = Plan.at(PE)
                            p.add("dve", lambda v: v.reciprocal(A_RECIP[:, :],
                                                                PS_C[0:1, :]),
                                  waits=pev, inc=DV)
                            dvv = Plan.at(DV)
                            p.add("pe", lambda t: t.matmul(
                                PS_C[:, :], c_onesrow_f[:, :], A_RECIP[:, :],
                                start=True, stop=True), waits=dvv, inc=PE)
                            pev = Plan.at(PE)
                            p.add("dve", lambda v: v.tensor_copy(rb[:, :], PS_C[:, :]),
                                  waits=pev)
                            p.add("dve", lambda v, h=h, qt=qt: v.tensor_mul(
                                attn_sl[:, h * T + qt * 512: h * T + (qt + 1) * 512],
                                PS_D[:, :], rb[:, :]), inc=DV)
                    _flush(nc, p)
                with contextlib.ExitStack() as wo_s:
                    wo_sb = [wo_s.enter_context(sb(f"wo_sb{i}", [128, DC * 128], BF16))
                             for i in range(2)]
                    xo = [wo_s.enter_context(sb(f"xo{i}", [128, T], F32))
                          for i in range(2)]
                    p = Plan()
                    woa_r = woa.ap().rearrange("(k p) m -> p k m", p=128)
                    gd_marks = []
                    for mc in range(DC):
                        load_w(p, wo_sb[mc % 2][:, :].rearrange("p (k m) -> p k m",
                                                                k=DC),
                               woa_r[:, :, mc * 128:(mc + 1) * 128],
                               waits=Plan.at(PE) if mc >= 2 else ())
                        gd_marks.append(Plan.at(GD))
                        p.add("sync", lambda s, i=mc % 2, mc=mc: s.dma_start(
                            out=xo[i][:, :],
                            in_=src_dram.ap()[mc * 128:(mc + 1) * 128, :]),
                            waits=Plan.at(DV) if mc >= 2 else (), inc=LD, by=16)
                        ldv = Plan.at(LD)
                        for tt in range(T4):
                            ps = [PS_A, PS_B][tt % 2]
                            wl = list(gd_marks[-1]) + list(ldv)
                            if mc * T4 + tt >= 2:
                                wl += list(Plan.at(DV))
                            for kc in range(DC):
                                p.add("pe", lambda t, ps=ps, kc=kc, mc=mc, tt=tt:
                                      t.matmul(
                                          ps[:, :],
                                          wo_sb[mc % 2][:, kc * 128:(kc + 1) * 128],
                                          attn_sl[:, kc * T + tt * 512:
                                                  kc * T + (tt + 1) * 512],
                                          start=(kc == 0), stop=(kc == DC - 1)),
                                      waits=tuple(wl) if kc == 0 else (),
                                      inc=PE if kc == DC - 1 else None)
                            pev = Plan.at(PE)
                            p.add("dve", lambda v, ps=ps, i=mc % 2, tt=tt:
                                  v.tensor_add(
                                      xo[i][:, tt * 512:(tt + 1) * 512], ps[:, :],
                                      xo[i][:, tt * 512:(tt + 1) * 512]),
                                  waits=pev, inc=DV)
                        dvv = Plan.at(DV)
                        p.add("sync", lambda s, i=mc % 2, mc=mc: s.dma_start(
                            out=dst_dram.ap()[mc * 128:(mc + 1) * 128, :],
                            in_=xo[i][:, :]), waits=dvv, inc=LD, by=16)
                    _flush(nc, p)


    def attn_b(src_dram, dst_dram):
        ln_stats(src_dram)
        with contextlib.ExitStack() as ph:
            bc_rstd = ph.enter_context(sb("bc_rstd_b", [128, T], F32))
            bc_mts = ph.enter_context(sb("bc_mts_b", [128, T], F32))
            ln_rows_and_bcast(bc_rstd, bc_mts)
            A_QSQ, A_KSQ, A_RECIP = R_SUM, R_SQ, R_EXTRA
            with contextlib.ExitStack() as bx:
                xn = bx.enter_context(sb("xn_b", [128, DC * T], BF16, side="right"))
                ln_normalize(src_dram, xn, bc_rstd, bc_mts)
                with contextlib.ExitStack() as hh:
                    wqb_ = [hh.enter_context(sb(f"bwq{i}", [128, DC * 256], BF16))
                            for i in range(2)]
                    wkb_ = [hh.enter_context(sb(f"bwk{i}", [128, DC * 256], BF16))
                            for i in range(2)]
                    wvb_ = [hh.enter_context(sb(f"bwv{i}", [128, DC * 256], BF16))
                            for i in range(2)]
                    q2 = hh.enter_context(sb("q2", [128, 2 * T], BF16))
                    k2 = hh.enter_context(sb("k2", [128, 2 * T], BF16))
                    v2 = hh.enter_context(sb("v2", [128, NT * 256], BF16))
                    sqt = hh.enter_context(sb("sqt_b", [128, T], BF16))
                    probs = hh.enter_context(sb("probs_b", [128, 4 * 512], BF16))
                    bias_c = hh.enter_context(sb("bias_cb", [128, 1], F32))
                    brow = hh.enter_context(sb("brow_b", [1, 4], F32))
                    rb = hh.enter_context(sb("rb_b", [128, 512], F32))
                    abuf = [hh.enter_context(sb(f"abuf{i}", [128, T], BF16))
                            for i in range(2)]
                    p = Plan()
                    wq_r = wq_b.ap().rearrange("(k p) m -> p k m", p=128)
                    wk_r = wk_b.ap().rearrange("(k p) m -> p k m", p=128)
                    wv_r = wv_b.ap().rearrange("(k p) m -> p k m", p=128)

                    def load_grp(gx, waits=()):
                        load_w(p, wqb_[gx % 2][:, :].rearrange(
                            "p (k m) -> p k m", k=DC),
                            wq_r[:, :, gx * 256:(gx + 1) * 256], waits=waits)
                        load_w(p, wkb_[gx % 2][:, :].rearrange(
                            "p (k m) -> p k m", k=DC),
                            wk_r[:, :, gx * 256:(gx + 1) * 256])
                        load_w(p, wvb_[gx % 2][:, :].rearrange(
                            "p (k m) -> p k m", k=DC),
                            wv_r[:, :, gx * 256:(gx + 1) * 256])

                    gd_marks = []
                    load_grp(0)
                    gd_marks.append(Plan.at(GD))
                    load_grp(1)
                    gd_marks.append(Plan.at(GD))
                    ab_store_marks = {}
                    for g in range(8):
                        wsel, ksel, vsel = wqb_[g % 2], wkb_[g % 2], wvb_[g % 2]
                        for (dst_sl, wmat) in ((q2, wsel), (k2, ksel)):
                            for hh_i in range(2):
                                for tt in range(T4):
                                    ps = [PS_A, PS_B][tt % 2]
                                    wl = list(gd_marks[g]) + list(Plan.at(DV))
                                    for kc in range(DC):
                                        p.add("pe", lambda t, ps=ps, kc=kc,
                                              hh_i=hh_i, tt=tt, w=wmat: t.matmul(
                                                  ps[:, :],
                                                  w[:, kc * 256 + hh_i * 128:
                                                    kc * 256 + (hh_i + 1) * 128],
                                                  xn[:, kc * T + tt * 512:
                                                     kc * T + (tt + 1) * 512],
                                                  start=(kc == 0),
                                                  stop=(kc == DC - 1)),
                                              waits=tuple(wl) if kc == 0 else (),
                                              inc=PE if kc == DC - 1 else None)
                                    pev = Plan.at(PE)
                                    p.add("dve", lambda v, ps=ps, d=dst_sl,
                                          hh_i=hh_i, tt=tt: v.tensor_copy(
                                              d[:, hh_i * T + tt * 512:
                                                hh_i * T + (tt + 1) * 512],
                                              ps[:, :]), waits=pev, inc=DV)
                        for tti in range(NT):
                            ps = [PS_C, PS_D][tti % 2]
                            wl = Plan.at(DV)
                            for kc in range(DC):
                                p.add("pe", lambda t, ps=ps, kc=kc, tti=tti,
                                      w=vsel: t.matmul(
                                          ps[:, 0:256],
                                          xn[:, kc * T + tti * 128:
                                             kc * T + (tti + 1) * 128],
                                          w[:, kc * 256:(kc + 1) * 256],
                                          start=(kc == 0), stop=(kc == DC - 1)),
                                      waits=wl if kc == 0 else (),
                                      inc=PE if kc == DC - 1 else None)
                            pev = Plan.at(PE)
                            p.add("dve", lambda v, ps=ps, tti=tti: v.tensor_copy(
                                v2[:, tti * 256:(tti + 1) * 256], ps[:, 0:256]),
                                waits=pev, inc=DV)
                        if g + 2 < 8:
                            load_grp(g + 2, waits=Plan.at(PE))
                            gd_marks.append(Plan.at(GD))
                        for hh_i in range(2):
                            h = 2 * g + hh_i
                            dvv = Plan.at(DV)
                            p.add("act", lambda s, hh_i=hh_i: s.activation(
                                sqt[:, :], q2[:, hh_i * T:(hh_i + 1) * T],
                                AF.Square), waits=dvv, inc=AC)
                            acv = Plan.at(AC)
                            for tt in range(T4):
                                p.add("pe", lambda t, tt=tt: t.matmul(
                                    PS_BIG[0:1, tt * 512:(tt + 1) * 512],
                                    c_ones_bf[:, :],
                                    sqt[:, tt * 512:(tt + 1) * 512],
                                    start=True, stop=True),
                                    waits=acv if tt == 0 else (),
                                    inc=PE if tt == T4 - 1 else None)
                            pev = Plan.at(PE)
                            p.add("dve", lambda v: v.tensor_copy(
                                A_QSQ[:, :], PS_BIG[0:1, :]), waits=pev, inc=DV)
                            p.add("act", lambda s, hh_i=hh_i: s.activation(
                                sqt[:, :], k2[:, hh_i * T:(hh_i + 1) * T],
                                AF.Square),
                                waits=tuple(list(Plan.at(DV)) + list(Plan.at(PE))),
                                inc=AC)
                            acv = Plan.at(AC)
                            for tt in range(T4):
                                p.add("pe", lambda t, tt=tt: t.matmul(
                                    PS_BIG[0:1, tt * 512:(tt + 1) * 512],
                                    c_ones_bf[:, :],
                                    sqt[:, tt * 512:(tt + 1) * 512],
                                    start=True, stop=True),
                                    waits=acv if tt == 0 else (),
                                    inc=PE if tt == T4 - 1 else None)
                            pev = Plan.at(PE)
                            p.add("dve", lambda v: v.tensor_copy(
                                A_KSQ[:, :], PS_BIG[0:1, :]), waits=pev, inc=DV)
                            p.add("dve", lambda v: v.reduce_max(
                                brow[:, 0:1], A_QSQ[:, :], axis=AX),
                                waits=Plan.at(DV), inc=DV)
                            p.add("dve", lambda v: v.reduce_max(
                                brow[:, 1:2], A_KSQ[:, :], axis=AX),
                                waits=Plan.at(DV), inc=DV)
                            p.add("dve", lambda v: v.tensor_tensor(
                                brow[:, 2:3], brow[:, 0:1], brow[:, 1:2],
                                ALU.add), waits=Plan.at(DV), inc=DV)
                            p.add("dve", lambda v: v.tensor_scalar_mul(
                                brow[:, 3:4], brow[:, 2:3], -0.5 * SCALE_B),
                                waits=Plan.at(DV), inc=DV)
                            dv3 = Plan.at(DV)
                            p.add("pe", lambda t: t.matmul(
                                PS_C[:, 0:1], c_onesrow_f[:, :], brow[:, 3:4],
                                start=True, stop=True), waits=dv3, inc=PE)
                            pev = Plan.at(PE)
                            p.add("dve", lambda v: v.tensor_copy(
                                bias_c[:, :], PS_C[:, 0:1]), waits=pev, inc=DV)
                            bias_ready = Plan.at(DV)
                            exp_marks = {}
                            for qt in range(T4):
                                for kt in range(NT):
                                    o = (kt % 4) * 512
                                    wl = []
                                    if kt >= 4:
                                        wl += list(exp_marks[(qt, kt - 4)])
                                    if kt == 0:
                                        wl += list(bias_ready)
                                        if qt == 0 and g * 2 + hh_i == 0:
                                            pass
                                    p.add("pe", lambda t, hh_i=hh_i, kt=kt,
                                          qt=qt, o=o: t.matmul(
                                              PS_BIG[:, o:o + 512],
                                              k2[:, hh_i * T + kt * 128:
                                                 hh_i * T + (kt + 1) * 128],
                                              q2[:, hh_i * T + qt * 512:
                                                 hh_i * T + (qt + 1) * 512],
                                              start=True, stop=True),
                                          waits=tuple(wl) if wl else (), inc=PE)
                                    pev = Plan.at(PE)
                                    p.add("act", lambda s, o=o: s.activation(
                                        probs[:, o:o + 512],
                                        PS_BIG[:, o:o + 512], AF.Exp,
                                        bias=bias_c[:, :], scale=SCALE_B),
                                        waits=pev, inc=AC)
                                    pr_ready = Plan.at(AC)
                                    exp_marks[(qt, kt)] = pr_ready
                                    p.add("pe", lambda t, o=o, kt=kt: t.matmul(
                                        PS_C[0:1, :], c_ones_bf[:, :],
                                        probs[:, o:o + 512],
                                        start=(kt == 0), stop=(kt == NT - 1)),
                                        waits=pr_ready)
                                    p.add("pe", lambda t, o=o, kt=kt, hh_i=hh_i:
                                          t.matmul(
                                              PS_D[:, :],
                                              v2[:, kt * 256 + hh_i * 128:
                                                 kt * 256 + (hh_i + 1) * 128],
                                              probs[:, o:o + 512],
                                              start=(kt == 0),
                                              stop=(kt == NT - 1)),
                                          inc=PE if kt == NT - 1 else None)
                                pev = Plan.at(PE)
                                p.add("dve", lambda v: v.reciprocal(
                                    A_RECIP[:, :], PS_C[0:1, :]),
                                    waits=pev, inc=DV)
                                dvv = Plan.at(DV)
                                p.add("pe", lambda t: t.matmul(
                                    PS_C[:, :], c_onesrow_f[:, :], A_RECIP[:, :],
                                    start=True, stop=True), waits=dvv, inc=PE)
                                pev = Plan.at(PE)
                                p.add("dve", lambda v: v.tensor_copy(
                                    rb[:, :], PS_C[:, :]), waits=pev)
                                wl = [*Plan.at(PE)]
                                if h >= 2:
                                    wl += list(ab_store_marks[h - 2])
                                p.add("dve", lambda v, h=h, qt=qt: v.tensor_mul(
                                    abuf[h % 2][:, qt * 512:(qt + 1) * 512],
                                    PS_D[:, :], rb[:, :]),
                                    waits=tuple(wl), inc=DV)
                            dvv = Plan.at(DV)
                            p.add("sync", lambda s, h=h: s.dma_start(
                                out=xnfm_d.ap()[h * 128:(h + 1) * 128, :],
                                in_=abuf[h % 2][:, :]), waits=dvv, inc=LD, by=16)
                            ab_store_marks[h] = Plan.at(LD)
                    _flush(nc, p)
            with contextlib.ExitStack() as wo_s:
                attn_sl = wo_s.enter_context(sb("attn_slb", [128, DC * T], BF16,
                                                side="right"))
                wo_sb = [wo_s.enter_context(sb(f"bwo{i}", [128, DC * 128], BF16))
                         for i in range(2)]
                xo = [wo_s.enter_context(sb(f"bxo{i}", [128, T], F32))
                      for i in range(2)]
                p = Plan()
                for c in range(DC):
                    p.add("sync", lambda s, c=c: s.dma_start(
                        out=attn_sl[:, c * T:(c + 1) * T],
                        in_=xnfm_d.ap()[c * 128:(c + 1) * 128, :]),
                        inc=LD, by=16)
                aload = Plan.at(LD)
                wob_r = wob.ap().rearrange("(k p) m -> p k m", p=128)
                gd_marks = []
                for mc in range(DC):
                    load_w(p, wo_sb[mc % 2][:, :].rearrange(
                        "p (k m) -> p k m", k=DC),
                        wob_r[:, :, mc * 128:(mc + 1) * 128],
                        waits=Plan.at(PE) if mc >= 2 else ())
                    gd_marks.append(Plan.at(GD))
                    p.add("sync", lambda s, i=mc % 2, mc=mc: s.dma_start(
                        out=xo[i][:, :],
                        in_=src_dram.ap()[mc * 128:(mc + 1) * 128, :]),
                        waits=Plan.at(DV) if mc >= 2 else (), inc=LD, by=16)
                    ldv = Plan.at(LD)
                    for tt in range(T4):
                        ps = [PS_A, PS_B][tt % 2]
                        wl = list(gd_marks[-1]) + list(ldv) + list(aload)
                        if mc * T4 + tt >= 2:
                            wl += list(Plan.at(DV))
                        for kc in range(DC):
                            p.add("pe", lambda t, ps=ps, kc=kc, mc=mc, tt=tt:
                                  t.matmul(
                                      ps[:, :],
                                      wo_sb[mc % 2][:, kc * 128:(kc + 1) * 128],
                                      attn_sl[:, kc * T + tt * 512:
                                              kc * T + (tt + 1) * 512],
                                      start=(kc == 0), stop=(kc == DC - 1)),
                                  waits=tuple(wl) if kc == 0 else (),
                                  inc=PE if kc == DC - 1 else None)
                        pev = Plan.at(PE)
                        p.add("dve", lambda v, ps=ps, i=mc % 2, tt=tt:
                              v.tensor_add(
                                  xo[i][:, tt * 512:(tt + 1) * 512], ps[:, :],
                                  xo[i][:, tt * 512:(tt + 1) * 512]),
                              waits=pev, inc=DV)
                    dvv = Plan.at(DV)
                    p.add("sync", lambda s, i=mc % 2, mc=mc: s.dma_start(
                        out=dst_dram.ap()[mc * 128:(mc + 1) * 128, :],
                        in_=xo[i][:, :]), waits=dvv, inc=LD, by=16)
                p.add("sync", lambda s: s.wait_ge(LD.h, LD.n) or s.engine_nop())
                _flush(nc, p)


    def ffn(src_dram, dst_dram, is_last):
        ln_stats(src_dram)
        with contextlib.ExitStack() as f0:
            gk_s = f0.enter_context(sb("gk_s", [128, NT], F32))
            gate_s = f0.enter_context(sb("gate_s", [128, NT], F32))
            retg_u = f0.enter_context(sb("retg_u", [128, NT], U32))
            sidx_u = f0.enter_context(sb("sidx_u", [128, NT], U32))
            oh_s = f0.enter_context(sb("oh_s", [128, E * NT], F32))
            offs_sb = f0.enter_context(sb("offs_sb", [8, 1], F32))
            cnt_sb = f0.enter_context(sb("cnt_sb", [64, 1], F32))
            rcol = f0.enter_context(sb("rcol", [128, 8], F32))

            with contextlib.ExitStack() as fx:
                xn = fx.enter_context(sb("xn_f", [128, DC * T], BF16, side="right"))
                with contextlib.ExitStack() as fb:
                    bc_rstd = fb.enter_context(sb("bc_rstd_f", [128, T], F32))
                    bc_mts = fb.enter_context(sb("bc_mts_f", [128, T], F32))
                    ln_rows_and_bcast(bc_rstd, bc_mts)
                    ln_normalize(src_dram, xn, bc_rstd, bc_mts,
                                 write_fm_dram=xnfm_d)

                with contextlib.ExitStack() as f2:
                    strip = [f2.enter_context(sb(f"strip{i}", [128, D], BF16))
                             for i in range(2)]
                    p = Plan()
                    xn_written = Plan.at(LD)
                    st_marks = []
                    for i in range(NT):
                        wl = list(xn_written)
                        if i >= 2:
                            wl += list(st_marks[i - 2])
                        p.add("sync", lambda s, i=i: s.dma_start(
                            out=strip[i % 2][:, :],
                            in_=xnfm_d.ap()[:, i * 128:(i + 1) * 128],
                            transpose=True), waits=tuple(wl), inc=LD, by=16)
                        ldv = Plan.at(LD)
                        p.add("sync", lambda s, i=i: s.dma_start(
                            out=ag_x_in.ap()[i * 128:(i + 1) * 128, :],
                            in_=strip[i % 2][:, :]), waits=ldv, inc=LD, by=16)
                        st_marks.append(Plan.at(LD))
                    agin_done = Plan.at(LD)
                    p.add("pool", lambda g: g.collective_compute(
                        "AllGather", ALU.bypass,
                        replica_groups=[list(range(NCORE))],
                        ins=[ag_x_in.ap().opt()], outs=[xall.ap().opt()]),
                        waits=agin_done, inc=CC)
                    _flush(nc, p)
                ag1_done = (CC, CC.n)

                with contextlib.ExitStack() as f3:
                    wg_sb = f3.enter_context(sb("wg_sb", [128, DC * E], BF16))
                    oh_T = f3.enter_context(sb("oh_T", [E, T], F32))
                    incl_T = f3.enter_context(sb("incl_T", [E, T], F32))
                    lg = f3.enter_context(sb("lg", [128, E], F32))
                    pexp = f3.enter_context(sb("pexp", [128, E], F32))
                    t8 = f3.enter_context(sb("t8", [128, E], F32))
                    inc_tm = f3.enter_context(sb("inc_tm", [128, E], F32))
                    zz = f3.enter_context(sb("zz", [128, 129], F32))
                    p = Plan()
                    load_w(p, wg_sb[:, :].rearrange("p (k m) -> p k m", k=DC),
                           wg.ap().rearrange("(k p) m -> p k m", p=128))
                    gdv = Plan.at(GD)
                    p.add("pool", lambda g: g.memset(zz[:, :], 0.0), inc=GD)
                    zz_done = Plan.at(GD)
                    p.add("pool", lambda g: g.dma_start(
                        out=slot_loc.ap().rearrange("(a b) c -> a (b c)", a=128),
                        in_=zz[:, :]), waits=zz_done, inc=GD, by=16)
                    slotz_done = Plan.at(GD)
                    for i in range(NT):
                        ps = [PS_A, PS_B][i % 2]
                        wl = list(gdv)
                        if i >= 2:
                            wl += list(Plan.at(DV))
                        for kc in range(DC):
                            p.add("pe", lambda t, ps=ps, kc=kc, i=i: t.matmul(
                                ps[:, 0:E],
                                xn[:, kc * T + i * 128: kc * T + (i + 1) * 128],
                                wg_sb[:, kc * E:(kc + 1) * E],
                                start=(kc == 0), stop=(kc == DC - 1)),
                                waits=tuple(wl) if kc == 0 else (),
                                inc=PE if kc == DC - 1 else None)
                        pev = Plan.at(PE)

                        def pd(fn, extra=()):
                            p.add("dve", fn,
                                  waits=tuple(list(Plan.at(DV, AC)) + list(extra)),
                                  inc=DV)

                        pd(lambda v, ps=ps: v.tensor_copy(lg[:, :], ps[:, 0:E]),
                           extra=pev)
                        dvv = Plan.at(DV)
                        p.add("act", lambda s: s.activation(
                            pexp[:, :], lg[:, :], AF.Exp,
                            accum_out=rcol[:, 0:1]), waits=dvv, inc=AC)
                        pd(lambda v: v.reduce_max(rcol[:, 1:2], pexp[:, :], axis=AX))
                        pd(lambda v: v.reciprocal(rcol[:, 2:3], rcol[:, 0:1]))
                        pd(lambda v, i=i: v.tensor_mul(
                            gate_s[:, i:i + 1], rcol[:, 1:2], rcol[:, 2:3]))
                        pd(lambda v: v.reduce_max(rcol[:, 3:4], lg[:, :], axis=AX))
                        pd(lambda v: v.tensor_scalar(
                            t8[:, :], lg[:, :], rcol[:, 3:4], None, ALU.subtract))
                        pd(lambda v, i=i: v.tensor_single_scalar(
                            oh_s[:, i * E:(i + 1) * E], t8[:, :], 0.0, ALU.is_ge))
                        if DBG and i == 0:
                            p.add("pool", lambda g: g.dma_start(
                                out=dbg2_lg.ap(), in_=lg[:, :]),
                                waits=Plan.at(DV), inc=GD, by=16)
                            p.add("pool", lambda g: g.dma_start(
                                out=dbg2_pexp.ap(), in_=pexp[:, :]), inc=GD, by=16)
                            p.add("pool", lambda g: g.dma_start(
                                out=dbg2_rcol.ap(), in_=rcol[:, :]), inc=GD, by=16)
                            gddbg = Plan.at(GD)
                            p.add("dve", lambda v: v.engine_nop(), waits=gddbg)
                        p.add("dve", lambda v: v.engine_nop(), inc=DV)
                        dvo = Plan.at(DV)
                        p.add("pe", lambda t, i=i: t.transpose(
                            PS_C[0:E, 0:128], oh_s[:, i * E:(i + 1) * E],
                            c_ident_f[:, :]), waits=dvo, inc=PE)
                        pev2 = Plan.at(PE)
                        p.add("dve", lambda v, i=i: v.tensor_copy(
                            oh_T[:, i * 128:(i + 1) * 128], PS_C[0:E, 0:128]),
                            waits=pev2, inc=DV)
                    dva = Plan.at(DV)
                    p.add("dve", lambda v: v.reduce_sum(cnt_sb[0:E, 0:1], oh_T[:, :],
                                                        axis=AX), waits=dva, inc=DV)
                    dvc = Plan.at(DV)
                    p.add("pool", lambda g: g.dma_start(out=cnt_in.ap(),
                                                        in_=cnt_sb[0:E, 0:1]),
                          waits=dvc, inc=GD, by=16)
                    cnt_w = Plan.at(GD)
                    p.add("pool", lambda g: g.collective_compute(
                        "AllGather", ALU.bypass,
                        replica_groups=[list(range(NCORE))],
                        ins=[cnt_in.ap().opt()], outs=[cnt_all.ap().opt()]),
                        waits=cnt_w, inc=CC)
                    ccv = Plan.at(CC)
                    p.add("pool", lambda g: g.dma_start(out=cnt_sb[:, :],
                                                        in_=cnt_all.ap()),
                          waits=ccv, inc=GD, by=16)
                    cnt_l = Plan.at(GD)
                    p.add("pe", lambda t: t.matmul(PS_D[0:E, 0:1], c_rankmsk[:, :],
                                                   cnt_sb[:, :], start=True,
                                                   stop=True), waits=cnt_l, inc=PE)
                    pev = Plan.at(PE)
                    p.add("dve", lambda v: v.tensor_copy(offs_sb[:, :],
                                                         PS_D[0:E, 0:1]),
                          waits=pev, inc=DV)
                    dvo = Plan.at(DV)
                    p.add("dve", lambda v: v.tensor_tensor_scan(
                        incl_T[:, :], oh_T[:, :], oh_T[:, :], offs_sb[:, :],
                        ALU.add, ALU.bypass), waits=dvo, inc=DV)
                    scan_done = Plan.at(DV)
                    evac_marks = []
                    for i in range(NT):
                        wtr = list(scan_done) if i == 0 else []
                        if i >= 1:
                            wtr += list(evac_marks[i - 1])
                        p.add("pe", lambda t, i=i: t.transpose(
                            PS_C[:, 0:E], incl_T[:, i * 128:(i + 1) * 128],
                            c_ident_f[0:E, 0:E]),
                            waits=tuple(wtr), inc=PE)
                        pev = Plan.at(PE)

                        def pd(fn, extra=()):
                            p.add("dve", fn,
                                  waits=tuple(list(Plan.at(DV)) + list(extra)),
                                  inc=DV)

                        pd(lambda v: v.tensor_copy(inc_tm[:, :], PS_C[:, 0:E]),
                           extra=pev)
                        evac_marks.append(Plan.at(DV))
                        pd(lambda v, i=i: v.tensor_mul(
                            t8[:, :], oh_s[:, i * E:(i + 1) * E], inc_tm[:, :]))
                        pd(lambda v: v.reduce_sum(rcol[:, 0:1], t8[:, :], axis=AX))
                        pd(lambda v, i=i: v.tensor_mul(
                            t8[:, :], oh_s[:, i * E:(i + 1) * E], c_iota8[:, :]))
                        pd(lambda v: v.reduce_sum(rcol[:, 1:2], t8[:, :], axis=AX))
                        pd(lambda v: v.tensor_single_scalar(
                            rcol[:, 2:3], rcol[:, 0:1], float(CAP), ALU.is_le))
                        pd(lambda v, i=i: v.tensor_mul(
                            gk_s[:, i:i + 1], gate_s[:, i:i + 1], rcol[:, 2:3]))
                        pd(lambda v: v.scalar_tensor_tensor(
                            rcol[:, 3:4], rcol[:, 1:2], float(CAP), rcol[:, 0:1],
                            ALU.mult, ALU.add))
                        pd(lambda v: v.tensor_mul(
                            rcol[:, 4:5], rcol[:, 3:4], rcol[:, 2:3]))
                        pd(lambda v: v.tensor_scalar(
                            rcol[:, 5:6], rcol[:, 2:3], -1.0, 1.0, ALU.mult,
                            ALU.add))
                        pd(lambda v, i=i: v.tensor_scalar_add(
                            rcol[:, 6:7], c_tokid[:, i:i + 1], 50000.0))
                        pd(lambda v: v.tensor_mul(
                            rcol[:, 7:8], rcol[:, 5:6], rcol[:, 6:7]))
                        pd(lambda v, i=i: v.tensor_add(
                            sidx_u[:, i:i + 1], rcol[:, 4:5], rcol[:, 7:8]))
                        pd(lambda v: v.tensor_single_scalar(
                            rcol[:, 5:6], rcol[:, 4:5], 0.0, ALU.is_equal))
                        pd(lambda v: v.scalar_tensor_tensor(
                            rcol[:, 6:7], rcol[:, 5:6], 1.0e6, rcol[:, 4:5],
                            ALU.mult, ALU.add))
                        pd(lambda v, i=i: v.tensor_scalar(
                            retg_u[:, i:i + 1], rcol[:, 6:7], -1.0, None,
                            ALU.add))
                        dvs = Plan.at(DV)
                        p.add("pool", lambda g, i=i: g.indirect_dma_start(
                            out=slot_loc.ap(),
                            out_offset=IndirectOffsetOnAxis(sidx_u[:, i:i + 1], 0),
                            in_=c_tokid[:, i:i + 1], in_offset=None,
                            bounds_check=RB_SLOT, oob_is_err=False),
                            waits=tuple(list(dvs) + (list(slotz_done) if i == 0
                                                     else [])), inc=GD, by=16)
                    sc_done = Plan.at(GD)
                    if DBG:
                        for dst_t, src_ap in [
                                (dbg2_oh, oh_s[:, :]), (dbg2_ohT, oh_T[:, :]),
                                (dbg2_incl, incl_T[:, :]), (dbg2_cnt, cnt_sb[:, :]),
                                (dbg2_offs, offs_sb[:, :]),
                                (dbg2_sidx, sidx_u[:, :]),
                                (dbg2_retg, retg_u[:, :]), (dbg2_gk, gk_s[:, :])]:
                            p.add("pool", lambda g, a=dst_t, b=src_ap: g.dma_start(
                                out=a.ap(), in_=b), waits=sc_done, inc=GD, by=16)
                        sc_done = Plan.at(GD)
                    p.add("pool", lambda g: g.collective_compute(
                        "AllReduce", ALU.add,
                        replica_groups=[list(range(NCORE))],
                        ins=[slot_loc.ap().opt()], outs=[slot_all.ap().opt()]),
                        waits=sc_done, inc=CC)
                    _flush(nc, p)
                ar3_done = (CC, CC.n)

                h1s = contextlib.ExitStack()
                h_sl = h1s.enter_context(sb("h_sl", [128, HC * T], BF16))
                ffn_h_production(xn, sw1, sw3, h_sl)

            ffn_w2(h_sl, sw2, src_dram, None, True)
            h1s.close()

            with contextlib.ExitStack() as f6:
                xd = f6.enter_context(sb("xd_f", [128, DC * T], BF16, side="right"))
                with contextlib.ExitStack() as f6a:
                    sv = f6a.enter_context(sb("sv", [128, NT], F32))
                    svt = f6a.enter_context(sb("svt", [128, NT], F32))
                    svt2 = f6a.enter_context(sb("svt2", [128, NT], F32))
                    xgi = f6a.enter_context(sb("xgi", [128, NT], U32))
                    gt = [f6a.enter_context(sb(f"gt{i}", [128, D], BF16))
                          for i in range(2)]
                    p = Plan()
                    p.add("pool", lambda g: g.wait_ge(ar3_done[0].h, ar3_done[1])
                          or g.engine_nop())
                    for i in range(NT):
                        p.add("pool", lambda g, i=i: g.indirect_dma_start(
                            out=sv[:, i:i + 1], out_offset=None,
                            in_=slot_all.ap(),
                            in_offset=IndirectOffsetOnAxis(c_eslot[:, i:i + 1], 0),
                            bounds_check=RB_SLOTALL, oob_is_err=False),
                            inc=GD, by=16)
                    gdv = Plan.at(GD)
                    p.add("dve", lambda v: v.tensor_single_scalar(
                        svt[:, :], sv[:, :], 0.0, ALU.is_equal),
                        waits=gdv, inc=DV)
                    p.add("dve", lambda v: v.scalar_tensor_tensor(
                        svt2[:, :], svt[:, :], 1.0e6, sv[:, :],
                        ALU.mult, ALU.add), waits=Plan.at(DV), inc=DV)
                    p.add("dve", lambda v: v.tensor_scalar(
                        xgi[:, :], svt2[:, :], -1.0, None, ALU.add),
                        waits=Plan.at(DV), inc=DV)
                    xgi_done = Plan.at(DV)
                    p.add("pool", lambda g: g.wait_ge(ag1_done[0].h, ag1_done[1])
                          or g.engine_nop())
                    ld_marks = []
                    for i in range(NT):
                        wl = list(xgi_done)
                        if i >= 2:
                            wl += list(ld_marks[i - 2])
                        p.add("pool", lambda g, i=i: g.memset(gt[i % 2][:, :], 0.0),
                              waits=tuple(wl), inc=GD)
                        p.add("pool", lambda g, i=i: g.indirect_dma_start(
                            out=gt[i % 2][:, :], out_offset=None,
                            in_=xall.ap(),
                            in_offset=IndirectOffsetOnAxis(xgi[:, i:i + 1], 0),
                            bounds_check=RB_TOK, oob_is_err=False),
                            inc=GD, by=16)
                        gdi = Plan.at(GD)
                        p.add("sync", lambda s, i=i: s.dma_start(
                            out=xd_tm_d.ap()[i * 128:(i + 1) * 128, :],
                            in_=gt[i % 2][:, :]), waits=gdi, inc=LD, by=16)
                        ld_marks.append(Plan.at(LD))
                    all_rows = Plan.at(LD)
                    tr_marks = []
                    for c in range(DC):
                        wl = list(all_rows)
                        p.add("sync", lambda s, c=c: s.dma_start(
                            out=xd[:, c * T:(c + 1) * T],
                            in_=xd_tm_d.ap()[:, c * 128:(c + 1) * 128],
                            transpose=True), waits=tuple(wl), inc=LD, by=16)
                    p.add("sync", lambda s: s.wait_ge(LD.h, LD.n) or s.engine_nop())
                    _flush(nc, p)

                h2s = contextlib.ExitStack()
                h_sl2 = h2s.enter_context(sb("h_sl2", [128, HC * T], BF16))
                ffn_h_production(xd, w1, w3, h_sl2)

            ffn_w2(h_sl2, w2, None, ag_oe_in, False)
            h2s.close()

            with contextlib.ExitStack() as f9:
                rt = [f9.enter_context(sb(f"rt{i}", [128, D], BF16))
                      for i in range(2)]
                pcb = [f9.enter_context(sb(f"pcb{i}", [128, T], F32))
                      for i in range(2)]
                rstrip = [f9.enter_context(sb(f"rstrip{i}", [128, T], BF16))
                          for i in range(2)]
                p = Plan()
                p.add("pool", lambda g: g.wait_ge(CC.h, CC.n) or g.engine_nop())
                dv_marks = []
                st_marks_r = []
                for i in range(NT):
                    wl = []
                    if i >= 2:
                        wl += list(dv_marks[i - 2]) + list(st_marks_r[i - 2])
                    p.add("pool", lambda g, i=i: g.memset(rt[i % 2][:, :], 0.0),
                          waits=tuple(wl), inc=GD)
                    p.add("pool", lambda g, i=i: g.indirect_dma_start(
                        out=rt[i % 2][:, :], out_offset=None,
                        in_=oe_all.ap(),
                        in_offset=IndirectOffsetOnAxis(retg_u[:, i:i + 1], 0),
                        bounds_check=RB_TOK, oob_is_err=False),
                        inc=GD, by=16)
                    gdi = Plan.at(GD)
                    p.add("dve", lambda v, i=i: v.tensor_scalar(
                        rt[i % 2][:, :], rt[i % 2][:, :], gk_s[:, i:i + 1], None,
                        ALU.mult), waits=gdi, inc=DV)
                    dv_marks.append(Plan.at(DV))
                    p.add("sync", lambda s, i=i: s.dma_start(
                        out=ret_tm_d.ap()[i * 128:(i + 1) * 128, :],
                        in_=rt[i % 2][:, :]), waits=dv_marks[-1], inc=LD, by=16)
                    st_marks_r.append(Plan.at(LD))
                rows_done = Plan.at(LD)
                st_marks = []
                for c in range(DC):
                    wl = list(rows_done)
                    if c >= 2:
                        wl += list(st_marks[c - 2])
                    p.add("sync", lambda s, c=c: s.dma_start(
                        out=rstrip[c % 2][:, :],
                        in_=ret_tm_d.ap()[:, c * 128:(c + 1) * 128],
                        transpose=True), waits=tuple(wl), inc=LD, by=16)
                    p.add("sync", lambda s, c=c: s.dma_start(
                        out=pcb[c % 2][:, :],
                        in_=partial_d.ap()[c * 128:(c + 1) * 128, :]),
                        inc=LD, by=16)
                    ldv = Plan.at(LD)
                    p.add("dve", lambda v, c=c: v.tensor_add(
                        pcb[c % 2][:, :], pcb[c % 2][:, :], rstrip[c % 2][:, :]),
                        waits=ldv, inc=DV)
                    dvv = Plan.at(DV)
                    p.add("sync", lambda s, c=c: s.dma_start(
                        out=dst_dram.ap()[c * 128:(c + 1) * 128, :],
                        in_=pcb[c % 2][:, :]), waits=dvv, inc=LD, by=16)
                    st_marks.append(Plan.at(LD))
                p.add("sync", lambda s: s.wait_ge(LD.h, LD.n) or s.engine_nop())
                _flush(nc, p)

    def ffn_h_production(act_slab, wu1, wu3, h_sl):
        with contextlib.ExitStack() as hs:
            w1b = [hs.enter_context(sb(f"w1b{i}", [128, DC * 128], BF16))
                   for i in range(2)]
            w3b = [hs.enter_context(sb(f"w3b{i}", [128, DC * 128], BF16))
                   for i in range(2)]
            sil = hs.enter_context(sb("sil", [128, 512], BF16))
            p = Plan()
            wu1_r = wu1.ap().rearrange("(k p) m -> p k m", p=128)
            wu3_r = wu3.ap().rearrange("(k p) m -> p k m", p=128)
            gd_marks = []
            for hc in range(HC):
                wl = Plan.at(PE) if hc >= 2 else ()
                load_w(p, w1b[hc % 2][:, :].rearrange("p (k m) -> p k m", k=DC),
                       wu1_r[:, :, hc * 128:(hc + 1) * 128], waits=wl)
                load_w(p, w3b[hc % 2][:, :].rearrange("p (k m) -> p k m", k=DC),
                       wu3_r[:, :, hc * 128:(hc + 1) * 128])
                gd_marks.append(Plan.at(GD))
                for tt in range(T4):
                    wl = list(gd_marks[-1])
                    if hc * T4 + tt >= 1:
                        wl += list(Plan.at(DV))
                    for kc in range(DC):
                        p.add("pe", lambda t, kc=kc, hc=hc, tt=tt: t.matmul(
                            PS_A[:, :],
                            w1b[hc % 2][:, kc * 128:(kc + 1) * 128],
                            act_slab[:, kc * T + tt * 512: kc * T + (tt + 1) * 512],
                            start=(kc == 0), stop=(kc == DC - 1)),
                            waits=tuple(wl) if kc == 0 else (),
                            inc=PE if kc == DC - 1 else None)
                    for kc in range(DC):
                        p.add("pe", lambda t, kc=kc, hc=hc, tt=tt: t.matmul(
                            PS_B[:, :],
                            w3b[hc % 2][:, kc * 128:(kc + 1) * 128],
                            act_slab[:, kc * T + tt * 512: kc * T + (tt + 1) * 512],
                            start=(kc == 0), stop=(kc == DC - 1)),
                            inc=PE if kc == DC - 1 else None)
                    pev = Plan.at(PE)
                    p.add("act", lambda s: s.activation(sil[:, :], PS_A[:, :],
                                                        AF.Silu), waits=pev, inc=AC)
                    acv = Plan.at(AC)
                    p.add("dve", lambda v, hc=hc, tt=tt: v.tensor_mul(
                        h_sl[:, hc * T + tt * 512: hc * T + (tt + 1) * 512],
                        PS_B[:, :], sil[:, :]), waits=acv, inc=DV)
            _flush(nc, p)

    def ffn_w2(h_sl, wu2, resid_src, oe_ag_in, is_shared):
        h_done = Plan.at(DV)
        with contextlib.ExitStack() as w2s:
            w2b = [w2s.enter_context(sb(f"w2b{i}", [128, HC * 256], BF16))
                   for i in range(2)]
            xo = [w2s.enter_context(sb(f"xo2_{i}", [128, T],
                                       F32 if is_shared else BF16))
                  for i in range(2)]
            p = Plan()
            wu2_r = wu2.ap().rearrange("(k p) m -> p k m", p=128)
            psg = [PS_BIG[:, 0:512], PS_BIG[:, 512:1024],
                   PS_BIG[:, 1024:1536], PS_BIG[:, 1536:2048],
                   PS_A[:, :], PS_B[:, :], PS_C[:, :], PS_D[:, :]]
            gd_marks = []
            ld_marks = []
            dv_marks = []
            for mp in range(8):
                wl = list(Plan.at(PE)) if mp >= 2 else []
                load_w(p, w2b[mp % 2][:, :].rearrange("p (k m) -> p k m", k=HC),
                       wu2_r[:, :, mp * 256:(mp + 1) * 256], waits=tuple(wl))
                gd_marks.append(Plan.at(GD))
                if is_shared:
                    for j in range(2):
                        p.add("sync", lambda s, j=j, mp=mp: s.dma_start(
                            out=xo[j][:, :],
                            in_=resid_src.ap()[(mp * 2 + j) * 128:
                                               (mp * 2 + j + 1) * 128, :]),
                            waits=Plan.at(DV) if mp >= 1 else (),
                            inc=LD, by=16)
                ld_marks.append(Plan.at(LD))
                for hc in range(HC):
                    wl = list(gd_marks[-1]) + (list(h_done) if hc == 0 else [])
                    if hc == 0 and mp >= 1:
                        wl += list(Plan.at(DV))
                    for j in range(2):
                        for tt in range(T4):
                            p.add("pe", lambda t, hc=hc, j=j, tt=tt, mp=mp:
                                  t.matmul(
                                      psg[j * 4 + tt],
                                      w2b[mp % 2][:, hc * 256 + j * 128:
                                                  hc * 256 + (j + 1) * 128],
                                      h_sl[:, hc * T + tt * 512:
                                           hc * T + (tt + 1) * 512],
                                      start=(hc == 0), stop=(hc == HC - 1)),
                                  waits=tuple(wl) if (hc == 0 and j == 0
                                                      and tt == 0) else (),
                                  inc=PE if (hc == HC - 1 and j == 1
                                             and tt == T4 - 1) else None)
                pev = Plan.at(PE)
                for j in range(2):
                    for tt in range(T4):
                        if is_shared:
                            p.add("dve", lambda v, j=j, tt=tt: v.tensor_add(
                                xo[j][:, tt * 512:(tt + 1) * 512],
                                psg[j * 4 + tt],
                                xo[j][:, tt * 512:(tt + 1) * 512]),
                                waits=tuple(list(pev) + list(ld_marks[-1]))
                                if tt == 0 and j == 0 else (),
                                inc=DV if (j == 1 and tt == T4 - 1) else None)
                        else:
                            wle = (tuple(list(pev) + list(ld_marks[-1]))
                                   if tt == 0 and j == 0 else ())
                            p.add("dve", lambda v, j=j, tt=tt: v.tensor_copy(
                                xo[j][:, tt * 512:(tt + 1) * 512],
                                psg[j * 4 + tt]),
                                waits=wle,
                                inc=DV if (j == 1 and tt == T4 - 1) else None)
                dv_marks.append(Plan.at(DV))
                for j in range(2):
                    p.add("sync", lambda s, j=j, mp=mp: s.dma_start(
                        out=(partial_d if is_shared else oe_fm_d).ap()
                        [(mp * 2 + j) * 128:(mp * 2 + j + 1) * 128, :],
                        in_=xo[j][:, :]), waits=dv_marks[-1], inc=LD, by=16)
                ld_marks[-1] = Plan.at(LD)
            if not is_shared:
                all_st = Plan.at(LD)
                for i in range(NT):
                    p.add("sync", lambda s, i=i: s.dma_start(
                        out=xo[i % 2][:, :],
                        in_=oe_fm_d.ap()[:, i * 128:(i + 1) * 128],
                        transpose=True), waits=all_st, inc=LD, by=16)
                    ldv = Plan.at(LD)
                    p.add("sync", lambda s, i=i: s.dma_start(
                        out=oe_ag_in.ap()[i * 128:(i + 1) * 128, :],
                        in_=xo[i % 2][:, :]), waits=ldv, inc=LD, by=16)
                agw = Plan.at(LD)
                p.add("pool", lambda g: g.collective_compute(
                    "AllGather", ALU.bypass,
                    replica_groups=[list(range(NCORE))],
                    ins=[oe_ag_in.ap().opt()], outs=[oe_all.ap().opt()]),
                    waits=agw, inc=CC)
            _flush(nc, p)


    cur = xfm_in
    dst = yfm if n_sub == 1 else xres[0]
    attn_a(cur, dst)
    cur = dst
    if n_sub >= 2:
        dst = yfm if n_sub == 2 else xres[1]
        ffn(cur, dst, n_sub == 2)
        cur = dst
    if n_sub >= 3:
        dst = yfm if n_sub == 3 else xres[0]
        attn_b(cur, dst)
        cur = dst
    if n_sub >= 4:
        ffn(cur, yfm, True)

    if DBG and n_sub >= 2:
        with contextlib.ExitStack() as ds:
            tmpf = ds.enter_context(sb("dbg_tmp", [128, T], F32))
            tmpb = ds.enter_context(sb("dbg_tmpb", [128, T], BF16))
            p = Plan()
            pairs = [(dbg_partial, partial_d, F32), (dbg_xd, xd_tm_d, BF16),
                     (dbg_oe, oe_fm_d, BF16), (dbg_ret, ret_tm_d, BF16),
                     (dbg_xn, xnfm_d, BF16)]
            for dst_t, src_t, dt_ in pairs:
                buf = tmpb[:, :] if dt_ == BF16 else tmpf[:, :]
                for c in range(DC):
                    wl = Plan.at(LD)
                    p.add("sync", lambda s, a=src_t, b=buf, c=c: s.dma_start(
                        out=b, in_=a.ap()[c * 128:(c + 1) * 128, :]),
                        waits=wl, inc=LD, by=16)
                    ldv = Plan.at(LD)
                    p.add("sync", lambda s, a=dst_t, b=buf, c=c: s.dma_start(
                        out=a.ap()[c * 128:(c + 1) * 128, :], in_=b),
                        waits=ldv, inc=LD, by=16)
            wl = Plan.at(LD)
            p.add("sync", lambda s: s.dma_start(
                out=tmpf[:, 0:129], in_=slot_all.ap().rearrange(
                    "(a b) c -> a (b c)", a=128)), waits=wl, inc=LD, by=16)
            ldv = Plan.at(LD)
            p.add("sync", lambda s: s.dma_start(
                out=dbg_slot.ap().rearrange("(a b) c -> a (b c)", a=128),
                in_=tmpf[:, 0:129]), waits=ldv, inc=LD, by=16)
            p.add("sync", lambda s: s.wait_ge(LD.h, LD.n) or s.engine_nop())
            _flush(nc, p)

    p = Plan()
    p.add("sync", lambda s: s.wait_ge(LD.h, LD.n) or s.engine_nop())
    _flush(nc, p)
    st.close()
    return nc



def _aux_inputs(core):
    rank = np.zeros((64, E), np.float32)
    for r in range(NCORE):
        if r < core:
            for e in range(E):
                rank[r * E + e, e] = 1.0
    tok = np.zeros((128, NT), np.float32)
    slot = np.zeros((128, NT), np.uint32)
    for tt in range(NT):
        for pp in range(128):
            t_loc = tt * 128 + pp
            tok[pp, tt] = core * T + t_loc + 1
            slot[pp, tt] = core * CAP + t_loc + 1
    return rank, tok, slot


def kernel(**inputs):
    def f32(a):
        return np.ascontiguousarray(np.asarray(a), dtype=np.float32)

    x = f32(inputs["L0_attn_a_x"])
    wqa = f32(inputs["L0_attn_a_Wqa"])
    wqb = f32(inputs["L0_attn_a_Wqb"])
    wkva = f32(inputs["L0_attn_a_Wkva"])
    wkvb = f32(inputs["L0_attn_a_Wkvb"])
    woa = f32(inputs["L0_attn_a_Wo"])
    wq = f32(inputs["L0_attn_b_w_q"])
    wk = f32(inputs["L0_attn_b_w_k"])
    wv = f32(inputs["L0_attn_b_w_v"])
    wob = f32(inputs["L0_attn_b_w_o"])
    wg = f32(inputs["L0_ffn_Wg"])
    w1 = f32(inputs["L0_ffn_W1"])
    w3 = f32(inputs["L0_ffn_W3"])
    w2 = f32(inputs["L0_ffn_W2"])
    sw1 = f32(inputs["L0_ffn_sw1"])
    sw3 = f32(inputs["L0_ffn_sw3"])
    sw2 = f32(inputs["L0_ffn_sw2"])

    wq_n = np.ascontiguousarray(wq.T)
    wk_n = np.ascontiguousarray(wk.T)
    wv_n = np.ascontiguousarray(wv.T)
    wob_n = np.ascontiguousarray(wob.T)

    nc = build_nc(N_SUB)
    in_maps = []
    for c in range(NCORE):
        rank, tok, slot = _aux_inputs(c)
        in_maps.append(dict(
            xfm=np.ascontiguousarray(x[c].T),
            wqa=wqa, wqb=wqb, wkva=wkva, wkvb=wkvb, woa=woa,
            wq=wq_n, wk=wk_n, wv=wv_n, wob=wob_n,
            wg=wg, w1=np.ascontiguousarray(w1[c]),
            w3=np.ascontiguousarray(w3[c]), w2=np.ascontiguousarray(w2[c]),
            sw1=sw1, sw3=sw3, sw2=sw2,
            rankmsk=rank, tokid=tok, eslot=slot,
        ))
    res = run_bass_kernel_spmd(nc, in_maps, core_ids=list(range(NCORE)))
    out = np.stack([np.asarray(r["yfm"]).T for r in res.results], axis=0)
    if DBG:
        kernel.dbg = res.results
    return out


# version 05
# speedup vs baseline: 1.2893x; 1.2893x over previous
import contextlib
import math

import numpy as np

import concourse.bass as bass
import concourse.mybir as mybir
from concourse.bass_core_rust import IndirectOffsetOnAxis
from concourse.bass_utils import run_bass_kernel_spmd

F32 = mybir.dt.float32
BF16 = mybir.dt.bfloat16
U32 = mybir.dt.uint32
AF = mybir.ActivationFunctionType
ALU = mybir.AluOpType
AX = mybir.AxisListType.X

D = 2048
T = 2048
NT = 16
T4 = 4
DC = 16
H = 16
RD = 64
ND = 128
VD = 128
QHD = ND + RD
QL = 512
KVL = 512
HF = 2816
HC = 22
E = 8
CAP = 2048
NCORE = 8
SCALE_A = 1.0 / math.sqrt(float(QHD))
SCALE_B = 1.0 / math.sqrt(float(VD))
LN_EPS = 1e-5

N_SUB = 4
DBG = False

ENGINES = ("pe", "dve", "act", "pool", "sync")
_ENG_ATTR = {"pe": "tensor", "dve": "vector", "act": "scalar", "pool": "gpsimd",
             "sync": "sync"}


class Sem:
    __slots__ = ("h", "n")

    def __init__(self, h):
        self.h = h
        self.n = 0


class Plan:

    def __init__(self):
        self.items = {e: [] for e in ENGINES}

    def add(self, eng, fn, waits=(), inc=None, by=1):
        if inc is not None:
            inc.n += by
        self.items[eng].append((tuple(waits), fn, inc, by))

    @staticmethod
    def at(*sems):
        return tuple((s, s.n) for s in sems)


def _flush(nc, plan):
    if not any(plan.items[e] for e in ENGINES):
        return
    with nc.Block() as blk:
        for eng in ENGINES:
            items = plan.items[eng]
            if not items:
                continue

            def mk(items=items):
                def body(E):
                    lastw = {}
                    for waits, fn, inc, by in items:
                        for (s, v) in waits:
                            if v > 0 and lastw.get(id(s), -1) < v:
                                E.wait_ge(s.h, v)
                                lastw[id(s)] = v
                        ins = fn(E)
                        if inc is not None:
                            ins.then_inc(inc.h, by)
                return body

            getattr(blk, _ENG_ATTR[eng])(mk())



def _rope_tables():
    inv_freq = 10000.0 ** (-np.arange(0, RD, 2, dtype=np.float64) / RD)
    pos = np.arange(T, dtype=np.float64)
    ang = pos[:, None] * inv_freq[None, :]
    cos = np.cos(ang).astype(np.float32)
    sin = np.sin(ang).astype(np.float32)
    C = np.empty((RD, T), np.float32)
    S = np.empty((RD, T), np.float32)
    C[0::2, :] = cos.T
    C[1::2, :] = cos.T
    S[0::2, :] = -sin.T
    S[1::2, :] = sin.T
    return C, S


def _mask_tiles():
    masks = np.zeros((4, 128, 512), np.float32)
    for j in range(4):
        for p in range(128):
            masks[j, p, 128 * j + p:] = 1.0
    return masks.reshape(4 * 128, 512)


def _swap64():
    P = np.zeros((RD, RD), np.float32)
    for p in range(RD):
        P[p, p ^ 1] = 1.0
    return P



def build_nc(n_sub=N_SUB):
    nc = bass.Bass("TRN2", target_bir_lowering=False)
    st = contextlib.ExitStack()

    xfm_in = nc.dram_tensor("xfm", [D, T], F32, kind="ExternalInput")
    wqa = nc.dram_tensor("wqa", [D, QL], F32, kind="ExternalInput")
    wqb = nc.dram_tensor("wqb", [QL, H * QHD], F32, kind="ExternalInput")
    wkva = nc.dram_tensor("wkva", [D, KVL + RD], F32, kind="ExternalInput")
    wkvb = nc.dram_tensor("wkvb", [KVL, H * (ND + VD)], F32, kind="ExternalInput")
    woa = nc.dram_tensor("woa", [D, D], F32, kind="ExternalInput")
    wq_b = nc.dram_tensor("wq", [D, D], F32, kind="ExternalInput")
    wk_b = nc.dram_tensor("wk", [D, D], F32, kind="ExternalInput")
    wv_b = nc.dram_tensor("wv", [D, D], F32, kind="ExternalInput")
    wob = nc.dram_tensor("wob", [D, D], F32, kind="ExternalInput")
    wg = nc.dram_tensor("wg", [D, E], F32, kind="ExternalInput")
    w1 = nc.dram_tensor("w1", [D, HF], F32, kind="ExternalInput")
    w3 = nc.dram_tensor("w3", [D, HF], F32, kind="ExternalInput")
    w2 = nc.dram_tensor("w2", [HF, D], F32, kind="ExternalInput")
    sw1 = nc.dram_tensor("sw1", [D, HF], F32, kind="ExternalInput")
    sw3 = nc.dram_tensor("sw3", [D, HF], F32, kind="ExternalInput")
    sw2 = nc.dram_tensor("sw2", [HF, D], F32, kind="ExternalInput")
    rankmsk = nc.dram_tensor("rankmsk", [64, E], F32, kind="ExternalInput")
    tokid = nc.dram_tensor("tokid", [128, NT], F32, kind="ExternalInput")
    eslot = nc.dram_tensor("eslot", [128, NT], U32, kind="ExternalInput")

    yfm = nc.dram_tensor("yfm", [D, T], F32, kind="ExternalOutput")
    if DBG:
        dbg_partial = nc.dram_tensor("dbg_partial", [D, T], F32, kind="ExternalOutput")
        dbg_slot = nc.dram_tensor("dbg_slot", [16512, 1], F32, kind="ExternalOutput")
        dbg_xd = nc.dram_tensor("dbg_xd", [CAP, D], BF16, kind="ExternalOutput")
        dbg_oe = nc.dram_tensor("dbg_oe", [D, CAP], BF16, kind="ExternalOutput")
        dbg_ret = nc.dram_tensor("dbg_ret", [T, D], BF16, kind="ExternalOutput")
        dbg_xn = nc.dram_tensor("dbg_xn", [D, T], BF16, kind="ExternalOutput")
        dbg2_oh = nc.dram_tensor("dbg2_oh", [128, E * NT], F32, kind="ExternalOutput")
        dbg2_ohT = nc.dram_tensor("dbg2_ohT", [E, T], F32, kind="ExternalOutput")
        dbg2_incl = nc.dram_tensor("dbg2_incl", [E, T], F32, kind="ExternalOutput")
        dbg2_cnt = nc.dram_tensor("dbg2_cnt", [64, 1], F32, kind="ExternalOutput")
        dbg2_offs = nc.dram_tensor("dbg2_offs", [8, 1], F32, kind="ExternalOutput")
        dbg2_sidx = nc.dram_tensor("dbg2_sidx", [128, NT], U32, kind="ExternalOutput")
        dbg2_retg = nc.dram_tensor("dbg2_retg", [128, NT], U32, kind="ExternalOutput")
        dbg2_gk = nc.dram_tensor("dbg2_gk", [128, NT], F32, kind="ExternalOutput")
        dbg2_lg = nc.dram_tensor("dbg2_lg", [128, E], F32, kind="ExternalOutput")
        dbg2_pexp = nc.dram_tensor("dbg2_pexp", [128, E], F32, kind="ExternalOutput")
        dbg2_rcol = nc.dram_tensor("dbg2_rcol", [128, 8], F32, kind="ExternalOutput")

    xres = [nc.dram_tensor(f"xres{i}", [D, T], F32, kind="Internal")
            for i in range(2)]
    xnfm_d = nc.dram_tensor("xnfm_d", [D, T], BF16, kind="Internal")
    ag_x_in = nc.dram_tensor("ag_x_in", [T, D], BF16, kind="Internal")
    xall = nc.dram_tensor("xall", [NCORE * T, D], BF16, kind="Internal",
                          addr_space="Shared")
    cnt_in = nc.dram_tensor("cnt_in", [E, 1], F32, kind="Internal")
    cnt_all = nc.dram_tensor("cnt_all", [NCORE * E, 1], F32, kind="Internal",
                             addr_space="Shared")
    slot_loc = nc.dram_tensor("slot_loc", [16512, 1], F32, kind="Internal")
    slot_all = nc.dram_tensor("slot_all", [16512, 1], F32, kind="Internal",
                              addr_space="Shared")
    xd_tm_d = nc.dram_tensor("xd_tm_d", [CAP, D], BF16, kind="Internal")
    ag_oe_in = nc.dram_tensor("ag_oe_in", [CAP, D], BF16, kind="Internal")
    oe_all = nc.dram_tensor("oe_all", [NCORE * CAP, D], BF16, kind="Internal",
                            addr_space="Shared")
    partial_d = nc.dram_tensor("partial_d", [D, T], F32, kind="Internal")
    oe_fm_d = nc.dram_tensor("oe_fm_d", [D, CAP], BF16, kind="Internal")
    ret_tm_d = nc.dram_tensor("ret_tm_d", [T, D], BF16, kind="Internal")

    ident_bf = nc.inline_tensor(np.eye(128, dtype=np.float32)
                                .astype(np.float32), name="identf")
    ropeC_h, ropeS_h = _rope_tables()
    ropeC_d = nc.inline_tensor(ropeC_h, name="ropeC")
    ropeS_d = nc.inline_tensor(ropeS_h, name="ropeS")
    masks_d = nc.inline_tensor(_mask_tiles(), name="masks")
    swap_d = nc.inline_tensor(_swap64(), name="swap64")
    ones128_d = nc.inline_tensor(np.ones((128, 1), np.float32), name="ones128")
    onesrow_d = nc.inline_tensor(np.ones((1, 128), np.float32), name="onesrow")

    LD = Sem(st.enter_context(nc.semaphore("LD")))
    GD = Sem(st.enter_context(nc.semaphore("GD")))
    PE = Sem(st.enter_context(nc.semaphore("PE")))
    DV = Sem(st.enter_context(nc.semaphore("DV")))
    AC = Sem(st.enter_context(nc.semaphore("AC")))
    CC = Sem(st.enter_context(nc.semaphore("CC")))

    _uid = [0]

    def sb(name, shape, dtype, **kw):
        _uid[0] += 1
        return nc.sbuf_tensor(f"{name}_{_uid[0]}", shape, dtype, **kw)
    c_ident_bf = st.enter_context(sb("c_ident_bf", [128, 128], BF16))
    c_ident_f = st.enter_context(sb("c_ident_f", [128, 128], F32))
    c_swap = st.enter_context(sb("c_swap", [RD, RD], BF16))
    c_ones_bf = st.enter_context(sb("c_ones_bf", [128, 1], BF16))
    c_onesrow_f = st.enter_context(sb("c_onesrow_f", [1, 128], F32))
    c_tokid = st.enter_context(sb("c_tokid", [128, NT], F32))
    c_eslot = st.enter_context(sb("c_eslot", [128, NT], U32))
    c_rankmsk = st.enter_context(sb("c_rankmsk", [64, E], F32))
    c_iota8 = st.enter_context(sb("c_iota8", [128, E], F32))
    rows = st.enter_context(sb("rows", [1, 3 * T + 512], F32))
    R_SUM, R_SQ, R_MTS = (rows[:, i * T:(i + 1) * T] for i in range(3))
    R_EXTRA = rows[:, 3 * T:3 * T + 512]

    PS_BIG = st.enter_context(nc.psum_tensor("PS_BIG", [128, 2048], F32))
    PS_A = st.enter_context(nc.psum_tensor("PS_A", [128, 512], F32))
    PS_B = st.enter_context(nc.psum_tensor("PS_B", [128, 512], F32))
    PS_C = st.enter_context(nc.psum_tensor("PS_C", [128, 512], F32))
    PS_D = st.enter_context(nc.psum_tensor("PS_D", [128, 512], F32))

    RB_SLOT = nc.alloc_register(mybir.EngineType.Pool, "rb_slot")
    RB_SLOTALL = nc.alloc_register(mybir.EngineType.Pool, "rb_sall")
    RB_TOK = nc.alloc_register(mybir.EngineType.Pool, "rb_tok")

    p = Plan()
    p.add("pool", lambda g: g.reg_mov(RB_SLOT, 16384) or g.engine_nop())
    p.add("pool", lambda g: g.reg_mov(RB_SLOTALL, 16511) or g.engine_nop())
    p.add("pool", lambda g: g.reg_mov(RB_TOK, 16383) or g.engine_nop())
    p.add("pool", lambda g: g.dma_start(out=c_ident_f[:, :], in_=ident_bf.ap()), inc=GD, by=16)
    p.add("pool", lambda g: g.dma_start(out=c_ident_bf[:, :], in_=ident_bf.ap()), inc=GD, by=16)
    p.add("pool", lambda g: g.dma_start(out=c_swap[:, :], in_=swap_d.ap()), inc=GD, by=16)
    p.add("pool", lambda g: g.dma_start(out=c_ones_bf[:, :], in_=ones128_d.ap()), inc=GD, by=16)
    p.add("pool", lambda g: g.dma_start(out=c_onesrow_f[:, :], in_=onesrow_d.ap()), inc=GD, by=16)
    p.add("pool", lambda g: g.dma_start(out=c_tokid[:, :], in_=tokid.ap()), inc=GD, by=16)
    p.add("pool", lambda g: g.dma_start(out=c_eslot[:, :], in_=eslot.ap()), inc=GD, by=16)
    p.add("pool", lambda g: g.dma_start(out=c_rankmsk[:, :], in_=rankmsk.ap()), inc=GD, by=16)
    p.add("pool", lambda g: g.iota(c_iota8[:, :], [[1, E]], base=0,
                                    channel_multiplier=0,
                                    allow_small_or_imprecise_dtypes=True))
    p.add("pool", lambda g: g.wait_ge(GD.h, GD.n) or g.engine_nop())
    _flush(nc, p)


    def ln_stats(src_dram):
        p = Plan()
        with contextlib.ExitStack() as ph:
            xc = [ph.enter_context(sb(f"lnst_x{i}", [128, T], F32)) for i in range(2)]
            xb = [ph.enter_context(sb(f"lnst_xb{i}", [128, T], BF16)) for i in range(2)]
            sq = [ph.enter_context(sb(f"lnst_sq{i}", [128, T], BF16)) for i in range(2)]
            for c in range(DC):
                b = c % 2
                if c >= 2:
                    p.add("sync", lambda s, a=xc[b], cc=c: s.dma_start(
                        out=a[:, :], in_=src_dram.ap()[cc * 128:(cc + 1) * 128, :]),
                        waits=Plan.at(PE), inc=LD, by=16)
                else:
                    p.add("sync", lambda s, a=xc[b], cc=c: s.dma_start(
                        out=a[:, :], in_=src_dram.ap()[cc * 128:(cc + 1) * 128, :]),
                        inc=LD, by=16)
                ldv = Plan.at(LD)
                wv = list(ldv) + (list(Plan.at(PE)) if c >= 2 else [])
                p.add("act", lambda s, i=b: s.copy(xb[i][:, :], xc[i][:, :]),
                      waits=tuple(wv), inc=AC)
                p.add("act", lambda s, i=b: s.activation(sq[i][:, :], xc[i][:, :], AF.Square),
                      inc=AC)
                acv = Plan.at(AC)
                for n in range(T4):
                    p.add("pe", lambda t, i=b, nn=n, cc=c: t.matmul(
                        PS_BIG[0:1, nn * 512:(nn + 1) * 512],
                        c_ones_bf[:, :], xb[i][:, nn * 512:(nn + 1) * 512],
                        start=(cc == 0), stop=(cc == DC - 1)),
                        waits=acv if n == 0 else ())
                for n in range(T4):
                    p.add("pe", lambda t, i=b, nn=n, cc=c: t.matmul(
                        PS_A[0:1, :] if nn == 0 else (PS_B[0:1, :] if nn == 1 else
                        (PS_C[0:1, :] if nn == 2 else PS_D[0:1, :])),
                        c_ones_bf[:, :], sq[i][:, nn * 512:(nn + 1) * 512],
                        start=(cc == 0), stop=(cc == DC - 1)),
                        inc=PE if (n == T4 - 1) else None)
            pev = Plan.at(PE)
            p.add("dve", lambda v: v.tensor_copy(R_SUM[:, :], PS_BIG[0:1, 0:2048]),
                  waits=pev)
            p.add("dve", lambda v: v.tensor_copy(R_SQ[:, 0:512], PS_A[0:1, :]))
            p.add("dve", lambda v: v.tensor_copy(R_SQ[:, 512:1024], PS_B[0:1, :]))
            p.add("dve", lambda v: v.tensor_copy(R_SQ[:, 1024:1536], PS_C[0:1, :]))
            p.add("dve", lambda v: v.tensor_copy(R_SQ[:, 1536:2048], PS_D[0:1, :]),
                  inc=DV)
            _flush(nc, p)

    def ln_rows_and_bcast(bc_rstd, bc_mts):
        p = Plan()
        inv = 1.0 / float(D)
        p.add("dve", lambda v: v.tensor_scalar_mul(R_SUM[:, :], R_SUM[:, :], inv))
        p.add("dve", lambda v: v.tensor_scalar_mul(R_SQ[:, :], R_SQ[:, :], inv))
        p.add("dve", lambda v: v.tensor_mul(R_MTS[:, :], R_SUM[:, :], R_SUM[:, :]))
        p.add("dve", lambda v: v.tensor_sub(R_SQ[:, :], R_SQ[:, :], R_MTS[:, :]),
              inc=DV)
        p.add("dve", lambda v: v.tensor_scalar_add(R_SQ[:, :], R_SQ[:, :], LN_EPS),
              inc=DV)
        dvv = Plan.at(DV)
        p.add("act", lambda s: s.activation(R_SQ[:, :], R_SQ[:, :], AF.Ln),
              waits=dvv)
        p.add("act", lambda s: s.activation(R_SQ[:, :], R_SQ[:, :], AF.Exp,
                                            scale=-0.5), inc=AC)
        acv = Plan.at(AC)
        p.add("dve", lambda v: v.tensor_mul(R_MTS[:, :], R_SUM[:, :], R_SQ[:, :]),
              waits=acv, inc=DV)
        dv2 = Plan.at(DV)
        for n in range(T4):
            p.add("pe", lambda t, nn=n: t.matmul(
                PS_BIG[:, nn * 512:(nn + 1) * 512], c_onesrow_f[:, :],
                R_SQ[:, nn * 512:(nn + 1) * 512], start=True, stop=True),
                waits=dv2 if n == 0 else (), inc=PE if n == T4 - 1 else None)
        pev = Plan.at(PE)
        p.add("dve", lambda v: v.tensor_copy(bc_rstd[:, :], PS_BIG[:, :]),
              waits=pev, inc=DV)
        dv3 = Plan.at(DV)
        for n in range(T4):
            p.add("pe", lambda t, nn=n: t.matmul(
                PS_BIG[:, nn * 512:(nn + 1) * 512], c_onesrow_f[:, :],
                R_MTS[:, nn * 512:(nn + 1) * 512], start=True, stop=True),
                waits=dv3 if n == 0 else (), inc=PE if n == T4 - 1 else None)
        pe2 = Plan.at(PE)
        p.add("dve", lambda v: v.tensor_copy(bc_mts[:, :], PS_BIG[:, :]),
              waits=pe2, inc=DV)
        _flush(nc, p)

    def ln_normalize(src_dram, xn_slab, bc_rstd, bc_mts, write_fm_dram=None):
        p = Plan()
        with contextlib.ExitStack() as ph:
            xc = [ph.enter_context(sb(f"lnn_x{i}", [128, T], F32)) for i in range(2)]
            for c in range(DC):
                b = c % 2
                w = Plan.at(DV) if c >= 2 else ()
                p.add("sync", lambda s, i=b, cc=c: s.dma_start(
                    out=xc[i][:, :], in_=src_dram.ap()[cc * 128:(cc + 1) * 128, :]),
                    waits=w, inc=LD, by=16)
                ldv = Plan.at(LD)
                p.add("dve", lambda v, i=b: v.tensor_sub(xc[i][:, :], xc[i][:, :],
                                                         bc_mts[:, :]), waits=ldv)
                p.add("dve", lambda v, i=b, cc=c: v.tensor_mul(
                    xn_slab[:, cc * T:(cc + 1) * T], xc[i][:, :], bc_rstd[:, :]),
                    inc=DV)
                if write_fm_dram is not None:
                    dvv = Plan.at(DV)
                    p.add("sync", lambda s, cc=c: s.dma_start(
                        out=write_fm_dram.ap()[cc * 128:(cc + 1) * 128, :],
                        in_=xn_slab[:, cc * T:(cc + 1) * T]), waits=dvv, inc=LD, by=16)
            _flush(nc, p)

    def load_w(p, dst_sb_ap, src_ap, waits=()):
        p.add("pool", lambda g, d=dst_sb_ap, s=src_ap: g.dma_start(out=d, in_=s),
              waits=waits, inc=GD, by=16)


    def attn_a(src_dram, dst_dram):
        ln_stats(src_dram)
        with contextlib.ExitStack() as ph:
            c_ropeC = ph.enter_context(sb("c_ropeC", [RD, T], BF16))
            c_ropeS = ph.enter_context(sb("c_ropeS", [RD, T], BF16))
            c_masks = ph.enter_context(sb("c_masks", [128, 4 * 512], BF16))
            bc_rstd = ph.enter_context(sb("bc_rstd_a", [128, T], F32))
            bc_mts = ph.enter_context(sb("bc_mts_a", [128, T], F32))
            pld = Plan()
            pld.add("pool", lambda g: g.dma_start(out=c_ropeC[:, :],
                                                  in_=ropeC_d.ap()), inc=GD, by=16)
            pld.add("pool", lambda g: g.dma_start(out=c_ropeS[:, :],
                                                  in_=ropeS_d.ap()), inc=GD, by=16)
            pld.add("pool", lambda g: g.dma_start(
                out=c_masks[:, :].rearrange("p (j f) -> p j f", j=4),
                in_=masks_d.ap().rearrange("(j p) f -> p j f", p=128)),
                inc=GD, by=16)
            pld.add("pool", lambda g: g.wait_ge(GD.h, GD.n) or g.engine_nop())
            _flush(nc, pld)
            ln_rows_and_bcast(bc_rstd, bc_mts)
            cq = ph.enter_context(sb("cq", [128, 4 * T], BF16))
            comp = ph.enter_context(sb("comp", [128, 4 * T], BF16))
            kpe_rot = ph.enter_context(sb("kpe_rot", [RD, T], BF16))
            A_QSQ, A_KSQ, A_KPESQ, A_RECIP = R_SUM, R_SQ, R_MTS, R_EXTRA

            with contextlib.ExitStack() as pp:
                xn = pp.enter_context(sb("xn_a", [128, DC * T], BF16, side="right"))
                ln_normalize(src_dram, xn, bc_rstd, bc_mts)
                wqa_sb = pp.enter_context(sb("wqa_sb", [128, DC * QL], BF16,
                                             side="right"))
                wkva_sb = pp.enter_context(sb("wkva_sb", [128, DC * 576], BF16,
                                              side="right"))
                kpe_raw = pp.enter_context(sb("kpe_raw", [RD, T], BF16,
                                              side="right"))
                rtmp1 = pp.enter_context(sb("rtmp1", [RD, 512], F32, side="right"))
                rtmp2 = pp.enter_context(sb("rtmp2", [RD, 512], F32, side="right"))
                p = Plan()
                load_w(p, wqa_sb[:, :].rearrange("p (k m) -> p k m", k=DC),
                       wqa.ap().rearrange("(k p) m -> p k m", p=128))
                load_w(p, wkva_sb[:, :].rearrange("p (k m) -> p k m", k=DC),
                       wkva.ap().rearrange("(k p) m -> p k m", p=128))
                gdv = Plan.at(GD)
                pss = [PS_A, PS_B]
                g = 0
                for mc in range(4):
                    for tt in range(T4):
                        ps = pss[g % 2]
                        wl = list(gdv)
                        if g >= 2:
                            wl += list(Plan.at(DV))
                        for kc in range(DC):
                            p.add("pe", lambda t, ps=ps, kc=kc, mc=mc, tt=tt: t.matmul(
                                ps[:, :],
                                wqa_sb[:, kc * QL + mc * 128: kc * QL + (mc + 1) * 128],
                                xn[:, kc * T + tt * 512: kc * T + (tt + 1) * 512],
                                start=(kc == 0), stop=(kc == DC - 1)),
                                waits=tuple(wl) if kc == 0 else (),
                                inc=PE if kc == DC - 1 else None)
                        pev = Plan.at(PE)
                        p.add("dve", lambda v, ps=ps, mc=mc, tt=tt: v.tensor_copy(
                            cq[:, mc * T + tt * 512: mc * T + (tt + 1) * 512],
                            ps[:, :]), waits=pev, inc=DV)
                        g += 1
                for mc in range(5):
                    m0 = mc * 128
                    mw = 128 if mc < 4 else RD
                    for tt in range(T4):
                        ps = pss[g % 2]
                        wl = Plan.at(DV) if g >= 2 else ()
                        for kc in range(DC):
                            p.add("pe", lambda t, ps=ps, kc=kc, m0=m0, mw=mw, tt=tt:
                                  t.matmul(
                                      ps[0:mw, :],
                                      wkva_sb[:, kc * 576 + m0: kc * 576 + m0 + mw],
                                      xn[:, kc * T + tt * 512: kc * T + (tt + 1) * 512],
                                      start=(kc == 0), stop=(kc == DC - 1)),
                                  waits=wl if kc == 0 else (),
                                  inc=PE if kc == DC - 1 else None)
                        pev = Plan.at(PE)
                        if mc < 4:
                            p.add("dve", lambda v, ps=ps, mc=mc, tt=tt: v.tensor_copy(
                                comp[:, mc * T + tt * 512: mc * T + (tt + 1) * 512],
                                ps[:, :]), waits=pev, inc=DV)
                        else:
                            p.add("dve", lambda v, ps=ps, tt=tt: v.tensor_copy(
                                kpe_raw[:, tt * 512:(tt + 1) * 512], ps[0:RD, :]),
                                waits=pev, inc=DV)
                        g += 1
                for tt in range(T4):
                    dvv = Plan.at(DV)
                    ps = pss[g % 2]
                    p.add("pe", lambda t, ps=ps, tt=tt: t.matmul(
                        ps[0:RD, :], c_swap[:, :],
                        kpe_raw[:, tt * 512:(tt + 1) * 512], start=True, stop=True),
                        waits=dvv, inc=PE)
                    pev = Plan.at(PE)
                    p.add("dve", lambda v, ps=ps, tt=tt: v.tensor_mul(
                        rtmp1[:, :], ps[0:RD, :], c_ropeS[:, tt * 512:(tt + 1) * 512]),
                        waits=pev, inc=DV)
                    p.add("dve", lambda v, tt=tt: v.tensor_mul(
                        rtmp2[:, :], kpe_raw[:, tt * 512:(tt + 1) * 512],
                        c_ropeC[:, tt * 512:(tt + 1) * 512]))
                    p.add("dve", lambda v, tt=tt: v.tensor_add(
                        kpe_rot[:, tt * 512:(tt + 1) * 512], rtmp2[:, :],
                        rtmp1[:, :]), inc=DV)
                    g += 1
                dvv = Plan.at(DV)
                p.add("act", lambda s: s.activation(kpe_raw[:, :], kpe_rot[:, :],
                                                    AF.Square), waits=dvv, inc=AC)
                acv = Plan.at(AC)
                for tt in range(T4):
                    p.add("pe", lambda t, tt=tt: t.matmul(
                        PS_BIG[0:1, tt * 512:(tt + 1) * 512], c_ones_bf[0:RD, :],
                        kpe_raw[:, tt * 512:(tt + 1) * 512], start=True, stop=True),
                        waits=acv if tt == 0 else (), inc=PE if tt == T4 - 1 else None)
                pev = Plan.at(PE)
                p.add("dve", lambda v: v.tensor_copy(A_KPESQ[:, :], PS_BIG[0:1, :]),
                      waits=pev, inc=DV)
                _flush(nc, p)
            with contextlib.ExitStack() as hw:
                attn_sl = hw.enter_context(sb("attn_sl", [128, DC * T], BF16))
                with contextlib.ExitStack() as hh:
                    wqbh = [hh.enter_context(sb(f"wqbh{i}", [128, 4 * QHD], BF16))
                            for i in range(2)]
                    wkvbh = [hh.enter_context(sb(f"wkvbh{i}", [128, 4 * 256], BF16))
                             for i in range(2)]
                    qn = hh.enter_context(sb("qn", [128, T], BF16))
                    qpe_raw = hh.enter_context(sb("qpe_raw", [RD, T], BF16))
                    qpe = hh.enter_context(sb("qpe", [RD, T], BF16))
                    kn = hh.enter_context(sb("kn", [128, T], BF16))
                    v_tm = hh.enter_context(sb("v_tm", [128, T], BF16))
                    sqt = hh.enter_context(sb("sqt", [128, T], BF16))
                    probs = hh.enter_context(sb("probs", [128, 4 * 512], BF16))
                    rtmp1 = hh.enter_context(sb("h_rt1", [RD, 512], F32))
                    rtmp2 = hh.enter_context(sb("h_rt2", [RD, 512], F32))
                    bias_c = hh.enter_context(sb("bias_c", [128, 1], F32))
                    brow = hh.enter_context(sb("brow", [1, 4], F32))
                    rb = hh.enter_context(sb("rb", [128, 512], F32))
                    p = Plan()
                    wqb_r = wqb.ap().rearrange("(k p) m -> p k m", p=128)
                    wkvb_r = wkvb.ap().rearrange("(k p) m -> p k m", p=128)

                    def load_head(hx, waits=()):
                        load_w(p, wqbh[hx % 2][:, :].rearrange("p (k m) -> p k m", k=4),
                               wqb_r[:, :, hx * QHD:(hx + 1) * QHD], waits=waits)
                        load_w(p, wkvbh[hx % 2][:, :].rearrange("p (k m) -> p k m", k=4),
                               wkvb_r[:, :, hx * 256:(hx + 1) * 256], waits=waits)

                    gd_marks = []
                    load_head(0)
                    gd_marks.append(Plan.at(GD))
                    load_head(1)
                    gd_marks.append(Plan.at(GD))
                    for h in range(H):
                        wsel = wqbh[h % 2]
                        ksel = wkvbh[h % 2]
                        for tt in range(T4):
                            ps = [PS_A, PS_B][tt % 2]
                            wl = list(gd_marks[h]) + list(Plan.at(AC, DV))
                            for kc in range(4):
                                p.add("pe", lambda t, ps=ps, kc=kc, tt=tt, w=wsel:
                                      t.matmul(
                                          ps[:, :],
                                          w[:, kc * QHD: kc * QHD + ND],
                                          cq[:, kc * T + tt * 512:
                                             kc * T + (tt + 1) * 512],
                                          start=(kc == 0), stop=(kc == 3)),
                                      waits=tuple(wl) if kc == 0 else (),
                                      inc=PE if kc == 3 else None)
                            pev = Plan.at(PE)
                            p.add("dve", lambda v, ps=ps, tt=tt: v.tensor_copy(
                                qn[:, tt * 512:(tt + 1) * 512], ps[:, :]),
                                waits=pev, inc=DV)
                        for tt in range(T4):
                            ps = [PS_A, PS_B][tt % 2]
                            wl = Plan.at(DV)
                            for kc in range(4):
                                p.add("pe", lambda t, ps=ps, kc=kc, tt=tt, w=wsel:
                                      t.matmul(
                                          ps[0:RD, :],
                                          w[:, kc * QHD + ND: kc * QHD + QHD],
                                          cq[:, kc * T + tt * 512:
                                             kc * T + (tt + 1) * 512],
                                          start=(kc == 0), stop=(kc == 3)),
                                      waits=wl if kc == 0 else (),
                                      inc=PE if kc == 3 else None)
                            pev = Plan.at(PE)
                            p.add("dve", lambda v, ps=ps, tt=tt: v.tensor_copy(
                                qpe_raw[:, tt * 512:(tt + 1) * 512], ps[0:RD, :]),
                                waits=pev, inc=DV)
                        for tt in range(T4):
                            dvv = Plan.at(DV)
                            ps = [PS_A, PS_B][tt % 2]
                            p.add("pe", lambda t, ps=ps, tt=tt: t.matmul(
                                ps[0:RD, :], c_swap[:, :],
                                qpe_raw[:, tt * 512:(tt + 1) * 512],
                                start=True, stop=True), waits=dvv, inc=PE)
                            pev = Plan.at(PE)
                            p.add("dve", lambda v, ps=ps, tt=tt: v.tensor_mul(
                                rtmp1[:, :], ps[0:RD, :],
                                c_ropeS[:, tt * 512:(tt + 1) * 512]), waits=pev)
                            p.add("dve", lambda v, tt=tt: v.tensor_mul(
                                rtmp2[:, :], qpe_raw[:, tt * 512:(tt + 1) * 512],
                                c_ropeC[:, tt * 512:(tt + 1) * 512]))
                            p.add("dve", lambda v, tt=tt: v.tensor_add(
                                qpe[:, tt * 512:(tt + 1) * 512], rtmp2[:, :],
                                rtmp1[:, :]), inc=DV)
                        for tt in range(T4):
                            ps = [PS_A, PS_B][tt % 2]
                            wl = Plan.at(DV)
                            for kc in range(4):
                                p.add("pe", lambda t, ps=ps, kc=kc, tt=tt, w=ksel:
                                      t.matmul(
                                          ps[:, :],
                                          w[:, kc * 256: kc * 256 + ND],
                                          comp[:, kc * T + tt * 512:
                                               kc * T + (tt + 1) * 512],
                                          start=(kc == 0), stop=(kc == 3)),
                                      waits=wl if kc == 0 else (),
                                      inc=PE if kc == 3 else None)
                            pev = Plan.at(PE)
                            p.add("dve", lambda v, ps=ps, tt=tt: v.tensor_copy(
                                kn[:, tt * 512:(tt + 1) * 512], ps[:, :]),
                                waits=pev, inc=DV)
                        for tti in range(NT):
                            ps = [PS_C, PS_D][tti % 2]
                            wl = Plan.at(DV) if tti >= 2 else ()
                            for kc in range(4):
                                p.add("pe", lambda t, ps=ps, kc=kc, tti=tti, w=ksel:
                                      t.matmul(
                                          ps[:, 0:128],
                                          comp[:, kc * T + tti * 128:
                                               kc * T + (tti + 1) * 128],
                                          w[:, kc * 256 + ND: kc * 256 + 256],
                                          start=(kc == 0), stop=(kc == 3)),
                                      waits=wl if kc == 0 else (),
                                      inc=PE if kc == 3 else None)
                            pev = Plan.at(PE)
                            p.add("dve", lambda v, ps=ps, tti=tti: v.tensor_copy(
                                v_tm[:, tti * 128:(tti + 1) * 128], ps[:, 0:128]),
                                waits=pev, inc=DV)
                        if h + 2 < H:
                            load_head(h + 2, waits=Plan.at(PE))
                            gd_marks.append(Plan.at(GD))
                        dvv = Plan.at(DV)
                        p.add("act", lambda s: s.activation(sqt[:, :], qn[:, :],
                                                            AF.Square),
                              waits=dvv, inc=AC)
                        acv = Plan.at(AC)
                        for tt in range(T4):
                            p.add("pe", lambda t, tt=tt: t.matmul(
                                PS_BIG[0:1, tt * 512:(tt + 1) * 512], c_ones_bf[:, :],
                                sqt[:, tt * 512:(tt + 1) * 512],
                                start=True, stop=False),
                                waits=acv if tt == 0 else (),
                                inc=PE if tt == T4 - 1 else None)
                        p.add("act", lambda s: s.activation(sqt[0:RD, 0:T], qpe[:, :],
                                                            AF.Square),
                              waits=Plan.at(PE), inc=AC)
                        acv = Plan.at(AC)
                        for tt in range(T4):
                            p.add("pe", lambda t, tt=tt: t.matmul(
                                PS_BIG[0:1, tt * 512:(tt + 1) * 512],
                                c_ones_bf[0:RD, :],
                                sqt[0:RD, tt * 512:(tt + 1) * 512],
                                start=False, stop=True),
                                waits=acv if tt == 0 else (),
                                inc=PE if tt == T4 - 1 else None)
                        pev = Plan.at(PE)
                        p.add("dve", lambda v: v.tensor_copy(A_QSQ[:, :],
                                                             PS_BIG[0:1, :]),
                              waits=pev, inc=DV)
                        dv2 = Plan.at(DV)
                        p.add("act", lambda s: s.activation(sqt[:, :], kn[:, :],
                                                            AF.Square),
                              waits=tuple(list(dv2) + list(Plan.at(PE))), inc=AC)
                        acv = Plan.at(AC)
                        for tt in range(T4):
                            p.add("pe", lambda t, tt=tt: t.matmul(
                                PS_BIG[0:1, tt * 512:(tt + 1) * 512], c_ones_bf[:, :],
                                sqt[:, tt * 512:(tt + 1) * 512],
                                start=True, stop=True),
                                waits=acv if tt == 0 else (),
                                inc=PE if tt == T4 - 1 else None)
                        pev = Plan.at(PE)
                        p.add("dve", lambda v: v.tensor_copy(A_KSQ[:, :],
                                                             PS_BIG[0:1, :]),
                              waits=pev)
                        p.add("dve", lambda v: v.tensor_add(A_KSQ[:, :], A_KSQ[:, :],
                                                            A_KPESQ[:, :]))
                        p.add("dve", lambda v: v.reduce_max(brow[:, 0:1], A_QSQ[:, :],
                                                            axis=AX), inc=DV)
                        p.add("dve", lambda v: v.reduce_max(brow[:, 1:2], A_KSQ[:, :],
                                                            axis=AX),
                              waits=Plan.at(DV), inc=DV)
                        p.add("dve", lambda v: v.tensor_tensor(
                            brow[:, 2:3], brow[:, 0:1], brow[:, 1:2], ALU.add),
                              waits=Plan.at(DV), inc=DV)
                        p.add("dve", lambda v: v.tensor_scalar_mul(
                            brow[:, 3:4], brow[:, 2:3], -0.5 * SCALE_A),
                              waits=Plan.at(DV), inc=DV)
                        dv3 = Plan.at(DV)
                        p.add("pe", lambda t: t.matmul(
                            PS_C[:, 0:1], c_onesrow_f[:, :], brow[:, 3:4],
                            start=True, stop=True), waits=dv3, inc=PE)
                        pev = Plan.at(PE)
                        p.add("dve", lambda v: v.tensor_copy(bias_c[:, :],
                                                             PS_C[:, 0:1]),
                              waits=pev, inc=DV)
                        bias_ready = Plan.at(DV)
                        norm_done = [Plan.at(DV)]
                        exp_marks = {}
                        pv_marks = {}
                        for qt in range(T4):
                            nkt = 4 * qt + 4
                            LAG = 2

                            def emit_sumpv(kt, qt=qt, nkt=nkt):
                                o = (kt % 4) * 512
                                wl0 = list(exp_marks[(qt, kt)])
                                if kt == 0:
                                    wl0 += list(norm_done[0])
                                p.add("pe", lambda t, o=o, kt=kt, nkt=nkt:
                                      t.matmul(
                                          PS_C[0:1, :], c_ones_bf[:, :],
                                          probs[:, o:o + 512],
                                          start=(kt == 0), stop=(kt == nkt - 1)),
                                      waits=tuple(wl0))
                                p.add("pe", lambda t, o=o, kt=kt, nkt=nkt:
                                      t.matmul(
                                          PS_D[:, :],
                                          v_tm[:, kt * 128:(kt + 1) * 128],
                                          probs[:, o:o + 512],
                                          start=(kt == 0), stop=(kt == nkt - 1)),
                                      inc=PE)
                                pv_marks[(qt, kt)] = Plan.at(PE)

                            for kt in range(nkt):
                                sb_off = (kt % 4) * 512
                                wl = []
                                if kt >= 4:
                                    wl += list(exp_marks[(qt, kt - 4)])
                                p.add("pe", lambda t, kt=kt, qt=qt, o=sb_off:
                                      t.matmul(
                                          PS_BIG[:, o:o + 512],
                                          kn[:, kt * 128:(kt + 1) * 128],
                                          qn[:, qt * 512:(qt + 1) * 512],
                                          start=True, stop=False),
                                      waits=tuple(wl) if wl else ())
                                p.add("pe", lambda t, kt=kt, qt=qt, o=sb_off:
                                      t.matmul(
                                          PS_BIG[:, o:o + 512],
                                          kpe_rot[:, kt * 128:(kt + 1) * 128],
                                          qpe[:, qt * 512:(qt + 1) * 512],
                                          start=False, stop=True), inc=PE)
                                pev = Plan.at(PE)
                                wle = list(pev) + list(bias_ready)
                                if kt >= 4 and (qt, kt - 4) in pv_marks:
                                    wle += list(pv_marks[(qt, kt - 4)])
                                p.add("act", lambda s, o=sb_off: s.activation(
                                    probs[:, o:o + 512], PS_BIG[:, o:o + 512],
                                    AF.Exp, bias=bias_c[:, :], scale=SCALE_A),
                                    waits=tuple(wle), inc=AC)
                                if kt >= 4 * qt:
                                    acv = Plan.at(AC)
                                    j = kt - 4 * qt
                                    p.add("dve", lambda v, o=sb_off, j=j: v.tensor_mul(
                                        probs[:, o:o + 512], probs[:, o:o + 512],
                                        c_masks[:, j * 512:(j + 1) * 512]),
                                        waits=acv, inc=DV)
                                    exp_marks[(qt, kt)] = Plan.at(DV)
                                else:
                                    exp_marks[(qt, kt)] = Plan.at(AC)
                                if kt >= LAG:
                                    emit_sumpv(kt - LAG)
                            for kt in range(max(0, nkt - LAG), nkt):
                                emit_sumpv(kt)
                            pev = Plan.at(PE)
                            p.add("dve", lambda v: v.reciprocal(A_RECIP[:, :],
                                                                PS_C[0:1, :]),
                                  waits=pev, inc=DV)
                            dvv = Plan.at(DV)
                            p.add("pe", lambda t: t.matmul(
                                PS_C[:, :], c_onesrow_f[:, :], A_RECIP[:, :],
                                start=True, stop=True), waits=dvv, inc=PE)
                            pev = Plan.at(PE)
                            p.add("dve", lambda v: v.tensor_copy(rb[:, :], PS_C[:, :]),
                                  waits=pev)
                            p.add("dve", lambda v, h=h, qt=qt: v.tensor_mul(
                                attn_sl[:, h * T + qt * 512: h * T + (qt + 1) * 512],
                                PS_D[:, :], rb[:, :]), inc=DV)
                            norm_done[0] = Plan.at(DV)
                    _flush(nc, p)
                with contextlib.ExitStack() as wo_s:
                    wo_sb = [wo_s.enter_context(sb(f"wo_sb{i}", [128, DC * 128], BF16))
                             for i in range(2)]
                    xo = [wo_s.enter_context(sb(f"xo{i}", [128, T], F32))
                          for i in range(2)]
                    p = Plan()
                    woa_r = woa.ap().rearrange("(k p) m -> p k m", p=128)
                    gd_marks = []
                    pe_marks_o = []
                    for mc in range(DC):
                        load_w(p, wo_sb[mc % 2][:, :].rearrange("p (k m) -> p k m",
                                                                k=DC),
                               woa_r[:, :, mc * 128:(mc + 1) * 128],
                               waits=pe_marks_o[mc - 2] if mc >= 2 else ())
                        gd_marks.append(Plan.at(GD))
                        p.add("sync", lambda s, i=mc % 2, mc=mc: s.dma_start(
                            out=xo[i][:, :],
                            in_=src_dram.ap()[mc * 128:(mc + 1) * 128, :]),
                            waits=Plan.at(DV) if mc >= 2 else (), inc=LD, by=16)
                        ldv = Plan.at(LD)
                        for tt in range(T4):
                            ps = [PS_A, PS_B][tt % 2]
                            wl = list(gd_marks[-1]) + list(ldv)
                            if mc * T4 + tt >= 2:
                                wl += list(Plan.at(DV))
                            for kc in range(DC):
                                p.add("pe", lambda t, ps=ps, kc=kc, mc=mc, tt=tt:
                                      t.matmul(
                                          ps[:, :],
                                          wo_sb[mc % 2][:, kc * 128:(kc + 1) * 128],
                                          attn_sl[:, kc * T + tt * 512:
                                                  kc * T + (tt + 1) * 512],
                                          start=(kc == 0), stop=(kc == DC - 1)),
                                      waits=tuple(wl) if kc == 0 else (),
                                      inc=PE if kc == DC - 1 else None)
                            pev = Plan.at(PE)
                            p.add("dve", lambda v, ps=ps, i=mc % 2, tt=tt:
                                  v.tensor_add(
                                      xo[i][:, tt * 512:(tt + 1) * 512], ps[:, :],
                                      xo[i][:, tt * 512:(tt + 1) * 512]),
                                  waits=pev, inc=DV)
                        pe_marks_o.append(Plan.at(PE))
                        dvv = Plan.at(DV)
                        p.add("sync", lambda s, i=mc % 2, mc=mc: s.dma_start(
                            out=dst_dram.ap()[mc * 128:(mc + 1) * 128, :],
                            in_=xo[i][:, :]), waits=dvv, inc=LD, by=16)
                    _flush(nc, p)


    def attn_b(src_dram, dst_dram):
        ln_stats(src_dram)
        with contextlib.ExitStack() as ph:
            bc_rstd = ph.enter_context(sb("bc_rstd_b", [128, T], F32))
            bc_mts = ph.enter_context(sb("bc_mts_b", [128, T], F32))
            ln_rows_and_bcast(bc_rstd, bc_mts)
            A_QSQ, A_KSQ, A_RECIP = R_SUM, R_SQ, R_EXTRA
            with contextlib.ExitStack() as bx:
                xn = bx.enter_context(sb("xn_b", [128, DC * T], BF16, side="right"))
                ln_normalize(src_dram, xn, bc_rstd, bc_mts)
                with contextlib.ExitStack() as hh:
                    wqb_ = [hh.enter_context(sb(f"bwq{i}", [128, DC * 256], BF16))
                            for i in range(2)]
                    wkb_ = [hh.enter_context(sb(f"bwk{i}", [128, DC * 256], BF16))
                            for i in range(2)]
                    wvb_ = [hh.enter_context(sb(f"bwv{i}", [128, DC * 256], BF16))
                            for i in range(2)]
                    q2 = hh.enter_context(sb("q2", [128, 2 * T], BF16))
                    k2 = hh.enter_context(sb("k2", [128, 2 * T], BF16))
                    v2 = hh.enter_context(sb("v2", [128, NT * 256], BF16))
                    sqt = hh.enter_context(sb("sqt_b", [128, T], BF16))
                    probs = hh.enter_context(sb("probs_b", [128, 4 * 512], BF16))
                    bias_c = hh.enter_context(sb("bias_cb", [128, 1], F32))
                    brow = hh.enter_context(sb("brow_b", [1, 4], F32))
                    rb = hh.enter_context(sb("rb_b", [128, 512], F32))
                    abuf = [hh.enter_context(sb(f"abuf{i}", [128, T], BF16))
                            for i in range(2)]
                    p = Plan()
                    wq_r = wq_b.ap().rearrange("(k p) m -> p k m", p=128)
                    wk_r = wk_b.ap().rearrange("(k p) m -> p k m", p=128)
                    wv_r = wv_b.ap().rearrange("(k p) m -> p k m", p=128)

                    def load_grp(gx, waits=()):
                        load_w(p, wqb_[gx % 2][:, :].rearrange(
                            "p (k m) -> p k m", k=DC),
                            wq_r[:, :, gx * 256:(gx + 1) * 256], waits=waits)
                        load_w(p, wkb_[gx % 2][:, :].rearrange(
                            "p (k m) -> p k m", k=DC),
                            wk_r[:, :, gx * 256:(gx + 1) * 256])
                        load_w(p, wvb_[gx % 2][:, :].rearrange(
                            "p (k m) -> p k m", k=DC),
                            wv_r[:, :, gx * 256:(gx + 1) * 256])

                    gd_marks = []
                    load_grp(0)
                    gd_marks.append(Plan.at(GD))
                    load_grp(1)
                    gd_marks.append(Plan.at(GD))
                    ab_store_marks = {}
                    for g in range(8):
                        wsel, ksel, vsel = wqb_[g % 2], wkb_[g % 2], wvb_[g % 2]
                        for (dst_sl, wmat) in ((q2, wsel), (k2, ksel)):
                            for hh_i in range(2):
                                for tt in range(T4):
                                    ps = [PS_A, PS_B][tt % 2]
                                    wl = list(gd_marks[g]) + list(Plan.at(DV))
                                    for kc in range(DC):
                                        p.add("pe", lambda t, ps=ps, kc=kc,
                                              hh_i=hh_i, tt=tt, w=wmat: t.matmul(
                                                  ps[:, :],
                                                  w[:, kc * 256 + hh_i * 128:
                                                    kc * 256 + (hh_i + 1) * 128],
                                                  xn[:, kc * T + tt * 512:
                                                     kc * T + (tt + 1) * 512],
                                                  start=(kc == 0),
                                                  stop=(kc == DC - 1)),
                                              waits=tuple(wl) if kc == 0 else (),
                                              inc=PE if kc == DC - 1 else None)
                                    pev = Plan.at(PE)
                                    p.add("dve", lambda v, ps=ps, d=dst_sl,
                                          hh_i=hh_i, tt=tt: v.tensor_copy(
                                              d[:, hh_i * T + tt * 512:
                                                hh_i * T + (tt + 1) * 512],
                                              ps[:, :]), waits=pev, inc=DV)
                        for tti in range(NT):
                            ps = [PS_C, PS_D][tti % 2]
                            wl = Plan.at(DV)
                            for kc in range(DC):
                                p.add("pe", lambda t, ps=ps, kc=kc, tti=tti,
                                      w=vsel: t.matmul(
                                          ps[:, 0:256],
                                          xn[:, kc * T + tti * 128:
                                             kc * T + (tti + 1) * 128],
                                          w[:, kc * 256:(kc + 1) * 256],
                                          start=(kc == 0), stop=(kc == DC - 1)),
                                      waits=wl if kc == 0 else (),
                                      inc=PE if kc == DC - 1 else None)
                            pev = Plan.at(PE)
                            p.add("dve", lambda v, ps=ps, tti=tti: v.tensor_copy(
                                v2[:, tti * 256:(tti + 1) * 256], ps[:, 0:256]),
                                waits=pev, inc=DV)
                        if g + 2 < 8:
                            load_grp(g + 2, waits=Plan.at(PE))
                            gd_marks.append(Plan.at(GD))
                        for hh_i in range(2):
                            h = 2 * g + hh_i
                            dvv = Plan.at(DV)
                            p.add("act", lambda s, hh_i=hh_i: s.activation(
                                sqt[:, :], q2[:, hh_i * T:(hh_i + 1) * T],
                                AF.Square), waits=dvv, inc=AC)
                            acv = Plan.at(AC)
                            for tt in range(T4):
                                p.add("pe", lambda t, tt=tt: t.matmul(
                                    PS_BIG[0:1, tt * 512:(tt + 1) * 512],
                                    c_ones_bf[:, :],
                                    sqt[:, tt * 512:(tt + 1) * 512],
                                    start=True, stop=True),
                                    waits=acv if tt == 0 else (),
                                    inc=PE if tt == T4 - 1 else None)
                            pev = Plan.at(PE)
                            p.add("dve", lambda v: v.tensor_copy(
                                A_QSQ[:, :], PS_BIG[0:1, :]), waits=pev, inc=DV)
                            p.add("act", lambda s, hh_i=hh_i: s.activation(
                                sqt[:, :], k2[:, hh_i * T:(hh_i + 1) * T],
                                AF.Square),
                                waits=tuple(list(Plan.at(DV)) + list(Plan.at(PE))),
                                inc=AC)
                            acv = Plan.at(AC)
                            for tt in range(T4):
                                p.add("pe", lambda t, tt=tt: t.matmul(
                                    PS_BIG[0:1, tt * 512:(tt + 1) * 512],
                                    c_ones_bf[:, :],
                                    sqt[:, tt * 512:(tt + 1) * 512],
                                    start=True, stop=True),
                                    waits=acv if tt == 0 else (),
                                    inc=PE if tt == T4 - 1 else None)
                            pev = Plan.at(PE)
                            p.add("dve", lambda v: v.tensor_copy(
                                A_KSQ[:, :], PS_BIG[0:1, :]), waits=pev, inc=DV)
                            p.add("dve", lambda v: v.reduce_max(
                                brow[:, 0:1], A_QSQ[:, :], axis=AX),
                                waits=Plan.at(DV), inc=DV)
                            p.add("dve", lambda v: v.reduce_max(
                                brow[:, 1:2], A_KSQ[:, :], axis=AX),
                                waits=Plan.at(DV), inc=DV)
                            p.add("dve", lambda v: v.tensor_tensor(
                                brow[:, 2:3], brow[:, 0:1], brow[:, 1:2],
                                ALU.add), waits=Plan.at(DV), inc=DV)
                            p.add("dve", lambda v: v.tensor_scalar_mul(
                                brow[:, 3:4], brow[:, 2:3], -0.5 * SCALE_B),
                                waits=Plan.at(DV), inc=DV)
                            dv3 = Plan.at(DV)
                            p.add("pe", lambda t: t.matmul(
                                PS_C[:, 0:1], c_onesrow_f[:, :], brow[:, 3:4],
                                start=True, stop=True), waits=dv3, inc=PE)
                            pev = Plan.at(PE)
                            p.add("dve", lambda v: v.tensor_copy(
                                bias_c[:, :], PS_C[:, 0:1]), waits=pev, inc=DV)
                            bias_ready = Plan.at(DV)
                            norm_done = [Plan.at(DV)]
                            exp_marks = {}
                            pv_marks = {}
                            for qt in range(T4):
                                LAG = 2

                                def emit_sumpv(kt, qt=qt, hh_i=hh_i):
                                    o = (kt % 4) * 512
                                    wl0 = list(exp_marks[(qt, kt)])
                                    if kt == 0:
                                        wl0 += list(norm_done[0])
                                    p.add("pe", lambda t, o=o, kt=kt: t.matmul(
                                        PS_C[0:1, :], c_ones_bf[:, :],
                                        probs[:, o:o + 512],
                                        start=(kt == 0), stop=(kt == NT - 1)),
                                        waits=tuple(wl0))
                                    p.add("pe", lambda t, o=o, kt=kt, hh_i=hh_i:
                                          t.matmul(
                                              PS_D[:, :],
                                              v2[:, kt * 256 + hh_i * 128:
                                                 kt * 256 + (hh_i + 1) * 128],
                                              probs[:, o:o + 512],
                                              start=(kt == 0),
                                              stop=(kt == NT - 1)),
                                          inc=PE)
                                    pv_marks[(qt, kt)] = Plan.at(PE)

                                for kt in range(NT):
                                    o = (kt % 4) * 512
                                    wl = []
                                    if kt >= 4:
                                        wl += list(exp_marks[(qt, kt - 4)])
                                    p.add("pe", lambda t, hh_i=hh_i, kt=kt,
                                          qt=qt, o=o: t.matmul(
                                              PS_BIG[:, o:o + 512],
                                              k2[:, hh_i * T + kt * 128:
                                                 hh_i * T + (kt + 1) * 128],
                                              q2[:, hh_i * T + qt * 512:
                                                 hh_i * T + (qt + 1) * 512],
                                              start=True, stop=True),
                                          waits=tuple(wl) if wl else (), inc=PE)
                                    pev = Plan.at(PE)
                                    wle = list(pev) + list(bias_ready)
                                    if kt >= 4 and (qt, kt - 4) in pv_marks:
                                        wle += list(pv_marks[(qt, kt - 4)])
                                    p.add("act", lambda s, o=o: s.activation(
                                        probs[:, o:o + 512],
                                        PS_BIG[:, o:o + 512], AF.Exp,
                                        bias=bias_c[:, :], scale=SCALE_B),
                                        waits=tuple(wle), inc=AC)
                                    exp_marks[(qt, kt)] = Plan.at(AC)
                                    if kt >= LAG:
                                        emit_sumpv(kt - LAG)
                                for kt in range(NT - LAG, NT):
                                    emit_sumpv(kt)
                                pev = Plan.at(PE)
                                p.add("dve", lambda v: v.reciprocal(
                                    A_RECIP[:, :], PS_C[0:1, :]),
                                    waits=pev, inc=DV)
                                dvv = Plan.at(DV)
                                p.add("pe", lambda t: t.matmul(
                                    PS_C[:, :], c_onesrow_f[:, :], A_RECIP[:, :],
                                    start=True, stop=True), waits=dvv, inc=PE)
                                pev = Plan.at(PE)
                                p.add("dve", lambda v: v.tensor_copy(
                                    rb[:, :], PS_C[:, :]), waits=pev)
                                wl = [*Plan.at(PE)]
                                if h >= 2:
                                    wl += list(ab_store_marks[h - 2])
                                p.add("dve", lambda v, h=h, qt=qt: v.tensor_mul(
                                    abuf[h % 2][:, qt * 512:(qt + 1) * 512],
                                    PS_D[:, :], rb[:, :]),
                                    waits=tuple(wl), inc=DV)
                                norm_done[0] = Plan.at(DV)
                            dvv = Plan.at(DV)
                            p.add("sync", lambda s, h=h: s.dma_start(
                                out=xnfm_d.ap()[h * 128:(h + 1) * 128, :],
                                in_=abuf[h % 2][:, :]), waits=dvv, inc=LD, by=16)
                            ab_store_marks[h] = Plan.at(LD)
                    _flush(nc, p)
            with contextlib.ExitStack() as wo_s:
                attn_sl = wo_s.enter_context(sb("attn_slb", [128, DC * T], BF16,
                                                side="right"))
                wo_sb = [wo_s.enter_context(sb(f"bwo{i}", [128, DC * 128], BF16))
                         for i in range(2)]
                xo = [wo_s.enter_context(sb(f"bxo{i}", [128, T], F32))
                      for i in range(2)]
                p = Plan()
                for c in range(DC):
                    p.add("sync", lambda s, c=c: s.dma_start(
                        out=attn_sl[:, c * T:(c + 1) * T],
                        in_=xnfm_d.ap()[c * 128:(c + 1) * 128, :]),
                        inc=LD, by=16)
                aload = Plan.at(LD)
                wob_r = wob.ap().rearrange("(k p) m -> p k m", p=128)
                gd_marks = []
                pe_marks_o = []
                for mc in range(DC):
                    load_w(p, wo_sb[mc % 2][:, :].rearrange(
                        "p (k m) -> p k m", k=DC),
                        wob_r[:, :, mc * 128:(mc + 1) * 128],
                        waits=pe_marks_o[mc - 2] if mc >= 2 else ())
                    gd_marks.append(Plan.at(GD))
                    p.add("sync", lambda s, i=mc % 2, mc=mc: s.dma_start(
                        out=xo[i][:, :],
                        in_=src_dram.ap()[mc * 128:(mc + 1) * 128, :]),
                        waits=Plan.at(DV) if mc >= 2 else (), inc=LD, by=16)
                    ldv = Plan.at(LD)
                    for tt in range(T4):
                        ps = [PS_A, PS_B][tt % 2]
                        wl = list(gd_marks[-1]) + list(ldv) + list(aload)
                        if mc * T4 + tt >= 2:
                            wl += list(Plan.at(DV))
                        for kc in range(DC):
                            p.add("pe", lambda t, ps=ps, kc=kc, mc=mc, tt=tt:
                                  t.matmul(
                                      ps[:, :],
                                      wo_sb[mc % 2][:, kc * 128:(kc + 1) * 128],
                                      attn_sl[:, kc * T + tt * 512:
                                              kc * T + (tt + 1) * 512],
                                      start=(kc == 0), stop=(kc == DC - 1)),
                                  waits=tuple(wl) if kc == 0 else (),
                                  inc=PE if kc == DC - 1 else None)
                        pev = Plan.at(PE)
                        p.add("dve", lambda v, ps=ps, i=mc % 2, tt=tt:
                              v.tensor_add(
                                  xo[i][:, tt * 512:(tt + 1) * 512], ps[:, :],
                                  xo[i][:, tt * 512:(tt + 1) * 512]),
                              waits=pev, inc=DV)
                    pe_marks_o.append(Plan.at(PE))
                    dvv = Plan.at(DV)
                    p.add("sync", lambda s, i=mc % 2, mc=mc: s.dma_start(
                        out=dst_dram.ap()[mc * 128:(mc + 1) * 128, :],
                        in_=xo[i][:, :]), waits=dvv, inc=LD, by=16)
                p.add("sync", lambda s: s.wait_ge(LD.h, LD.n) or s.engine_nop())
                _flush(nc, p)


    def ffn(src_dram, dst_dram, is_last):
        ln_stats(src_dram)
        with contextlib.ExitStack() as f0:
            gk_s = f0.enter_context(sb("gk_s", [128, NT], F32))
            gate_s = f0.enter_context(sb("gate_s", [128, NT], F32))
            retg_u = f0.enter_context(sb("retg_u", [128, NT], U32))
            sidx_u = f0.enter_context(sb("sidx_u", [128, NT], U32))
            oh_s = f0.enter_context(sb("oh_s", [128, E * NT], F32))
            offs_sb = f0.enter_context(sb("offs_sb", [8, 1], F32))
            cnt_sb = f0.enter_context(sb("cnt_sb", [64, 1], F32))
            rcol = f0.enter_context(sb("rcol", [128, 8], F32))

            with contextlib.ExitStack() as fx:
                xn = fx.enter_context(sb("xn_f", [128, DC * T], BF16, side="right"))
                with contextlib.ExitStack() as fb:
                    bc_rstd = fb.enter_context(sb("bc_rstd_f", [128, T], F32))
                    bc_mts = fb.enter_context(sb("bc_mts_f", [128, T], F32))
                    ln_rows_and_bcast(bc_rstd, bc_mts)
                    ln_normalize(src_dram, xn, bc_rstd, bc_mts,
                                 write_fm_dram=xnfm_d)

                with contextlib.ExitStack() as f2:
                    strip = [f2.enter_context(sb(f"strip{i}", [128, D], BF16))
                             for i in range(2)]
                    p = Plan()
                    xn_written = Plan.at(LD)
                    st_marks = []
                    for i in range(NT):
                        wl = list(xn_written)
                        if i >= 2:
                            wl += list(st_marks[i - 2])
                        p.add("sync", lambda s, i=i: s.dma_start(
                            out=strip[i % 2][:, :],
                            in_=xnfm_d.ap()[:, i * 128:(i + 1) * 128],
                            transpose=True), waits=tuple(wl), inc=LD, by=16)
                        ldv = Plan.at(LD)
                        p.add("sync", lambda s, i=i: s.dma_start(
                            out=ag_x_in.ap()[i * 128:(i + 1) * 128, :],
                            in_=strip[i % 2][:, :]), waits=ldv, inc=LD, by=16)
                        st_marks.append(Plan.at(LD))
                    agin_done = Plan.at(LD)
                    p.add("pool", lambda g: g.collective_compute(
                        "AllGather", ALU.bypass,
                        replica_groups=[list(range(NCORE))],
                        ins=[ag_x_in.ap().opt()], outs=[xall.ap().opt()]),
                        waits=agin_done, inc=CC)
                    _flush(nc, p)
                ag1_done = (CC, CC.n)

                with contextlib.ExitStack() as f3:
                    wg_sb = f3.enter_context(sb("wg_sb", [128, DC * E], BF16))
                    oh_T = f3.enter_context(sb("oh_T", [E, T], F32))
                    incl_T = f3.enter_context(sb("incl_T", [E, T], F32))
                    lg = f3.enter_context(sb("lg", [128, E], F32))
                    pexp = f3.enter_context(sb("pexp", [128, E], F32))
                    t8 = f3.enter_context(sb("t8", [128, E], F32))
                    inc_tm = f3.enter_context(sb("inc_tm", [128, E], F32))
                    zz = f3.enter_context(sb("zz", [128, 129], F32))
                    p = Plan()
                    load_w(p, wg_sb[:, :].rearrange("p (k m) -> p k m", k=DC),
                           wg.ap().rearrange("(k p) m -> p k m", p=128))
                    gdv = Plan.at(GD)
                    p.add("pool", lambda g: g.memset(zz[:, :], 0.0), inc=GD)
                    zz_done = Plan.at(GD)
                    p.add("pool", lambda g: g.dma_start(
                        out=slot_loc.ap().rearrange("(a b) c -> a (b c)", a=128),
                        in_=zz[:, :]), waits=zz_done, inc=GD, by=16)
                    slotz_done = Plan.at(GD)
                    for i in range(NT):
                        ps = [PS_A, PS_B][i % 2]
                        wl = list(gdv)
                        if i >= 2:
                            wl += list(Plan.at(DV))
                        for kc in range(DC):
                            p.add("pe", lambda t, ps=ps, kc=kc, i=i: t.matmul(
                                ps[:, 0:E],
                                xn[:, kc * T + i * 128: kc * T + (i + 1) * 128],
                                wg_sb[:, kc * E:(kc + 1) * E],
                                start=(kc == 0), stop=(kc == DC - 1)),
                                waits=tuple(wl) if kc == 0 else (),
                                inc=PE if kc == DC - 1 else None)
                        pev = Plan.at(PE)

                        def pd(fn, extra=()):
                            p.add("dve", fn,
                                  waits=tuple(list(Plan.at(DV, AC)) + list(extra)),
                                  inc=DV)

                        pd(lambda v, ps=ps: v.tensor_copy(lg[:, :], ps[:, 0:E]),
                           extra=pev)
                        dvv = Plan.at(DV)
                        p.add("act", lambda s: s.activation(
                            pexp[:, :], lg[:, :], AF.Exp,
                            accum_out=rcol[:, 0:1]), waits=dvv, inc=AC)
                        pd(lambda v: v.reduce_max(rcol[:, 1:2], pexp[:, :], axis=AX))
                        pd(lambda v: v.reciprocal(rcol[:, 2:3], rcol[:, 0:1]))
                        pd(lambda v, i=i: v.tensor_mul(
                            gate_s[:, i:i + 1], rcol[:, 1:2], rcol[:, 2:3]))
                        pd(lambda v: v.reduce_max(rcol[:, 3:4], lg[:, :], axis=AX))
                        pd(lambda v: v.tensor_scalar(
                            t8[:, :], lg[:, :], rcol[:, 3:4], None, ALU.subtract))
                        pd(lambda v, i=i: v.tensor_single_scalar(
                            oh_s[:, i * E:(i + 1) * E], t8[:, :], 0.0, ALU.is_ge))
                        if DBG and i == 0:
                            p.add("pool", lambda g: g.dma_start(
                                out=dbg2_lg.ap(), in_=lg[:, :]),
                                waits=Plan.at(DV), inc=GD, by=16)
                            p.add("pool", lambda g: g.dma_start(
                                out=dbg2_pexp.ap(), in_=pexp[:, :]), inc=GD, by=16)
                            p.add("pool", lambda g: g.dma_start(
                                out=dbg2_rcol.ap(), in_=rcol[:, :]), inc=GD, by=16)
                            gddbg = Plan.at(GD)
                            p.add("dve", lambda v: v.engine_nop(), waits=gddbg)
                        p.add("dve", lambda v: v.engine_nop(), inc=DV)
                        dvo = Plan.at(DV)
                        p.add("pe", lambda t, i=i: t.transpose(
                            PS_C[0:E, 0:128], oh_s[:, i * E:(i + 1) * E],
                            c_ident_f[:, :]), waits=dvo, inc=PE)
                        pev2 = Plan.at(PE)
                        p.add("dve", lambda v, i=i: v.tensor_copy(
                            oh_T[:, i * 128:(i + 1) * 128], PS_C[0:E, 0:128]),
                            waits=pev2, inc=DV)
                    dva = Plan.at(DV)
                    p.add("dve", lambda v: v.reduce_sum(cnt_sb[0:E, 0:1], oh_T[:, :],
                                                        axis=AX), waits=dva, inc=DV)
                    dvc = Plan.at(DV)
                    p.add("pool", lambda g: g.dma_start(out=cnt_in.ap(),
                                                        in_=cnt_sb[0:E, 0:1]),
                          waits=dvc, inc=GD, by=16)
                    cnt_w = Plan.at(GD)
                    p.add("pool", lambda g: g.collective_compute(
                        "AllGather", ALU.bypass,
                        replica_groups=[list(range(NCORE))],
                        ins=[cnt_in.ap().opt()], outs=[cnt_all.ap().opt()]),
                        waits=cnt_w, inc=CC)
                    ccv = Plan.at(CC)
                    p.add("pool", lambda g: g.dma_start(out=cnt_sb[:, :],
                                                        in_=cnt_all.ap()),
                          waits=ccv, inc=GD, by=16)
                    cnt_l = Plan.at(GD)
                    p.add("pe", lambda t: t.matmul(PS_D[0:E, 0:1], c_rankmsk[:, :],
                                                   cnt_sb[:, :], start=True,
                                                   stop=True), waits=cnt_l, inc=PE)
                    pev = Plan.at(PE)
                    p.add("dve", lambda v: v.tensor_copy(offs_sb[:, :],
                                                         PS_D[0:E, 0:1]),
                          waits=pev, inc=DV)
                    dvo = Plan.at(DV)
                    p.add("dve", lambda v: v.tensor_tensor_scan(
                        incl_T[:, :], oh_T[:, :], oh_T[:, :], offs_sb[:, :],
                        ALU.add, ALU.bypass), waits=dvo, inc=DV)
                    scan_done = Plan.at(DV)
                    evac_marks = []
                    for i in range(NT):
                        wtr = list(scan_done) if i == 0 else []
                        if i >= 1:
                            wtr += list(evac_marks[i - 1])
                        p.add("pe", lambda t, i=i: t.transpose(
                            PS_C[:, 0:E], incl_T[:, i * 128:(i + 1) * 128],
                            c_ident_f[0:E, 0:E]),
                            waits=tuple(wtr), inc=PE)
                        pev = Plan.at(PE)

                        def pd(fn, extra=()):
                            p.add("dve", fn,
                                  waits=tuple(list(Plan.at(DV)) + list(extra)),
                                  inc=DV)

                        pd(lambda v: v.tensor_copy(inc_tm[:, :], PS_C[:, 0:E]),
                           extra=pev)
                        evac_marks.append(Plan.at(DV))
                        pd(lambda v, i=i: v.tensor_mul(
                            t8[:, :], oh_s[:, i * E:(i + 1) * E], inc_tm[:, :]))
                        pd(lambda v: v.reduce_sum(rcol[:, 0:1], t8[:, :], axis=AX))
                        pd(lambda v, i=i: v.tensor_mul(
                            t8[:, :], oh_s[:, i * E:(i + 1) * E], c_iota8[:, :]))
                        pd(lambda v: v.reduce_sum(rcol[:, 1:2], t8[:, :], axis=AX))
                        pd(lambda v: v.tensor_single_scalar(
                            rcol[:, 2:3], rcol[:, 0:1], float(CAP), ALU.is_le))
                        pd(lambda v, i=i: v.tensor_mul(
                            gk_s[:, i:i + 1], gate_s[:, i:i + 1], rcol[:, 2:3]))
                        pd(lambda v: v.scalar_tensor_tensor(
                            rcol[:, 3:4], rcol[:, 1:2], float(CAP), rcol[:, 0:1],
                            ALU.mult, ALU.add))
                        pd(lambda v: v.tensor_mul(
                            rcol[:, 4:5], rcol[:, 3:4], rcol[:, 2:3]))
                        pd(lambda v: v.tensor_scalar(
                            rcol[:, 5:6], rcol[:, 2:3], -1.0, 1.0, ALU.mult,
                            ALU.add))
                        pd(lambda v, i=i: v.tensor_scalar_add(
                            rcol[:, 6:7], c_tokid[:, i:i + 1], 50000.0))
                        pd(lambda v: v.tensor_mul(
                            rcol[:, 7:8], rcol[:, 5:6], rcol[:, 6:7]))
                        pd(lambda v, i=i: v.tensor_add(
                            sidx_u[:, i:i + 1], rcol[:, 4:5], rcol[:, 7:8]))
                        pd(lambda v: v.tensor_single_scalar(
                            rcol[:, 5:6], rcol[:, 4:5], 0.0, ALU.is_equal))
                        pd(lambda v: v.scalar_tensor_tensor(
                            rcol[:, 6:7], rcol[:, 5:6], 1.0e6, rcol[:, 4:5],
                            ALU.mult, ALU.add))
                        pd(lambda v, i=i: v.tensor_scalar(
                            retg_u[:, i:i + 1], rcol[:, 6:7], -1.0, None,
                            ALU.add))
                        dvs = Plan.at(DV)
                        p.add("pool", lambda g, i=i: g.indirect_dma_start(
                            out=slot_loc.ap(),
                            out_offset=IndirectOffsetOnAxis(sidx_u[:, i:i + 1], 0),
                            in_=c_tokid[:, i:i + 1], in_offset=None,
                            bounds_check=RB_SLOT, oob_is_err=False),
                            waits=tuple(list(dvs) + (list(slotz_done) if i == 0
                                                     else [])), inc=GD, by=16)
                    sc_done = Plan.at(GD)
                    if DBG:
                        for dst_t, src_ap in [
                                (dbg2_oh, oh_s[:, :]), (dbg2_ohT, oh_T[:, :]),
                                (dbg2_incl, incl_T[:, :]), (dbg2_cnt, cnt_sb[:, :]),
                                (dbg2_offs, offs_sb[:, :]),
                                (dbg2_sidx, sidx_u[:, :]),
                                (dbg2_retg, retg_u[:, :]), (dbg2_gk, gk_s[:, :])]:
                            p.add("pool", lambda g, a=dst_t, b=src_ap: g.dma_start(
                                out=a.ap(), in_=b), waits=sc_done, inc=GD, by=16)
                        sc_done = Plan.at(GD)
                    p.add("pool", lambda g: g.collective_compute(
                        "AllReduce", ALU.add,
                        replica_groups=[list(range(NCORE))],
                        ins=[slot_loc.ap().opt()], outs=[slot_all.ap().opt()]),
                        waits=sc_done, inc=CC)
                    _flush(nc, p)
                ar3_done = (CC, CC.n)

                h1s = contextlib.ExitStack()
                h_sl = h1s.enter_context(sb("h_sl", [128, HC * T], BF16))
                ffn_h_production(xn, sw1, sw3, h_sl)

            ffn_w2(h_sl, sw2, src_dram, None, True)
            h1s.close()

            with contextlib.ExitStack() as f6:
                xd = f6.enter_context(sb("xd_f", [128, DC * T], BF16, side="right"))
                with contextlib.ExitStack() as f6a:
                    sv = f6a.enter_context(sb("sv", [128, NT], F32))
                    svt = f6a.enter_context(sb("svt", [128, NT], F32))
                    svt2 = f6a.enter_context(sb("svt2", [128, NT], F32))
                    xgi = f6a.enter_context(sb("xgi", [128, NT], U32))
                    gt = [f6a.enter_context(sb(f"gt{i}", [128, D], BF16))
                          for i in range(2)]
                    p = Plan()
                    p.add("pool", lambda g: g.wait_ge(ar3_done[0].h, ar3_done[1])
                          or g.engine_nop())
                    for i in range(NT):
                        p.add("pool", lambda g, i=i: g.indirect_dma_start(
                            out=sv[:, i:i + 1], out_offset=None,
                            in_=slot_all.ap(),
                            in_offset=IndirectOffsetOnAxis(c_eslot[:, i:i + 1], 0),
                            bounds_check=RB_SLOTALL, oob_is_err=False),
                            inc=GD, by=16)
                    gdv = Plan.at(GD)
                    p.add("dve", lambda v: v.tensor_single_scalar(
                        svt[:, :], sv[:, :], 0.0, ALU.is_equal),
                        waits=gdv, inc=DV)
                    p.add("dve", lambda v: v.scalar_tensor_tensor(
                        svt2[:, :], svt[:, :], 1.0e6, sv[:, :],
                        ALU.mult, ALU.add), waits=Plan.at(DV), inc=DV)
                    p.add("dve", lambda v: v.tensor_scalar(
                        xgi[:, :], svt2[:, :], -1.0, None, ALU.add),
                        waits=Plan.at(DV), inc=DV)
                    xgi_done = Plan.at(DV)
                    p.add("pool", lambda g: g.wait_ge(ag1_done[0].h, ag1_done[1])
                          or g.engine_nop())
                    ld_marks = []
                    for i in range(NT):
                        wl = list(xgi_done)
                        if i >= 2:
                            wl += list(ld_marks[i - 2])
                        p.add("pool", lambda g, i=i: g.memset(gt[i % 2][:, :], 0.0),
                              waits=tuple(wl), inc=GD)
                        p.add("pool", lambda g, i=i: g.indirect_dma_start(
                            out=gt[i % 2][:, :], out_offset=None,
                            in_=xall.ap(),
                            in_offset=IndirectOffsetOnAxis(xgi[:, i:i + 1], 0),
                            bounds_check=RB_TOK, oob_is_err=False),
                            inc=GD, by=16)
                        gdi = Plan.at(GD)
                        p.add("sync", lambda s, i=i: s.dma_start(
                            out=xd_tm_d.ap()[i * 128:(i + 1) * 128, :],
                            in_=gt[i % 2][:, :]), waits=gdi, inc=LD, by=16)
                        ld_marks.append(Plan.at(LD))
                    all_rows = Plan.at(LD)
                    tr_marks = []
                    for c in range(DC):
                        wl = list(all_rows)
                        p.add("sync", lambda s, c=c: s.dma_start(
                            out=xd[:, c * T:(c + 1) * T],
                            in_=xd_tm_d.ap()[:, c * 128:(c + 1) * 128],
                            transpose=True), waits=tuple(wl), inc=LD, by=16)
                    p.add("sync", lambda s: s.wait_ge(LD.h, LD.n) or s.engine_nop())
                    _flush(nc, p)

                h2s = contextlib.ExitStack()
                h_sl2 = h2s.enter_context(sb("h_sl2", [128, HC * T], BF16))
                ffn_h_production(xd, w1, w3, h_sl2)

            ffn_w2(h_sl2, w2, None, ag_oe_in, False)
            h2s.close()

            with contextlib.ExitStack() as f9:
                rt = [f9.enter_context(sb(f"rt{i}", [128, D], BF16))
                      for i in range(2)]
                pcb = [f9.enter_context(sb(f"pcb{i}", [128, T], F32))
                      for i in range(2)]
                rstrip = [f9.enter_context(sb(f"rstrip{i}", [128, T], BF16))
                          for i in range(2)]
                p = Plan()
                p.add("pool", lambda g: g.wait_ge(CC.h, CC.n) or g.engine_nop())
                dv_marks = []
                st_marks_r = []
                for i in range(NT):
                    wl = []
                    if i >= 2:
                        wl += list(dv_marks[i - 2]) + list(st_marks_r[i - 2])
                    p.add("pool", lambda g, i=i: g.memset(rt[i % 2][:, :], 0.0),
                          waits=tuple(wl), inc=GD)
                    p.add("pool", lambda g, i=i: g.indirect_dma_start(
                        out=rt[i % 2][:, :], out_offset=None,
                        in_=oe_all.ap(),
                        in_offset=IndirectOffsetOnAxis(retg_u[:, i:i + 1], 0),
                        bounds_check=RB_TOK, oob_is_err=False),
                        inc=GD, by=16)
                    gdi = Plan.at(GD)
                    p.add("dve", lambda v, i=i: v.tensor_scalar(
                        rt[i % 2][:, :], rt[i % 2][:, :], gk_s[:, i:i + 1], None,
                        ALU.mult), waits=gdi, inc=DV)
                    dv_marks.append(Plan.at(DV))
                    p.add("sync", lambda s, i=i: s.dma_start(
                        out=ret_tm_d.ap()[i * 128:(i + 1) * 128, :],
                        in_=rt[i % 2][:, :]), waits=dv_marks[-1], inc=LD, by=16)
                    st_marks_r.append(Plan.at(LD))
                rows_done = Plan.at(LD)
                st_marks = []
                for c in range(DC):
                    wl = list(rows_done)
                    if c >= 2:
                        wl += list(st_marks[c - 2])
                    p.add("sync", lambda s, c=c: s.dma_start(
                        out=rstrip[c % 2][:, :],
                        in_=ret_tm_d.ap()[:, c * 128:(c + 1) * 128],
                        transpose=True), waits=tuple(wl), inc=LD, by=16)
                    p.add("sync", lambda s, c=c: s.dma_start(
                        out=pcb[c % 2][:, :],
                        in_=partial_d.ap()[c * 128:(c + 1) * 128, :]),
                        inc=LD, by=16)
                    ldv = Plan.at(LD)
                    p.add("dve", lambda v, c=c: v.tensor_add(
                        pcb[c % 2][:, :], pcb[c % 2][:, :], rstrip[c % 2][:, :]),
                        waits=ldv, inc=DV)
                    dvv = Plan.at(DV)
                    p.add("sync", lambda s, c=c: s.dma_start(
                        out=dst_dram.ap()[c * 128:(c + 1) * 128, :],
                        in_=pcb[c % 2][:, :]), waits=dvv, inc=LD, by=16)
                    st_marks.append(Plan.at(LD))
                p.add("sync", lambda s: s.wait_ge(LD.h, LD.n) or s.engine_nop())
                _flush(nc, p)

    def ffn_h_production(act_slab, wu1, wu3, h_sl):
        with contextlib.ExitStack() as hs:
            w1b = [hs.enter_context(sb(f"w1b{i}", [128, DC * 128], BF16))
                   for i in range(2)]
            w3b = [hs.enter_context(sb(f"w3b{i}", [128, DC * 128], BF16))
                   for i in range(2)]
            sil = hs.enter_context(sb("sil", [128, 1024], BF16))
            p = Plan()
            wu1_r = wu1.ap().rearrange("(k p) m -> p k m", p=128)
            wu3_r = wu3.ap().rearrange("(k p) m -> p k m", p=128)
            gd_marks = []
            ev_marks = []
            pe_marks_h = []
            for hc in range(HC):
                wl = pe_marks_h[hc - 2] if hc >= 2 else ()
                load_w(p, w1b[hc % 2][:, :].rearrange("p (k m) -> p k m", k=DC),
                       wu1_r[:, :, hc * 128:(hc + 1) * 128], waits=wl)
                load_w(p, w3b[hc % 2][:, :].rearrange("p (k m) -> p k m", k=DC),
                       wu3_r[:, :, hc * 128:(hc + 1) * 128])
                gd_marks.append(Plan.at(GD))
                for tt in range(T4):
                    g_i = hc * T4 + tt
                    psA, psB = ([PS_A, PS_B], [PS_C, PS_D])[g_i % 2]
                    silb = sil[:, (g_i % 2) * 512:(g_i % 2 + 1) * 512]
                    wl = list(gd_marks[-1])
                    if g_i >= 2:
                        wl += list(ev_marks[g_i - 2])
                    for kc in range(DC):
                        p.add("pe", lambda t, kc=kc, hc=hc, tt=tt, ps=psA: t.matmul(
                            ps[:, :],
                            w1b[hc % 2][:, kc * 128:(kc + 1) * 128],
                            act_slab[:, kc * T + tt * 512: kc * T + (tt + 1) * 512],
                            start=(kc == 0), stop=(kc == DC - 1)),
                            waits=tuple(wl) if kc == 0 else (),
                            inc=PE if kc == DC - 1 else None)
                    for kc in range(DC):
                        p.add("pe", lambda t, kc=kc, hc=hc, tt=tt, ps=psB: t.matmul(
                            ps[:, :],
                            w3b[hc % 2][:, kc * 128:(kc + 1) * 128],
                            act_slab[:, kc * T + tt * 512: kc * T + (tt + 1) * 512],
                            start=(kc == 0), stop=(kc == DC - 1)),
                            inc=PE if kc == DC - 1 else None)
                    pev = Plan.at(PE)
                    p.add("act", lambda s, ps=psA, sb_=silb: s.activation(
                        sb_, ps[:, :], AF.Silu), waits=pev, inc=AC)
                    acv = Plan.at(AC)
                    p.add("dve", lambda v, hc=hc, tt=tt, ps=psB, sb_=silb:
                          v.tensor_mul(
                              h_sl[:, hc * T + tt * 512: hc * T + (tt + 1) * 512],
                              ps[:, :], sb_), waits=acv, inc=DV)
                    ev_marks.append(Plan.at(DV))
                pe_marks_h.append(Plan.at(PE))
            _flush(nc, p)

    def ffn_w2(h_sl, wu2, resid_src, oe_ag_in, is_shared):
        h_done = Plan.at(DV)
        with contextlib.ExitStack() as w2s:
            w2b = [w2s.enter_context(sb(f"w2b{i}", [128, HC * 256], BF16))
                   for i in range(2)]
            xo = [w2s.enter_context(sb(f"xo2_{i}", [128, T],
                                       F32 if is_shared else BF16))
                  for i in range(2)]
            p = Plan()
            wu2_r = wu2.ap().rearrange("(k p) m -> p k m", p=128)
            psg = [PS_BIG[:, 0:512], PS_BIG[:, 512:1024],
                   PS_BIG[:, 1024:1536], PS_BIG[:, 1536:2048],
                   PS_A[:, :], PS_B[:, :], PS_C[:, :], PS_D[:, :]]
            gd_marks = []
            ld_marks = []
            dv_marks = []
            pe_marks_w = []
            for mp in range(8):
                wl = list(pe_marks_w[mp - 2]) if mp >= 2 else []
                load_w(p, w2b[mp % 2][:, :].rearrange("p (k m) -> p k m", k=HC),
                       wu2_r[:, :, mp * 256:(mp + 1) * 256], waits=tuple(wl))
                gd_marks.append(Plan.at(GD))
                if is_shared:
                    for j in range(2):
                        p.add("sync", lambda s, j=j, mp=mp: s.dma_start(
                            out=xo[j][:, :],
                            in_=resid_src.ap()[(mp * 2 + j) * 128:
                                               (mp * 2 + j + 1) * 128, :]),
                            waits=Plan.at(DV) if mp >= 1 else (),
                            inc=LD, by=16)
                ld_marks.append(Plan.at(LD))
                for hc in range(HC):
                    wl = list(gd_marks[-1]) + (list(h_done) if hc == 0 else [])
                    if hc == 0 and mp >= 1:
                        wl += list(Plan.at(DV))
                    for j in range(2):
                        for tt in range(T4):
                            p.add("pe", lambda t, hc=hc, j=j, tt=tt, mp=mp:
                                  t.matmul(
                                      psg[j * 4 + tt],
                                      w2b[mp % 2][:, hc * 256 + j * 128:
                                                  hc * 256 + (j + 1) * 128],
                                      h_sl[:, hc * T + tt * 512:
                                           hc * T + (tt + 1) * 512],
                                      start=(hc == 0), stop=(hc == HC - 1)),
                                  waits=tuple(wl) if (hc == 0 and j == 0
                                                      and tt == 0) else (),
                                  inc=PE if (hc == HC - 1 and j == 1
                                             and tt == T4 - 1) else None)
                pev = Plan.at(PE)
                pe_marks_w.append(pev)
                for j in range(2):
                    for tt in range(T4):
                        if is_shared:
                            p.add("dve", lambda v, j=j, tt=tt: v.tensor_add(
                                xo[j][:, tt * 512:(tt + 1) * 512],
                                psg[j * 4 + tt],
                                xo[j][:, tt * 512:(tt + 1) * 512]),
                                waits=tuple(list(pev) + list(ld_marks[-1]))
                                if tt == 0 and j == 0 else (),
                                inc=DV if (j == 1 and tt == T4 - 1) else None)
                        else:
                            wle = (tuple(list(pev) + list(ld_marks[-1]))
                                   if tt == 0 and j == 0 else ())
                            p.add("dve", lambda v, j=j, tt=tt: v.tensor_copy(
                                xo[j][:, tt * 512:(tt + 1) * 512],
                                psg[j * 4 + tt]),
                                waits=wle,
                                inc=DV if (j == 1 and tt == T4 - 1) else None)
                dv_marks.append(Plan.at(DV))
                for j in range(2):
                    p.add("sync", lambda s, j=j, mp=mp: s.dma_start(
                        out=(partial_d if is_shared else oe_fm_d).ap()
                        [(mp * 2 + j) * 128:(mp * 2 + j + 1) * 128, :],
                        in_=xo[j][:, :]), waits=dv_marks[-1], inc=LD, by=16)
                ld_marks[-1] = Plan.at(LD)
            if not is_shared:
                all_st = Plan.at(LD)
                for i in range(NT):
                    p.add("sync", lambda s, i=i: s.dma_start(
                        out=xo[i % 2][:, :],
                        in_=oe_fm_d.ap()[:, i * 128:(i + 1) * 128],
                        transpose=True), waits=all_st, inc=LD, by=16)
                    ldv = Plan.at(LD)
                    p.add("sync", lambda s, i=i: s.dma_start(
                        out=oe_ag_in.ap()[i * 128:(i + 1) * 128, :],
                        in_=xo[i % 2][:, :]), waits=ldv, inc=LD, by=16)
                agw = Plan.at(LD)
                p.add("pool", lambda g: g.collective_compute(
                    "AllGather", ALU.bypass,
                    replica_groups=[list(range(NCORE))],
                    ins=[oe_ag_in.ap().opt()], outs=[oe_all.ap().opt()]),
                    waits=agw, inc=CC)
            _flush(nc, p)


    cur = xfm_in
    dst = yfm if n_sub == 1 else xres[0]
    attn_a(cur, dst)
    cur = dst
    if n_sub >= 2:
        dst = yfm if n_sub == 2 else xres[1]
        ffn(cur, dst, n_sub == 2)
        cur = dst
    if n_sub >= 3:
        dst = yfm if n_sub == 3 else xres[0]
        attn_b(cur, dst)
        cur = dst
    if n_sub >= 4:
        ffn(cur, yfm, True)

    if DBG and n_sub >= 2:
        with contextlib.ExitStack() as ds:
            tmpf = ds.enter_context(sb("dbg_tmp", [128, T], F32))
            tmpb = ds.enter_context(sb("dbg_tmpb", [128, T], BF16))
            p = Plan()
            pairs = [(dbg_partial, partial_d, F32), (dbg_xd, xd_tm_d, BF16),
                     (dbg_oe, oe_fm_d, BF16), (dbg_ret, ret_tm_d, BF16),
                     (dbg_xn, xnfm_d, BF16)]
            for dst_t, src_t, dt_ in pairs:
                buf = tmpb[:, :] if dt_ == BF16 else tmpf[:, :]
                for c in range(DC):
                    wl = Plan.at(LD)
                    p.add("sync", lambda s, a=src_t, b=buf, c=c: s.dma_start(
                        out=b, in_=a.ap()[c * 128:(c + 1) * 128, :]),
                        waits=wl, inc=LD, by=16)
                    ldv = Plan.at(LD)
                    p.add("sync", lambda s, a=dst_t, b=buf, c=c: s.dma_start(
                        out=a.ap()[c * 128:(c + 1) * 128, :], in_=b),
                        waits=ldv, inc=LD, by=16)
            wl = Plan.at(LD)
            p.add("sync", lambda s: s.dma_start(
                out=tmpf[:, 0:129], in_=slot_all.ap().rearrange(
                    "(a b) c -> a (b c)", a=128)), waits=wl, inc=LD, by=16)
            ldv = Plan.at(LD)
            p.add("sync", lambda s: s.dma_start(
                out=dbg_slot.ap().rearrange("(a b) c -> a (b c)", a=128),
                in_=tmpf[:, 0:129]), waits=ldv, inc=LD, by=16)
            p.add("sync", lambda s: s.wait_ge(LD.h, LD.n) or s.engine_nop())
            _flush(nc, p)

    p = Plan()
    p.add("sync", lambda s: s.wait_ge(LD.h, LD.n) or s.engine_nop())
    _flush(nc, p)
    st.close()
    return nc



def _aux_inputs(core):
    rank = np.zeros((64, E), np.float32)
    for r in range(NCORE):
        if r < core:
            for e in range(E):
                rank[r * E + e, e] = 1.0
    tok = np.zeros((128, NT), np.float32)
    slot = np.zeros((128, NT), np.uint32)
    for tt in range(NT):
        for pp in range(128):
            t_loc = tt * 128 + pp
            tok[pp, tt] = core * T + t_loc + 1
            slot[pp, tt] = core * CAP + t_loc + 1
    return rank, tok, slot


def kernel(**inputs):
    def f32(a):
        return np.ascontiguousarray(np.asarray(a), dtype=np.float32)

    x = f32(inputs["L0_attn_a_x"])
    wqa = f32(inputs["L0_attn_a_Wqa"])
    wqb = f32(inputs["L0_attn_a_Wqb"])
    wkva = f32(inputs["L0_attn_a_Wkva"])
    wkvb = f32(inputs["L0_attn_a_Wkvb"])
    woa = f32(inputs["L0_attn_a_Wo"])
    wq = f32(inputs["L0_attn_b_w_q"])
    wk = f32(inputs["L0_attn_b_w_k"])
    wv = f32(inputs["L0_attn_b_w_v"])
    wob = f32(inputs["L0_attn_b_w_o"])
    wg = f32(inputs["L0_ffn_Wg"])
    w1 = f32(inputs["L0_ffn_W1"])
    w3 = f32(inputs["L0_ffn_W3"])
    w2 = f32(inputs["L0_ffn_W2"])
    sw1 = f32(inputs["L0_ffn_sw1"])
    sw3 = f32(inputs["L0_ffn_sw3"])
    sw2 = f32(inputs["L0_ffn_sw2"])

    wq_n = np.ascontiguousarray(wq.T)
    wk_n = np.ascontiguousarray(wk.T)
    wv_n = np.ascontiguousarray(wv.T)
    wob_n = np.ascontiguousarray(wob.T)

    nc = build_nc(N_SUB)
    in_maps = []
    for c in range(NCORE):
        rank, tok, slot = _aux_inputs(c)
        in_maps.append(dict(
            xfm=np.ascontiguousarray(x[c].T),
            wqa=wqa, wqb=wqb, wkva=wkva, wkvb=wkvb, woa=woa,
            wq=wq_n, wk=wk_n, wv=wv_n, wob=wob_n,
            wg=wg, w1=np.ascontiguousarray(w1[c]),
            w3=np.ascontiguousarray(w3[c]), w2=np.ascontiguousarray(w2[c]),
            sw1=sw1, sw3=sw3, sw2=sw2,
            rankmsk=rank, tokid=tok, eslot=slot,
        ))
    res = run_bass_kernel_spmd(nc, in_maps, core_ids=list(range(NCORE)))
    out = np.stack([np.asarray(r["yfm"]).T for r in res.results], axis=0)
    if DBG:
        kernel.dbg = res.results
    return out


# version 07 (final)
# speedup vs baseline: 1.3430x; 1.0417x over previous
import contextlib
import math

import numpy as np

import concourse.bass as bass
import concourse.mybir as mybir
from concourse.bass_core_rust import IndirectOffsetOnAxis
from concourse.bass_utils import run_bass_kernel_spmd

F32 = mybir.dt.float32
BF16 = mybir.dt.bfloat16
U32 = mybir.dt.uint32
AF = mybir.ActivationFunctionType
ALU = mybir.AluOpType
AX = mybir.AxisListType.X

D = 2048
T = 2048
NT = 16
T4 = 4
DC = 16
H = 16
RD = 64
ND = 128
VD = 128
QHD = ND + RD
QL = 512
KVL = 512
HF = 2816
HC = 22
E = 8
CAP = 2048
NCORE = 8
SCALE_A = 1.0 / math.sqrt(float(QHD))
SCALE_B = 1.0 / math.sqrt(float(VD))
LN_EPS = 1e-5

N_SUB = 4
DBG = False

ENGINES = ("pe", "dve", "act", "pool", "sync")
_ENG_ATTR = {"pe": "tensor", "dve": "vector", "act": "scalar", "pool": "gpsimd",
             "sync": "sync"}


class Sem:
    __slots__ = ("h", "n")

    def __init__(self, h):
        self.h = h
        self.n = 0


class Plan:

    def __init__(self):
        self.items = {e: [] for e in ENGINES}

    def add(self, eng, fn, waits=(), inc=None, by=1):
        if inc is not None:
            inc.n += by
        self.items[eng].append((tuple(waits), fn, inc, by))

    @staticmethod
    def at(*sems):
        return tuple((s, s.n) for s in sems)


def _flush(nc, plan):
    if not any(plan.items[e] for e in ENGINES):
        return
    with nc.Block() as blk:
        for eng in ENGINES:
            items = plan.items[eng]
            if not items:
                continue

            def mk(items=items):
                def body(E):
                    lastw = {}
                    for waits, fn, inc, by in items:
                        for (s, v) in waits:
                            if v > 0 and lastw.get(id(s), -1) < v:
                                E.wait_ge(s.h, v)
                                lastw[id(s)] = v
                        ins = fn(E)
                        if inc is not None:
                            ins.then_inc(inc.h, by)
                return body

            getattr(blk, _ENG_ATTR[eng])(mk())



def _rope_tables():
    inv_freq = 10000.0 ** (-np.arange(0, RD, 2, dtype=np.float64) / RD)
    pos = np.arange(T, dtype=np.float64)
    ang = pos[:, None] * inv_freq[None, :]
    cos = np.cos(ang).astype(np.float32)
    sin = np.sin(ang).astype(np.float32)
    C = np.empty((RD, T), np.float32)
    S = np.empty((RD, T), np.float32)
    C[0::2, :] = cos.T
    C[1::2, :] = cos.T
    S[0::2, :] = -sin.T
    S[1::2, :] = sin.T
    return C, S


def _mask_tiles():
    masks = np.zeros((4, 128, 512), np.float32)
    for j in range(4):
        for p in range(128):
            masks[j, p, 128 * j + p:] = 1.0
    return masks.reshape(4 * 128, 512)


def _swap64():
    P = np.zeros((RD, RD), np.float32)
    for p in range(RD):
        P[p, p ^ 1] = 1.0
    return P



def build_nc(n_sub=N_SUB):
    nc = bass.Bass("TRN2", target_bir_lowering=False)
    st = contextlib.ExitStack()

    xfm_in = nc.dram_tensor("xfm", [D, T], F32, kind="ExternalInput")
    wqa = nc.dram_tensor("wqa", [D, QL], F32, kind="ExternalInput")
    wqb = nc.dram_tensor("wqb", [QL, H * QHD], F32, kind="ExternalInput")
    wkva = nc.dram_tensor("wkva", [D, KVL + RD], F32, kind="ExternalInput")
    wkvb = nc.dram_tensor("wkvb", [KVL, H * (ND + VD)], F32, kind="ExternalInput")
    woa = nc.dram_tensor("woa", [D, D], F32, kind="ExternalInput")
    wq_b = nc.dram_tensor("wq", [D, D], F32, kind="ExternalInput")
    wk_b = nc.dram_tensor("wk", [D, D], F32, kind="ExternalInput")
    wv_b = nc.dram_tensor("wv", [D, D], F32, kind="ExternalInput")
    wob = nc.dram_tensor("wob", [D, D], F32, kind="ExternalInput")
    wg = nc.dram_tensor("wg", [D, E], F32, kind="ExternalInput")
    w1 = nc.dram_tensor("w1", [D, HF], F32, kind="ExternalInput")
    w3 = nc.dram_tensor("w3", [D, HF], F32, kind="ExternalInput")
    w2 = nc.dram_tensor("w2", [HF, D], F32, kind="ExternalInput")
    sw1 = nc.dram_tensor("sw1", [D, HF], F32, kind="ExternalInput")
    sw3 = nc.dram_tensor("sw3", [D, HF], F32, kind="ExternalInput")
    sw2 = nc.dram_tensor("sw2", [HF, D], F32, kind="ExternalInput")
    rankmsk = nc.dram_tensor("rankmsk", [64, E], F32, kind="ExternalInput")
    tokid = nc.dram_tensor("tokid", [128, NT], F32, kind="ExternalInput")
    eslot = nc.dram_tensor("eslot", [128, NT], U32, kind="ExternalInput")

    yfm = nc.dram_tensor("yfm", [D, T], F32, kind="ExternalOutput")
    if DBG:
        dbg_partial = nc.dram_tensor("dbg_partial", [D, T], F32, kind="ExternalOutput")
        dbg_slot = nc.dram_tensor("dbg_slot", [16512, 1], F32, kind="ExternalOutput")
        dbg_xd = nc.dram_tensor("dbg_xd", [CAP, D], BF16, kind="ExternalOutput")
        dbg_oe = nc.dram_tensor("dbg_oe", [D, CAP], BF16, kind="ExternalOutput")
        dbg_ret = nc.dram_tensor("dbg_ret", [T, D], BF16, kind="ExternalOutput")
        dbg_xn = nc.dram_tensor("dbg_xn", [D, T], BF16, kind="ExternalOutput")
        dbg2_oh = nc.dram_tensor("dbg2_oh", [128, E * NT], F32, kind="ExternalOutput")
        dbg2_ohT = nc.dram_tensor("dbg2_ohT", [E, T], F32, kind="ExternalOutput")
        dbg2_incl = nc.dram_tensor("dbg2_incl", [E, T], F32, kind="ExternalOutput")
        dbg2_cnt = nc.dram_tensor("dbg2_cnt", [64, 1], F32, kind="ExternalOutput")
        dbg2_offs = nc.dram_tensor("dbg2_offs", [8, 1], F32, kind="ExternalOutput")
        dbg2_sidx = nc.dram_tensor("dbg2_sidx", [128, NT], U32, kind="ExternalOutput")
        dbg2_retg = nc.dram_tensor("dbg2_retg", [128, NT], U32, kind="ExternalOutput")
        dbg2_gk = nc.dram_tensor("dbg2_gk", [128, NT], F32, kind="ExternalOutput")
        dbg2_lg = nc.dram_tensor("dbg2_lg", [128, E], F32, kind="ExternalOutput")
        dbg2_pexp = nc.dram_tensor("dbg2_pexp", [128, E], F32, kind="ExternalOutput")
        dbg2_rcol = nc.dram_tensor("dbg2_rcol", [128, 8], F32, kind="ExternalOutput")

    xres = [nc.dram_tensor(f"xres{i}", [D, T], F32, kind="Internal")
            for i in range(2)]
    xnfm_d = nc.dram_tensor("xnfm_d", [D, T], BF16, kind="Internal")
    ag_x_in = nc.dram_tensor("ag_x_in", [T, D], BF16, kind="Internal")
    xall = nc.dram_tensor("xall", [NCORE * T, D], BF16, kind="Internal",
                          addr_space="Shared")
    cnt_in = nc.dram_tensor("cnt_in", [E, 1], F32, kind="Internal")
    cnt_all = nc.dram_tensor("cnt_all", [NCORE * E, 1], F32, kind="Internal",
                             addr_space="Shared")
    slot_loc = nc.dram_tensor("slot_loc", [16512, 1], F32, kind="Internal")
    slot_all = nc.dram_tensor("slot_all", [16512, 1], F32, kind="Internal",
                              addr_space="Shared")
    xd_tm_d = nc.dram_tensor("xd_tm_d", [CAP, D], BF16, kind="Internal")
    ag_oe_in = nc.dram_tensor("ag_oe_in", [CAP, D], BF16, kind="Internal")
    oe_all = nc.dram_tensor("oe_all", [NCORE * CAP, D], BF16, kind="Internal",
                            addr_space="Shared")
    partial_d = nc.dram_tensor("partial_d", [D, T], F32, kind="Internal")
    oe_fm_d = nc.dram_tensor("oe_fm_d", [D, CAP], BF16, kind="Internal")
    ret_tm_d = nc.dram_tensor("ret_tm_d", [T, D], BF16, kind="Internal")

    ident_bf = nc.inline_tensor(np.eye(128, dtype=np.float32)
                                .astype(np.float32), name="identf")
    ropeC_h, ropeS_h = _rope_tables()
    ropeC_d = nc.inline_tensor(ropeC_h, name="ropeC")
    ropeS_d = nc.inline_tensor(ropeS_h, name="ropeS")
    masks_d = nc.inline_tensor(_mask_tiles(), name="masks")
    swap_d = nc.inline_tensor(_swap64(), name="swap64")
    ones128_d = nc.inline_tensor(np.ones((128, 1), np.float32), name="ones128")
    onesrow_d = nc.inline_tensor(np.ones((1, 128), np.float32), name="onesrow")

    LD = Sem(st.enter_context(nc.semaphore("LD")))
    GD = Sem(st.enter_context(nc.semaphore("GD")))
    PE = Sem(st.enter_context(nc.semaphore("PE")))
    DV = Sem(st.enter_context(nc.semaphore("DV")))
    AC = Sem(st.enter_context(nc.semaphore("AC")))
    CC = Sem(st.enter_context(nc.semaphore("CC")))

    _uid = [0]

    def sb(name, shape, dtype, **kw):
        _uid[0] += 1
        return nc.sbuf_tensor(f"{name}_{_uid[0]}", shape, dtype, **kw)
    c_ident_bf = st.enter_context(sb("c_ident_bf", [128, 128], BF16))
    c_ident_f = st.enter_context(sb("c_ident_f", [128, 128], F32))
    c_swap = st.enter_context(sb("c_swap", [RD, RD], BF16))
    c_ones_bf = st.enter_context(sb("c_ones_bf", [128, 1], BF16))
    c_onesrow_f = st.enter_context(sb("c_onesrow_f", [1, 128], F32))
    c_tokid = st.enter_context(sb("c_tokid", [128, NT], F32))
    c_eslot = st.enter_context(sb("c_eslot", [128, NT], U32))
    c_rankmsk = st.enter_context(sb("c_rankmsk", [64, E], F32))
    c_iota8 = st.enter_context(sb("c_iota8", [128, E], F32))
    rows = st.enter_context(sb("rows", [1, 3 * T + 512], F32))
    R_SUM, R_SQ, R_MTS = (rows[:, i * T:(i + 1) * T] for i in range(3))
    R_EXTRA = rows[:, 3 * T:3 * T + 512]

    PS_BIG = st.enter_context(nc.psum_tensor("PS_BIG", [128, 2048], F32))
    PS_A = st.enter_context(nc.psum_tensor("PS_A", [128, 512], F32))
    PS_B = st.enter_context(nc.psum_tensor("PS_B", [128, 512], F32))
    PS_C = st.enter_context(nc.psum_tensor("PS_C", [128, 512], F32))
    PS_D = st.enter_context(nc.psum_tensor("PS_D", [128, 512], F32))

    RB_SLOT = nc.alloc_register(mybir.EngineType.Pool, "rb_slot")
    RB_SLOTALL = nc.alloc_register(mybir.EngineType.Pool, "rb_sall")
    RB_TOK = nc.alloc_register(mybir.EngineType.Pool, "rb_tok")

    p = Plan()
    p.add("pool", lambda g: g.reg_mov(RB_SLOT, 16384) or g.engine_nop())
    p.add("pool", lambda g: g.reg_mov(RB_SLOTALL, 16511) or g.engine_nop())
    p.add("pool", lambda g: g.reg_mov(RB_TOK, 16383) or g.engine_nop())
    p.add("pool", lambda g: g.dma_start(out=c_ident_f[:, :], in_=ident_bf.ap()), inc=GD, by=16)
    p.add("pool", lambda g: g.dma_start(out=c_ident_bf[:, :], in_=ident_bf.ap()), inc=GD, by=16)
    p.add("pool", lambda g: g.dma_start(out=c_swap[:, :], in_=swap_d.ap()), inc=GD, by=16)
    p.add("pool", lambda g: g.dma_start(out=c_ones_bf[:, :], in_=ones128_d.ap()), inc=GD, by=16)
    p.add("pool", lambda g: g.dma_start(out=c_onesrow_f[:, :], in_=onesrow_d.ap()), inc=GD, by=16)
    p.add("pool", lambda g: g.dma_start(out=c_tokid[:, :], in_=tokid.ap()), inc=GD, by=16)
    p.add("pool", lambda g: g.dma_start(out=c_eslot[:, :], in_=eslot.ap()), inc=GD, by=16)
    p.add("pool", lambda g: g.dma_start(out=c_rankmsk[:, :], in_=rankmsk.ap()), inc=GD, by=16)
    p.add("pool", lambda g: g.iota(c_iota8[:, :], [[1, E]], base=0,
                                    channel_multiplier=0,
                                    allow_small_or_imprecise_dtypes=True))
    p.add("pool", lambda g: g.wait_ge(GD.h, GD.n) or g.engine_nop())
    _flush(nc, p)


    def ln_stats(src_dram):
        p = Plan()
        with contextlib.ExitStack() as ph:
            xc = [ph.enter_context(sb(f"lnst_x{i}", [128, T], F32)) for i in range(2)]
            xb = [ph.enter_context(sb(f"lnst_xb{i}", [128, T], BF16)) for i in range(2)]
            sq = [ph.enter_context(sb(f"lnst_sq{i}", [128, T], BF16)) for i in range(2)]
            for c in range(DC):
                b = c % 2
                if c >= 2:
                    p.add("sync", lambda s, a=xc[b], cc=c: s.dma_start(
                        out=a[:, :], in_=src_dram.ap()[cc * 128:(cc + 1) * 128, :]),
                        waits=Plan.at(PE), inc=LD, by=16)
                else:
                    p.add("sync", lambda s, a=xc[b], cc=c: s.dma_start(
                        out=a[:, :], in_=src_dram.ap()[cc * 128:(cc + 1) * 128, :]),
                        inc=LD, by=16)
                ldv = Plan.at(LD)
                wv = list(ldv) + (list(Plan.at(PE)) if c >= 2 else [])
                p.add("act", lambda s, i=b: s.copy(xb[i][:, :], xc[i][:, :]),
                      waits=tuple(wv), inc=AC)
                p.add("act", lambda s, i=b: s.activation(sq[i][:, :], xc[i][:, :], AF.Square),
                      inc=AC)
                acv = Plan.at(AC)
                for n in range(T4):
                    p.add("pe", lambda t, i=b, nn=n, cc=c: t.matmul(
                        PS_BIG[0:1, nn * 512:(nn + 1) * 512],
                        c_ones_bf[:, :], xb[i][:, nn * 512:(nn + 1) * 512],
                        start=(cc == 0), stop=(cc == DC - 1)),
                        waits=acv if n == 0 else ())
                for n in range(T4):
                    p.add("pe", lambda t, i=b, nn=n, cc=c: t.matmul(
                        PS_A[0:1, :] if nn == 0 else (PS_B[0:1, :] if nn == 1 else
                        (PS_C[0:1, :] if nn == 2 else PS_D[0:1, :])),
                        c_ones_bf[:, :], sq[i][:, nn * 512:(nn + 1) * 512],
                        start=(cc == 0), stop=(cc == DC - 1)),
                        inc=PE if (n == T4 - 1) else None)
            pev = Plan.at(PE)
            p.add("dve", lambda v: v.tensor_copy(R_SUM[:, :], PS_BIG[0:1, 0:2048]),
                  waits=pev)
            p.add("dve", lambda v: v.tensor_copy(R_SQ[:, 0:512], PS_A[0:1, :]))
            p.add("dve", lambda v: v.tensor_copy(R_SQ[:, 512:1024], PS_B[0:1, :]))
            p.add("dve", lambda v: v.tensor_copy(R_SQ[:, 1024:1536], PS_C[0:1, :]))
            p.add("dve", lambda v: v.tensor_copy(R_SQ[:, 1536:2048], PS_D[0:1, :]),
                  inc=DV)
            _flush(nc, p)

    def emit_fused_stats(p, src_ap_fn, mc, xbf, sqbf, st_marks):
        b = mc % 2
        wl = list(Plan.at(DV))
        if mc >= 2:
            wl += list(st_marks[mc - 2])
        p.add("act", lambda s, b=b: s.copy(xbf[b][:, :], src_ap_fn()),
              waits=tuple(wl), inc=AC)
        p.add("act", lambda s, b=b: s.activation(sqbf[b][:, :], src_ap_fn(),
                                                 AF.Square), inc=AC)
        acv = Plan.at(AC)
        for n in range(T4):
            p.add("pe", lambda t, b=b, nn=n, mc=mc: t.matmul(
                PS_BIG[0:1, nn * 512:(nn + 1) * 512],
                c_ones_bf[:, :], xbf[b][:, nn * 512:(nn + 1) * 512],
                start=(mc == 0), stop=(mc == DC - 1)),
                waits=acv if n == 0 else ())
        for n in range(T4):
            p.add("pe", lambda t, b=b, nn=n, mc=mc: t.matmul(
                PS_BIG[32:33, nn * 512:(nn + 1) * 512],
                c_ones_bf[:, :], sqbf[b][:, nn * 512:(nn + 1) * 512],
                start=(mc == 0), stop=(mc == DC - 1)),
                inc=PE if n == T4 - 1 else None)
        st_marks.append(Plan.at(PE))

    def ln_rows_from_psum(p):
        pev = Plan.at(PE)
        p.add("dve", lambda v: v.tensor_copy(R_SUM[:, :], PS_BIG[0:1, 0:2048]),
              waits=pev)
        p.add("dve", lambda v: v.tensor_copy(R_SQ[:, :], PS_BIG[32:33, 0:2048]),
              inc=DV)

    def ln_rows_and_bcast(bc_rstd, bc_mts, from_psum=False):
        p = Plan()
        if from_psum:
            ln_rows_from_psum(p)
        inv = 1.0 / float(D)
        p.add("dve", lambda v: v.tensor_scalar_mul(R_SUM[:, :], R_SUM[:, :], inv))
        p.add("dve", lambda v: v.tensor_scalar_mul(R_SQ[:, :], R_SQ[:, :], inv))
        p.add("dve", lambda v: v.tensor_mul(R_MTS[:, :], R_SUM[:, :], R_SUM[:, :]))
        p.add("dve", lambda v: v.tensor_sub(R_SQ[:, :], R_SQ[:, :], R_MTS[:, :]),
              inc=DV)
        p.add("dve", lambda v: v.tensor_scalar_add(R_SQ[:, :], R_SQ[:, :], LN_EPS),
              inc=DV)
        dvv = Plan.at(DV)
        p.add("act", lambda s: s.activation(R_SQ[:, :], R_SQ[:, :], AF.Ln),
              waits=dvv)
        p.add("act", lambda s: s.activation(R_SQ[:, :], R_SQ[:, :], AF.Exp,
                                            scale=-0.5), inc=AC)
        acv = Plan.at(AC)
        p.add("dve", lambda v: v.tensor_mul(R_MTS[:, :], R_SUM[:, :], R_SQ[:, :]),
              waits=acv, inc=DV)
        dv2 = Plan.at(DV)
        for n in range(T4):
            p.add("pe", lambda t, nn=n: t.matmul(
                PS_BIG[:, nn * 512:(nn + 1) * 512], c_onesrow_f[:, :],
                R_SQ[:, nn * 512:(nn + 1) * 512], start=True, stop=True),
                waits=dv2 if n == 0 else (), inc=PE if n == T4 - 1 else None)
        pev = Plan.at(PE)
        p.add("dve", lambda v: v.tensor_copy(bc_rstd[:, :], PS_BIG[:, :]),
              waits=pev, inc=DV)
        dv3 = Plan.at(DV)
        for n in range(T4):
            p.add("pe", lambda t, nn=n: t.matmul(
                PS_BIG[:, nn * 512:(nn + 1) * 512], c_onesrow_f[:, :],
                R_MTS[:, nn * 512:(nn + 1) * 512], start=True, stop=True),
                waits=dv3 if n == 0 else (), inc=PE if n == T4 - 1 else None)
        pe2 = Plan.at(PE)
        p.add("dve", lambda v: v.tensor_copy(bc_mts[:, :], PS_BIG[:, :]),
              waits=pe2, inc=DV)
        _flush(nc, p)

    def ln_normalize(src_dram, xn_slab, bc_rstd, bc_mts, write_fm_dram=None):
        p = Plan()
        with contextlib.ExitStack() as ph:
            xc = [ph.enter_context(sb(f"lnn_x{i}", [128, T], F32)) for i in range(2)]
            for c in range(DC):
                b = c % 2
                w = Plan.at(DV) if c >= 2 else ()
                p.add("sync", lambda s, i=b, cc=c: s.dma_start(
                    out=xc[i][:, :], in_=src_dram.ap()[cc * 128:(cc + 1) * 128, :]),
                    waits=w, inc=LD, by=16)
                ldv = Plan.at(LD)
                p.add("dve", lambda v, i=b: v.tensor_sub(xc[i][:, :], xc[i][:, :],
                                                         bc_mts[:, :]), waits=ldv)
                p.add("dve", lambda v, i=b, cc=c: v.tensor_mul(
                    xn_slab[:, cc * T:(cc + 1) * T], xc[i][:, :], bc_rstd[:, :]),
                    inc=DV)
                if write_fm_dram is not None:
                    dvv = Plan.at(DV)
                    p.add("sync", lambda s, cc=c: s.dma_start(
                        out=write_fm_dram.ap()[cc * 128:(cc + 1) * 128, :],
                        in_=xn_slab[:, cc * T:(cc + 1) * T]), waits=dvv, inc=LD, by=16)
            _flush(nc, p)

    def load_w(p, dst_sb_ap, src_ap, waits=()):
        p.add("pool", lambda g, d=dst_sb_ap, s=src_ap: g.dma_start(out=d, in_=s),
              waits=waits, inc=GD, by=16)


    def attn_a(src_dram, dst_dram, stats_from_psum=False, fuse_next_stats=True):
        if not stats_from_psum:
            ln_stats(src_dram)
        with contextlib.ExitStack() as ph:
            c_ropeC = ph.enter_context(sb("c_ropeC", [RD, T], BF16))
            c_ropeS = ph.enter_context(sb("c_ropeS", [RD, T], BF16))
            c_masks = ph.enter_context(sb("c_masks", [128, 4 * 512], BF16))
            bc_rstd = ph.enter_context(sb("bc_rstd_a", [128, T], F32))
            bc_mts = ph.enter_context(sb("bc_mts_a", [128, T], F32))
            pld = Plan()
            pld.add("pool", lambda g: g.dma_start(out=c_ropeC[:, :],
                                                  in_=ropeC_d.ap()), inc=GD, by=16)
            pld.add("pool", lambda g: g.dma_start(out=c_ropeS[:, :],
                                                  in_=ropeS_d.ap()), inc=GD, by=16)
            pld.add("pool", lambda g: g.dma_start(
                out=c_masks[:, :].rearrange("p (j f) -> p j f", j=4),
                in_=masks_d.ap().rearrange("(j p) f -> p j f", p=128)),
                inc=GD, by=16)
            pld.add("pool", lambda g: g.wait_ge(GD.h, GD.n) or g.engine_nop())
            _flush(nc, pld)
            ln_rows_and_bcast(bc_rstd, bc_mts, from_psum=stats_from_psum)
            cq = ph.enter_context(sb("cq", [128, 4 * T], BF16))
            comp = ph.enter_context(sb("comp", [128, 4 * T], BF16))
            kpe_rot = ph.enter_context(sb("kpe_rot", [RD, T], BF16))
            A_QSQ, A_KSQ, A_KPESQ, A_RECIP = R_SUM, R_SQ, R_MTS, R_EXTRA

            with contextlib.ExitStack() as pp:
                xn = pp.enter_context(sb("xn_a", [128, DC * T], BF16, side="right"))
                ln_normalize(src_dram, xn, bc_rstd, bc_mts)
                wqa_sb = pp.enter_context(sb("wqa_sb", [128, DC * QL], BF16,
                                             side="right"))
                wkva_sb = pp.enter_context(sb("wkva_sb", [128, DC * 576], BF16,
                                              side="right"))
                kpe_raw = pp.enter_context(sb("kpe_raw", [RD, T], BF16,
                                              side="right"))
                rtmp1 = pp.enter_context(sb("rtmp1", [RD, 512], F32, side="right"))
                rtmp2 = pp.enter_context(sb("rtmp2", [RD, 512], F32, side="right"))
                p = Plan()
                load_w(p, wqa_sb[:, :].rearrange("p (k m) -> p k m", k=DC),
                       wqa.ap().rearrange("(k p) m -> p k m", p=128))
                load_w(p, wkva_sb[:, :].rearrange("p (k m) -> p k m", k=DC),
                       wkva.ap().rearrange("(k p) m -> p k m", p=128))
                gdv = Plan.at(GD)
                pss = [PS_A, PS_B]
                g = 0
                for mc in range(4):
                    for tt in range(T4):
                        ps = pss[g % 2]
                        wl = list(gdv)
                        if g >= 2:
                            wl += list(Plan.at(DV))
                        for kc in range(DC):
                            p.add("pe", lambda t, ps=ps, kc=kc, mc=mc, tt=tt: t.matmul(
                                ps[:, :],
                                wqa_sb[:, kc * QL + mc * 128: kc * QL + (mc + 1) * 128],
                                xn[:, kc * T + tt * 512: kc * T + (tt + 1) * 512],
                                start=(kc == 0), stop=(kc == DC - 1)),
                                waits=tuple(wl) if kc == 0 else (),
                                inc=PE if kc == DC - 1 else None)
                        pev = Plan.at(PE)
                        p.add("dve", lambda v, ps=ps, mc=mc, tt=tt: v.tensor_copy(
                            cq[:, mc * T + tt * 512: mc * T + (tt + 1) * 512],
                            ps[:, :]), waits=pev, inc=DV)
                        g += 1
                for mc in range(5):
                    m0 = mc * 128
                    mw = 128 if mc < 4 else RD
                    for tt in range(T4):
                        ps = pss[g % 2]
                        wl = Plan.at(DV) if g >= 2 else ()
                        for kc in range(DC):
                            p.add("pe", lambda t, ps=ps, kc=kc, m0=m0, mw=mw, tt=tt:
                                  t.matmul(
                                      ps[0:mw, :],
                                      wkva_sb[:, kc * 576 + m0: kc * 576 + m0 + mw],
                                      xn[:, kc * T + tt * 512: kc * T + (tt + 1) * 512],
                                      start=(kc == 0), stop=(kc == DC - 1)),
                                  waits=wl if kc == 0 else (),
                                  inc=PE if kc == DC - 1 else None)
                        pev = Plan.at(PE)
                        if mc < 4:
                            p.add("dve", lambda v, ps=ps, mc=mc, tt=tt: v.tensor_copy(
                                comp[:, mc * T + tt * 512: mc * T + (tt + 1) * 512],
                                ps[:, :]), waits=pev, inc=DV)
                        else:
                            p.add("dve", lambda v, ps=ps, tt=tt: v.tensor_copy(
                                kpe_raw[:, tt * 512:(tt + 1) * 512], ps[0:RD, :]),
                                waits=pev, inc=DV)
                        g += 1
                for tt in range(T4):
                    dvv = Plan.at(DV)
                    ps = pss[g % 2]
                    p.add("pe", lambda t, ps=ps, tt=tt: t.matmul(
                        ps[0:RD, :], c_swap[:, :],
                        kpe_raw[:, tt * 512:(tt + 1) * 512], start=True, stop=True),
                        waits=dvv, inc=PE)
                    pev = Plan.at(PE)
                    p.add("dve", lambda v, ps=ps, tt=tt: v.tensor_mul(
                        rtmp1[:, :], ps[0:RD, :], c_ropeS[:, tt * 512:(tt + 1) * 512]),
                        waits=pev, inc=DV)
                    p.add("dve", lambda v, tt=tt: v.tensor_mul(
                        rtmp2[:, :], kpe_raw[:, tt * 512:(tt + 1) * 512],
                        c_ropeC[:, tt * 512:(tt + 1) * 512]))
                    p.add("dve", lambda v, tt=tt: v.tensor_add(
                        kpe_rot[:, tt * 512:(tt + 1) * 512], rtmp2[:, :],
                        rtmp1[:, :]), inc=DV)
                    g += 1
                dvv = Plan.at(DV)
                p.add("act", lambda s: s.activation(kpe_raw[:, :], kpe_rot[:, :],
                                                    AF.Square), waits=dvv, inc=AC)
                acv = Plan.at(AC)
                for tt in range(T4):
                    p.add("pe", lambda t, tt=tt: t.matmul(
                        PS_BIG[0:1, tt * 512:(tt + 1) * 512], c_ones_bf[0:RD, :],
                        kpe_raw[:, tt * 512:(tt + 1) * 512], start=True, stop=True),
                        waits=acv if tt == 0 else (), inc=PE if tt == T4 - 1 else None)
                pev = Plan.at(PE)
                p.add("dve", lambda v: v.tensor_copy(A_KPESQ[:, :], PS_BIG[0:1, :]),
                      waits=pev, inc=DV)
                _flush(nc, p)
            with contextlib.ExitStack() as hw:
                attn_sl = hw.enter_context(sb("attn_sl", [128, DC * T], BF16))
                with contextlib.ExitStack() as hh:
                    wqbh = [hh.enter_context(sb(f"wqbh{i}", [128, 4 * QHD], BF16))
                            for i in range(2)]
                    wkvbh = [hh.enter_context(sb(f"wkvbh{i}", [128, 4 * 256], BF16))
                             for i in range(2)]
                    qn = hh.enter_context(sb("qn", [128, T], BF16))
                    qpe_raw = hh.enter_context(sb("qpe_raw", [RD, T], BF16))
                    qpe = hh.enter_context(sb("qpe", [RD, T], BF16))
                    kn = hh.enter_context(sb("kn", [128, T], BF16))
                    v_tm = hh.enter_context(sb("v_tm", [128, T], BF16))
                    sqt = hh.enter_context(sb("sqt", [128, T], BF16))
                    probs = hh.enter_context(sb("probs", [128, 4 * 512], BF16))
                    rtmp1 = hh.enter_context(sb("h_rt1", [RD, 512], F32))
                    rtmp2 = hh.enter_context(sb("h_rt2", [RD, 512], F32))
                    bias_c = hh.enter_context(sb("bias_c", [128, 1], F32))
                    brow = hh.enter_context(sb("brow", [1, 4], F32))
                    rb = hh.enter_context(sb("rb", [128, 512], F32))
                    p = Plan()
                    wqb_r = wqb.ap().rearrange("(k p) m -> p k m", p=128)
                    wkvb_r = wkvb.ap().rearrange("(k p) m -> p k m", p=128)

                    def load_head(hx, waits=()):
                        load_w(p, wqbh[hx % 2][:, :].rearrange("p (k m) -> p k m", k=4),
                               wqb_r[:, :, hx * QHD:(hx + 1) * QHD], waits=waits)
                        load_w(p, wkvbh[hx % 2][:, :].rearrange("p (k m) -> p k m", k=4),
                               wkvb_r[:, :, hx * 256:(hx + 1) * 256], waits=waits)

                    gd_marks = []
                    load_head(0)
                    gd_marks.append(Plan.at(GD))
                    load_head(1)
                    gd_marks.append(Plan.at(GD))
                    for h in range(H):
                        wsel = wqbh[h % 2]
                        ksel = wkvbh[h % 2]
                        for tt in range(T4):
                            ps = [PS_A, PS_B][tt % 2]
                            wl = list(gd_marks[h]) + list(Plan.at(AC, DV))
                            for kc in range(4):
                                p.add("pe", lambda t, ps=ps, kc=kc, tt=tt, w=wsel:
                                      t.matmul(
                                          ps[:, :],
                                          w[:, kc * QHD: kc * QHD + ND],
                                          cq[:, kc * T + tt * 512:
                                             kc * T + (tt + 1) * 512],
                                          start=(kc == 0), stop=(kc == 3)),
                                      waits=tuple(wl) if kc == 0 else (),
                                      inc=PE if kc == 3 else None)
                            pev = Plan.at(PE)
                            p.add("dve", lambda v, ps=ps, tt=tt: v.tensor_copy(
                                qn[:, tt * 512:(tt + 1) * 512], ps[:, :]),
                                waits=pev, inc=DV)
                        for tt in range(T4):
                            ps = [PS_A, PS_B][tt % 2]
                            wl = Plan.at(DV)
                            for kc in range(4):
                                p.add("pe", lambda t, ps=ps, kc=kc, tt=tt, w=wsel:
                                      t.matmul(
                                          ps[0:RD, :],
                                          w[:, kc * QHD + ND: kc * QHD + QHD],
                                          cq[:, kc * T + tt * 512:
                                             kc * T + (tt + 1) * 512],
                                          start=(kc == 0), stop=(kc == 3)),
                                      waits=wl if kc == 0 else (),
                                      inc=PE if kc == 3 else None)
                            pev = Plan.at(PE)
                            p.add("dve", lambda v, ps=ps, tt=tt: v.tensor_copy(
                                qpe_raw[:, tt * 512:(tt + 1) * 512], ps[0:RD, :]),
                                waits=pev, inc=DV)
                        for tt in range(T4):
                            dvv = Plan.at(DV)
                            ps = [PS_A, PS_B][tt % 2]
                            p.add("pe", lambda t, ps=ps, tt=tt: t.matmul(
                                ps[0:RD, :], c_swap[:, :],
                                qpe_raw[:, tt * 512:(tt + 1) * 512],
                                start=True, stop=True), waits=dvv, inc=PE)
                            pev = Plan.at(PE)
                            p.add("dve", lambda v, ps=ps, tt=tt: v.tensor_mul(
                                rtmp1[:, :], ps[0:RD, :],
                                c_ropeS[:, tt * 512:(tt + 1) * 512]), waits=pev)
                            p.add("dve", lambda v, tt=tt: v.tensor_mul(
                                rtmp2[:, :], qpe_raw[:, tt * 512:(tt + 1) * 512],
                                c_ropeC[:, tt * 512:(tt + 1) * 512]))
                            p.add("dve", lambda v, tt=tt: v.tensor_add(
                                qpe[:, tt * 512:(tt + 1) * 512], rtmp2[:, :],
                                rtmp1[:, :]), inc=DV)
                        for tt in range(T4):
                            ps = [PS_A, PS_B][tt % 2]
                            wl = Plan.at(DV)
                            for kc in range(4):
                                p.add("pe", lambda t, ps=ps, kc=kc, tt=tt, w=ksel:
                                      t.matmul(
                                          ps[:, :],
                                          w[:, kc * 256: kc * 256 + ND],
                                          comp[:, kc * T + tt * 512:
                                               kc * T + (tt + 1) * 512],
                                          start=(kc == 0), stop=(kc == 3)),
                                      waits=wl if kc == 0 else (),
                                      inc=PE if kc == 3 else None)
                            pev = Plan.at(PE)
                            p.add("dve", lambda v, ps=ps, tt=tt: v.tensor_copy(
                                kn[:, tt * 512:(tt + 1) * 512], ps[:, :]),
                                waits=pev, inc=DV)
                        for tti in range(NT):
                            ps = [PS_C, PS_D][tti % 2]
                            wl = Plan.at(DV) if tti >= 2 else ()
                            for kc in range(4):
                                p.add("pe", lambda t, ps=ps, kc=kc, tti=tti, w=ksel:
                                      t.matmul(
                                          ps[:, 0:128],
                                          comp[:, kc * T + tti * 128:
                                               kc * T + (tti + 1) * 128],
                                          w[:, kc * 256 + ND: kc * 256 + 256],
                                          start=(kc == 0), stop=(kc == 3)),
                                      waits=wl if kc == 0 else (),
                                      inc=PE if kc == 3 else None)
                            pev = Plan.at(PE)
                            p.add("dve", lambda v, ps=ps, tti=tti: v.tensor_copy(
                                v_tm[:, tti * 128:(tti + 1) * 128], ps[:, 0:128]),
                                waits=pev, inc=DV)
                        if h + 2 < H:
                            load_head(h + 2, waits=Plan.at(PE))
                            gd_marks.append(Plan.at(GD))
                        dvv = Plan.at(DV)
                        p.add("act", lambda s: s.activation(sqt[:, :], qn[:, :],
                                                            AF.Square),
                              waits=dvv, inc=AC)
                        acv = Plan.at(AC)
                        for tt in range(T4):
                            p.add("pe", lambda t, tt=tt: t.matmul(
                                PS_BIG[0:1, tt * 512:(tt + 1) * 512], c_ones_bf[:, :],
                                sqt[:, tt * 512:(tt + 1) * 512],
                                start=True, stop=False),
                                waits=acv if tt == 0 else (),
                                inc=PE if tt == T4 - 1 else None)
                        p.add("act", lambda s: s.activation(sqt[0:RD, 0:T], qpe[:, :],
                                                            AF.Square),
                              waits=Plan.at(PE), inc=AC)
                        acv = Plan.at(AC)
                        for tt in range(T4):
                            p.add("pe", lambda t, tt=tt: t.matmul(
                                PS_BIG[0:1, tt * 512:(tt + 1) * 512],
                                c_ones_bf[0:RD, :],
                                sqt[0:RD, tt * 512:(tt + 1) * 512],
                                start=False, stop=True),
                                waits=acv if tt == 0 else (),
                                inc=PE if tt == T4 - 1 else None)
                        pev = Plan.at(PE)
                        p.add("dve", lambda v: v.tensor_copy(A_QSQ[:, :],
                                                             PS_BIG[0:1, :]),
                              waits=pev, inc=DV)
                        dv2 = Plan.at(DV)
                        p.add("act", lambda s: s.activation(sqt[:, :], kn[:, :],
                                                            AF.Square),
                              waits=tuple(list(dv2) + list(Plan.at(PE))), inc=AC)
                        acv = Plan.at(AC)
                        for tt in range(T4):
                            p.add("pe", lambda t, tt=tt: t.matmul(
                                PS_BIG[0:1, tt * 512:(tt + 1) * 512], c_ones_bf[:, :],
                                sqt[:, tt * 512:(tt + 1) * 512],
                                start=True, stop=True),
                                waits=acv if tt == 0 else (),
                                inc=PE if tt == T4 - 1 else None)
                        pev = Plan.at(PE)
                        p.add("dve", lambda v: v.tensor_copy(A_KSQ[:, :],
                                                             PS_BIG[0:1, :]),
                              waits=pev)
                        p.add("dve", lambda v: v.tensor_add(A_KSQ[:, :], A_KSQ[:, :],
                                                            A_KPESQ[:, :]))
                        p.add("dve", lambda v: v.reduce_max(brow[:, 0:1], A_QSQ[:, :],
                                                            axis=AX), inc=DV)
                        p.add("dve", lambda v: v.reduce_max(brow[:, 1:2], A_KSQ[:, :],
                                                            axis=AX),
                              waits=Plan.at(DV), inc=DV)
                        p.add("dve", lambda v: v.tensor_tensor(
                            brow[:, 2:3], brow[:, 0:1], brow[:, 1:2], ALU.add),
                              waits=Plan.at(DV), inc=DV)
                        p.add("dve", lambda v: v.tensor_scalar_mul(
                            brow[:, 3:4], brow[:, 2:3], -0.5 * SCALE_A),
                              waits=Plan.at(DV), inc=DV)
                        dv3 = Plan.at(DV)
                        p.add("pe", lambda t: t.matmul(
                            PS_C[:, 0:1], c_onesrow_f[:, :], brow[:, 3:4],
                            start=True, stop=True), waits=dv3, inc=PE)
                        pev = Plan.at(PE)
                        p.add("dve", lambda v: v.tensor_copy(bias_c[:, :],
                                                             PS_C[:, 0:1]),
                              waits=pev, inc=DV)
                        bias_ready = Plan.at(DV)
                        norm_done = [Plan.at(DV)]
                        exp_marks = {}
                        pv_marks = {}
                        for qt in range(T4):
                            nkt = 4 * qt + 4
                            LAG = 2

                            def emit_sumpv(kt, qt=qt, nkt=nkt):
                                o = (kt % 4) * 512
                                wl0 = list(exp_marks[(qt, kt)])
                                if kt == 0:
                                    wl0 += list(norm_done[0])
                                p.add("pe", lambda t, o=o, kt=kt, nkt=nkt:
                                      t.matmul(
                                          PS_C[0:1, :], c_ones_bf[:, :],
                                          probs[:, o:o + 512],
                                          start=(kt == 0), stop=(kt == nkt - 1)),
                                      waits=tuple(wl0))
                                p.add("pe", lambda t, o=o, kt=kt, nkt=nkt:
                                      t.matmul(
                                          PS_D[:, :],
                                          v_tm[:, kt * 128:(kt + 1) * 128],
                                          probs[:, o:o + 512],
                                          start=(kt == 0), stop=(kt == nkt - 1)),
                                      inc=PE)
                                pv_marks[(qt, kt)] = Plan.at(PE)

                            for kt in range(nkt):
                                sb_off = (kt % 4) * 512
                                wl = []
                                if kt >= 4:
                                    wl += list(exp_marks[(qt, kt - 4)])
                                p.add("pe", lambda t, kt=kt, qt=qt, o=sb_off:
                                      t.matmul(
                                          PS_BIG[:, o:o + 512],
                                          kn[:, kt * 128:(kt + 1) * 128],
                                          qn[:, qt * 512:(qt + 1) * 512],
                                          start=True, stop=False),
                                      waits=tuple(wl) if wl else ())
                                p.add("pe", lambda t, kt=kt, qt=qt, o=sb_off:
                                      t.matmul(
                                          PS_BIG[:, o:o + 512],
                                          kpe_rot[:, kt * 128:(kt + 1) * 128],
                                          qpe[:, qt * 512:(qt + 1) * 512],
                                          start=False, stop=True), inc=PE)
                                pev = Plan.at(PE)
                                wle = list(pev) + list(bias_ready)
                                if kt >= 4 and (qt, kt - 4) in pv_marks:
                                    wle += list(pv_marks[(qt, kt - 4)])
                                p.add("act", lambda s, o=sb_off: s.activation(
                                    probs[:, o:o + 512], PS_BIG[:, o:o + 512],
                                    AF.Exp, bias=bias_c[:, :], scale=SCALE_A),
                                    waits=tuple(wle), inc=AC)
                                if kt >= 4 * qt:
                                    acv = Plan.at(AC)
                                    j = kt - 4 * qt
                                    p.add("dve", lambda v, o=sb_off, j=j: v.tensor_mul(
                                        probs[:, o:o + 512], probs[:, o:o + 512],
                                        c_masks[:, j * 512:(j + 1) * 512]),
                                        waits=acv, inc=DV)
                                    exp_marks[(qt, kt)] = Plan.at(DV)
                                else:
                                    exp_marks[(qt, kt)] = Plan.at(AC)
                                if kt >= LAG:
                                    emit_sumpv(kt - LAG)
                            for kt in range(max(0, nkt - LAG), nkt):
                                emit_sumpv(kt)
                            pev = Plan.at(PE)
                            p.add("dve", lambda v: v.reciprocal(A_RECIP[:, :],
                                                                PS_C[0:1, :]),
                                  waits=pev, inc=DV)
                            dvv = Plan.at(DV)
                            p.add("pe", lambda t: t.matmul(
                                PS_C[:, :], c_onesrow_f[:, :], A_RECIP[:, :],
                                start=True, stop=True), waits=dvv, inc=PE)
                            pev = Plan.at(PE)
                            p.add("dve", lambda v: v.tensor_copy(rb[:, :], PS_C[:, :]),
                                  waits=pev)
                            p.add("dve", lambda v, h=h, qt=qt: v.tensor_mul(
                                attn_sl[:, h * T + qt * 512: h * T + (qt + 1) * 512],
                                PS_D[:, :], rb[:, :]), inc=DV)
                            norm_done[0] = Plan.at(DV)
                    _flush(nc, p)
                with contextlib.ExitStack() as wo_s:
                    wo_sb = [wo_s.enter_context(sb(f"wo_sb{i}", [128, DC * 128], BF16))
                             for i in range(2)]
                    xo = [wo_s.enter_context(sb(f"xo{i}", [128, T], F32))
                          for i in range(2)]
                    xbf_s = [wo_s.enter_context(sb(f"wxbf{i}", [128, T], BF16))
                             for i in range(2)]
                    sqbf_s = [wo_s.enter_context(sb(f"wsq{i}", [128, T], BF16))
                              for i in range(2)]
                    p = Plan()
                    woa_r = woa.ap().rearrange("(k p) m -> p k m", p=128)
                    gd_marks = []
                    pe_marks_o = []
                    fstat_marks = []
                    for mc in range(DC):
                        load_w(p, wo_sb[mc % 2][:, :].rearrange("p (k m) -> p k m",
                                                                k=DC),
                               woa_r[:, :, mc * 128:(mc + 1) * 128],
                               waits=pe_marks_o[mc - 2] if mc >= 2 else ())
                        gd_marks.append(Plan.at(GD))
                        wxo = list(Plan.at(DV)) if mc >= 2 else []
                        if fuse_next_stats and mc >= 2:
                            wxo += list(fstat_marks[mc - 2])
                        p.add("sync", lambda s, i=mc % 2, mc=mc: s.dma_start(
                            out=xo[i][:, :],
                            in_=src_dram.ap()[mc * 128:(mc + 1) * 128, :]),
                            waits=tuple(wxo), inc=LD, by=16)
                        ldv = Plan.at(LD)
                        for tt in range(T4):
                            ps = [PS_A, PS_B][tt % 2]
                            wl = list(gd_marks[-1]) + list(ldv)
                            if mc * T4 + tt >= 2:
                                wl += list(Plan.at(DV))
                            for kc in range(DC):
                                p.add("pe", lambda t, ps=ps, kc=kc, mc=mc, tt=tt:
                                      t.matmul(
                                          ps[:, :],
                                          wo_sb[mc % 2][:, kc * 128:(kc + 1) * 128],
                                          attn_sl[:, kc * T + tt * 512:
                                                  kc * T + (tt + 1) * 512],
                                          start=(kc == 0), stop=(kc == DC - 1)),
                                      waits=tuple(wl) if kc == 0 else (),
                                      inc=PE if kc == DC - 1 else None)
                            pev = Plan.at(PE)
                            p.add("dve", lambda v, ps=ps, i=mc % 2, tt=tt:
                                  v.tensor_add(
                                      xo[i][:, tt * 512:(tt + 1) * 512], ps[:, :],
                                      xo[i][:, tt * 512:(tt + 1) * 512]),
                                  waits=pev, inc=DV)
                        if fuse_next_stats:
                            emit_fused_stats(p, (lambda i=mc % 2: xo[i][:, :]),
                                             mc, xbf_s, sqbf_s, fstat_marks)
                        pe_marks_o.append(Plan.at(PE))
                        dvv = Plan.at(DV)
                        p.add("sync", lambda s, i=mc % 2, mc=mc: s.dma_start(
                            out=dst_dram.ap()[mc * 128:(mc + 1) * 128, :],
                            in_=xo[i][:, :]), waits=dvv, inc=LD, by=16)
                    _flush(nc, p)


    def attn_b(src_dram, dst_dram, stats_from_psum=False, fuse_next_stats=True):
        if not stats_from_psum:
            ln_stats(src_dram)
        with contextlib.ExitStack() as ph:
            bc_rstd = ph.enter_context(sb("bc_rstd_b", [128, T], F32))
            bc_mts = ph.enter_context(sb("bc_mts_b", [128, T], F32))
            ln_rows_and_bcast(bc_rstd, bc_mts, from_psum=stats_from_psum)
            A_QSQ, A_KSQ, A_RECIP = R_SUM, R_SQ, R_EXTRA
            with contextlib.ExitStack() as bx:
                xn = bx.enter_context(sb("xn_b", [128, DC * T], BF16, side="right"))
                ln_normalize(src_dram, xn, bc_rstd, bc_mts)
                with contextlib.ExitStack() as hh:
                    wqb_ = [hh.enter_context(sb(f"bwq{i}", [128, DC * 256], BF16))
                            for i in range(2)]
                    wkb_ = [hh.enter_context(sb(f"bwk{i}", [128, DC * 256], BF16))
                            for i in range(2)]
                    wvb_ = [hh.enter_context(sb(f"bwv{i}", [128, DC * 256], BF16))
                            for i in range(2)]
                    q2 = hh.enter_context(sb("q2", [128, 2 * T], BF16))
                    k2 = hh.enter_context(sb("k2", [128, 2 * T], BF16))
                    v2 = hh.enter_context(sb("v2", [128, NT * 256], BF16))
                    sqt = hh.enter_context(sb("sqt_b", [128, T], BF16))
                    probs = hh.enter_context(sb("probs_b", [128, 4 * 512], BF16))
                    bias_c = hh.enter_context(sb("bias_cb", [128, 1], F32))
                    brow = hh.enter_context(sb("brow_b", [1, 4], F32))
                    rb = hh.enter_context(sb("rb_b", [128, 512], F32))
                    abuf = [hh.enter_context(sb(f"abuf{i}", [128, T], BF16))
                            for i in range(2)]
                    p = Plan()
                    wq_r = wq_b.ap().rearrange("(k p) m -> p k m", p=128)
                    wk_r = wk_b.ap().rearrange("(k p) m -> p k m", p=128)
                    wv_r = wv_b.ap().rearrange("(k p) m -> p k m", p=128)

                    def load_grp(gx, waits=()):
                        load_w(p, wqb_[gx % 2][:, :].rearrange(
                            "p (k m) -> p k m", k=DC),
                            wq_r[:, :, gx * 256:(gx + 1) * 256], waits=waits)
                        load_w(p, wkb_[gx % 2][:, :].rearrange(
                            "p (k m) -> p k m", k=DC),
                            wk_r[:, :, gx * 256:(gx + 1) * 256])
                        load_w(p, wvb_[gx % 2][:, :].rearrange(
                            "p (k m) -> p k m", k=DC),
                            wv_r[:, :, gx * 256:(gx + 1) * 256])

                    gd_marks = []
                    load_grp(0)
                    gd_marks.append(Plan.at(GD))
                    load_grp(1)
                    gd_marks.append(Plan.at(GD))
                    ab_store_marks = {}
                    for g in range(8):
                        wsel, ksel, vsel = wqb_[g % 2], wkb_[g % 2], wvb_[g % 2]
                        for (dst_sl, wmat) in ((q2, wsel), (k2, ksel)):
                            for hh_i in range(2):
                                for tt in range(T4):
                                    ps = [PS_A, PS_B][tt % 2]
                                    wl = list(gd_marks[g]) + list(Plan.at(DV))
                                    for kc in range(DC):
                                        p.add("pe", lambda t, ps=ps, kc=kc,
                                              hh_i=hh_i, tt=tt, w=wmat: t.matmul(
                                                  ps[:, :],
                                                  w[:, kc * 256 + hh_i * 128:
                                                    kc * 256 + (hh_i + 1) * 128],
                                                  xn[:, kc * T + tt * 512:
                                                     kc * T + (tt + 1) * 512],
                                                  start=(kc == 0),
                                                  stop=(kc == DC - 1)),
                                              waits=tuple(wl) if kc == 0 else (),
                                              inc=PE if kc == DC - 1 else None)
                                    pev = Plan.at(PE)
                                    p.add("dve", lambda v, ps=ps, d=dst_sl,
                                          hh_i=hh_i, tt=tt: v.tensor_copy(
                                              d[:, hh_i * T + tt * 512:
                                                hh_i * T + (tt + 1) * 512],
                                              ps[:, :]), waits=pev, inc=DV)
                        for tti in range(NT):
                            ps = [PS_C, PS_D][tti % 2]
                            wl = Plan.at(DV)
                            for kc in range(DC):
                                p.add("pe", lambda t, ps=ps, kc=kc, tti=tti,
                                      w=vsel: t.matmul(
                                          ps[:, 0:256],
                                          xn[:, kc * T + tti * 128:
                                             kc * T + (tti + 1) * 128],
                                          w[:, kc * 256:(kc + 1) * 256],
                                          start=(kc == 0), stop=(kc == DC - 1)),
                                      waits=wl if kc == 0 else (),
                                      inc=PE if kc == DC - 1 else None)
                            pev = Plan.at(PE)
                            p.add("dve", lambda v, ps=ps, tti=tti: v.tensor_copy(
                                v2[:, tti * 256:(tti + 1) * 256], ps[:, 0:256]),
                                waits=pev, inc=DV)
                        if g + 2 < 8:
                            load_grp(g + 2, waits=Plan.at(PE))
                            gd_marks.append(Plan.at(GD))
                        for hh_i in range(2):
                            h = 2 * g + hh_i
                            dvv = Plan.at(DV)
                            p.add("act", lambda s, hh_i=hh_i: s.activation(
                                sqt[:, :], q2[:, hh_i * T:(hh_i + 1) * T],
                                AF.Square), waits=dvv, inc=AC)
                            acv = Plan.at(AC)
                            for tt in range(T4):
                                p.add("pe", lambda t, tt=tt: t.matmul(
                                    PS_BIG[0:1, tt * 512:(tt + 1) * 512],
                                    c_ones_bf[:, :],
                                    sqt[:, tt * 512:(tt + 1) * 512],
                                    start=True, stop=True),
                                    waits=acv if tt == 0 else (),
                                    inc=PE if tt == T4 - 1 else None)
                            pev = Plan.at(PE)
                            p.add("dve", lambda v: v.tensor_copy(
                                A_QSQ[:, :], PS_BIG[0:1, :]), waits=pev, inc=DV)
                            p.add("act", lambda s, hh_i=hh_i: s.activation(
                                sqt[:, :], k2[:, hh_i * T:(hh_i + 1) * T],
                                AF.Square),
                                waits=tuple(list(Plan.at(DV)) + list(Plan.at(PE))),
                                inc=AC)
                            acv = Plan.at(AC)
                            for tt in range(T4):
                                p.add("pe", lambda t, tt=tt: t.matmul(
                                    PS_BIG[0:1, tt * 512:(tt + 1) * 512],
                                    c_ones_bf[:, :],
                                    sqt[:, tt * 512:(tt + 1) * 512],
                                    start=True, stop=True),
                                    waits=acv if tt == 0 else (),
                                    inc=PE if tt == T4 - 1 else None)
                            pev = Plan.at(PE)
                            p.add("dve", lambda v: v.tensor_copy(
                                A_KSQ[:, :], PS_BIG[0:1, :]), waits=pev, inc=DV)
                            p.add("dve", lambda v: v.reduce_max(
                                brow[:, 0:1], A_QSQ[:, :], axis=AX),
                                waits=Plan.at(DV), inc=DV)
                            p.add("dve", lambda v: v.reduce_max(
                                brow[:, 1:2], A_KSQ[:, :], axis=AX),
                                waits=Plan.at(DV), inc=DV)
                            p.add("dve", lambda v: v.tensor_tensor(
                                brow[:, 2:3], brow[:, 0:1], brow[:, 1:2],
                                ALU.add), waits=Plan.at(DV), inc=DV)
                            p.add("dve", lambda v: v.tensor_scalar_mul(
                                brow[:, 3:4], brow[:, 2:3], -0.5 * SCALE_B),
                                waits=Plan.at(DV), inc=DV)
                            dv3 = Plan.at(DV)
                            p.add("pe", lambda t: t.matmul(
                                PS_C[:, 0:1], c_onesrow_f[:, :], brow[:, 3:4],
                                start=True, stop=True), waits=dv3, inc=PE)
                            pev = Plan.at(PE)
                            p.add("dve", lambda v: v.tensor_copy(
                                bias_c[:, :], PS_C[:, 0:1]), waits=pev, inc=DV)
                            bias_ready = Plan.at(DV)
                            norm_done = [Plan.at(DV)]
                            exp_marks = {}
                            pv_marks = {}
                            for qt in range(T4):
                                LAG = 2

                                def emit_sumpv(kt, qt=qt, hh_i=hh_i):
                                    o = (kt % 4) * 512
                                    wl0 = list(exp_marks[(qt, kt)])
                                    if kt == 0:
                                        wl0 += list(norm_done[0])
                                    p.add("pe", lambda t, o=o, kt=kt: t.matmul(
                                        PS_C[0:1, :], c_ones_bf[:, :],
                                        probs[:, o:o + 512],
                                        start=(kt == 0), stop=(kt == NT - 1)),
                                        waits=tuple(wl0))
                                    p.add("pe", lambda t, o=o, kt=kt, hh_i=hh_i:
                                          t.matmul(
                                              PS_D[:, :],
                                              v2[:, kt * 256 + hh_i * 128:
                                                 kt * 256 + (hh_i + 1) * 128],
                                              probs[:, o:o + 512],
                                              start=(kt == 0),
                                              stop=(kt == NT - 1)),
                                          inc=PE)
                                    pv_marks[(qt, kt)] = Plan.at(PE)

                                for kt in range(NT):
                                    o = (kt % 4) * 512
                                    wl = []
                                    if kt >= 4:
                                        wl += list(exp_marks[(qt, kt - 4)])
                                    p.add("pe", lambda t, hh_i=hh_i, kt=kt,
                                          qt=qt, o=o: t.matmul(
                                              PS_BIG[:, o:o + 512],
                                              k2[:, hh_i * T + kt * 128:
                                                 hh_i * T + (kt + 1) * 128],
                                              q2[:, hh_i * T + qt * 512:
                                                 hh_i * T + (qt + 1) * 512],
                                              start=True, stop=True),
                                          waits=tuple(wl) if wl else (), inc=PE)
                                    pev = Plan.at(PE)
                                    wle = list(pev) + list(bias_ready)
                                    if kt >= 4 and (qt, kt - 4) in pv_marks:
                                        wle += list(pv_marks[(qt, kt - 4)])
                                    p.add("act", lambda s, o=o: s.activation(
                                        probs[:, o:o + 512],
                                        PS_BIG[:, o:o + 512], AF.Exp,
                                        bias=bias_c[:, :], scale=SCALE_B),
                                        waits=tuple(wle), inc=AC)
                                    exp_marks[(qt, kt)] = Plan.at(AC)
                                    if kt >= LAG:
                                        emit_sumpv(kt - LAG)
                                for kt in range(NT - LAG, NT):
                                    emit_sumpv(kt)
                                pev = Plan.at(PE)
                                p.add("dve", lambda v: v.reciprocal(
                                    A_RECIP[:, :], PS_C[0:1, :]),
                                    waits=pev, inc=DV)
                                dvv = Plan.at(DV)
                                p.add("pe", lambda t: t.matmul(
                                    PS_C[:, :], c_onesrow_f[:, :], A_RECIP[:, :],
                                    start=True, stop=True), waits=dvv, inc=PE)
                                pev = Plan.at(PE)
                                p.add("dve", lambda v: v.tensor_copy(
                                    rb[:, :], PS_C[:, :]), waits=pev)
                                wl = [*Plan.at(PE)]
                                if h >= 2:
                                    wl += list(ab_store_marks[h - 2])
                                p.add("dve", lambda v, h=h, qt=qt: v.tensor_mul(
                                    abuf[h % 2][:, qt * 512:(qt + 1) * 512],
                                    PS_D[:, :], rb[:, :]),
                                    waits=tuple(wl), inc=DV)
                                norm_done[0] = Plan.at(DV)
                            dvv = Plan.at(DV)
                            p.add("sync", lambda s, h=h: s.dma_start(
                                out=xnfm_d.ap()[h * 128:(h + 1) * 128, :],
                                in_=abuf[h % 2][:, :]), waits=dvv, inc=LD, by=16)
                            ab_store_marks[h] = Plan.at(LD)
                    _flush(nc, p)
            with contextlib.ExitStack() as wo_s:
                attn_sl = wo_s.enter_context(sb("attn_slb", [128, DC * T], BF16,
                                                side="right"))
                wo_sb = [wo_s.enter_context(sb(f"bwo{i}", [128, DC * 128], BF16))
                         for i in range(2)]
                xo = [wo_s.enter_context(sb(f"bxo{i}", [128, T], F32))
                      for i in range(2)]
                p = Plan()
                for c in range(DC):
                    p.add("sync", lambda s, c=c: s.dma_start(
                        out=attn_sl[:, c * T:(c + 1) * T],
                        in_=xnfm_d.ap()[c * 128:(c + 1) * 128, :]),
                        inc=LD, by=16)
                aload = Plan.at(LD)
                xbf_s = [wo_s.enter_context(sb(f"bxbf{i}", [128, T], BF16))
                         for i in range(2)]
                sqbf_s = [wo_s.enter_context(sb(f"bsq{i}", [128, T], BF16))
                          for i in range(2)]
                wob_r = wob.ap().rearrange("(k p) m -> p k m", p=128)
                gd_marks = []
                pe_marks_o = []
                fstat_marks = []
                for mc in range(DC):
                    load_w(p, wo_sb[mc % 2][:, :].rearrange(
                        "p (k m) -> p k m", k=DC),
                        wob_r[:, :, mc * 128:(mc + 1) * 128],
                        waits=pe_marks_o[mc - 2] if mc >= 2 else ())
                    gd_marks.append(Plan.at(GD))
                    wxo = list(Plan.at(DV)) if mc >= 2 else []
                    if fuse_next_stats and mc >= 2:
                        wxo += list(fstat_marks[mc - 2])
                    p.add("sync", lambda s, i=mc % 2, mc=mc: s.dma_start(
                        out=xo[i][:, :],
                        in_=src_dram.ap()[mc * 128:(mc + 1) * 128, :]),
                        waits=tuple(wxo), inc=LD, by=16)
                    ldv = Plan.at(LD)
                    for tt in range(T4):
                        ps = [PS_A, PS_B][tt % 2]
                        wl = list(gd_marks[-1]) + list(ldv) + list(aload)
                        if mc * T4 + tt >= 2:
                            wl += list(Plan.at(DV))
                        for kc in range(DC):
                            p.add("pe", lambda t, ps=ps, kc=kc, mc=mc, tt=tt:
                                  t.matmul(
                                      ps[:, :],
                                      wo_sb[mc % 2][:, kc * 128:(kc + 1) * 128],
                                      attn_sl[:, kc * T + tt * 512:
                                              kc * T + (tt + 1) * 512],
                                      start=(kc == 0), stop=(kc == DC - 1)),
                                  waits=tuple(wl) if kc == 0 else (),
                                  inc=PE if kc == DC - 1 else None)
                        pev = Plan.at(PE)
                        p.add("dve", lambda v, ps=ps, i=mc % 2, tt=tt:
                              v.tensor_add(
                                  xo[i][:, tt * 512:(tt + 1) * 512], ps[:, :],
                                  xo[i][:, tt * 512:(tt + 1) * 512]),
                              waits=pev, inc=DV)
                    if fuse_next_stats:
                        emit_fused_stats(p, (lambda i=mc % 2: xo[i][:, :]),
                                         mc, xbf_s, sqbf_s, fstat_marks)
                    pe_marks_o.append(Plan.at(PE))
                    dvv = Plan.at(DV)
                    p.add("sync", lambda s, i=mc % 2, mc=mc: s.dma_start(
                        out=dst_dram.ap()[mc * 128:(mc + 1) * 128, :],
                        in_=xo[i][:, :]), waits=dvv, inc=LD, by=16)
                p.add("sync", lambda s: s.wait_ge(LD.h, LD.n) or s.engine_nop())
                _flush(nc, p)


    def ffn(src_dram, dst_dram, is_last, stats_from_psum=False):
        if not stats_from_psum:
            ln_stats(src_dram)
        with contextlib.ExitStack() as f0:
            gk_s = f0.enter_context(sb("gk_s", [128, NT], F32))
            gate_s = f0.enter_context(sb("gate_s", [128, NT], F32))
            retg_u = f0.enter_context(sb("retg_u", [128, NT], U32))
            sidx_u = f0.enter_context(sb("sidx_u", [128, NT], U32))
            oh_s = f0.enter_context(sb("oh_s", [128, E * NT], F32))
            offs_sb = f0.enter_context(sb("offs_sb", [8, 1], F32))
            cnt_sb = f0.enter_context(sb("cnt_sb", [64, 1], F32))
            rcol = f0.enter_context(sb("rcol", [128, 8], F32))

            with contextlib.ExitStack() as fx:
                xn = fx.enter_context(sb("xn_f", [128, DC * T], BF16, side="right"))
                with contextlib.ExitStack() as fb:
                    bc_rstd = fb.enter_context(sb("bc_rstd_f", [128, T], F32))
                    bc_mts = fb.enter_context(sb("bc_mts_f", [128, T], F32))
                    ln_rows_and_bcast(bc_rstd, bc_mts, from_psum=stats_from_psum)
                    ln_normalize(src_dram, xn, bc_rstd, bc_mts,
                                 write_fm_dram=xnfm_d)

                with contextlib.ExitStack() as f2:
                    strip = [f2.enter_context(sb(f"strip{i}", [128, D], BF16))
                             for i in range(2)]
                    p = Plan()
                    xn_written = Plan.at(LD)
                    st_marks = []
                    for i in range(NT):
                        wl = list(xn_written)
                        if i >= 2:
                            wl += list(st_marks[i - 2])
                        p.add("sync", lambda s, i=i: s.dma_start(
                            out=strip[i % 2][:, :],
                            in_=xnfm_d.ap()[:, i * 128:(i + 1) * 128],
                            transpose=True), waits=tuple(wl), inc=LD, by=16)
                        ldv = Plan.at(LD)
                        p.add("sync", lambda s, i=i: s.dma_start(
                            out=ag_x_in.ap()[i * 128:(i + 1) * 128, :],
                            in_=strip[i % 2][:, :]), waits=ldv, inc=LD, by=16)
                        st_marks.append(Plan.at(LD))
                    agin_done = Plan.at(LD)
                    p.add("pool", lambda g: g.collective_compute(
                        "AllGather", ALU.bypass,
                        replica_groups=[list(range(NCORE))],
                        ins=[ag_x_in.ap().opt()], outs=[xall.ap().opt()]),
                        waits=agin_done, inc=CC)
                    _flush(nc, p)
                ag1_done = (CC, CC.n)

                with contextlib.ExitStack() as f3:
                    wg_sb = f3.enter_context(sb("wg_sb", [128, DC * E], BF16))
                    oh_T = f3.enter_context(sb("oh_T", [E, T], F32))
                    incl_T = f3.enter_context(sb("incl_T", [E, T], F32))
                    lg = f3.enter_context(sb("lg", [128, E], F32))
                    pexp = f3.enter_context(sb("pexp", [128, E], F32))
                    t8 = f3.enter_context(sb("t8", [128, E], F32))
                    inc_tm = f3.enter_context(sb("inc_tm", [128, E], F32))
                    zz = f3.enter_context(sb("zz", [128, 129], F32))
                    p = Plan()
                    load_w(p, wg_sb[:, :].rearrange("p (k m) -> p k m", k=DC),
                           wg.ap().rearrange("(k p) m -> p k m", p=128))
                    gdv = Plan.at(GD)
                    p.add("pool", lambda g: g.memset(zz[:, :], 0.0), inc=GD)
                    zz_done = Plan.at(GD)
                    p.add("pool", lambda g: g.dma_start(
                        out=slot_loc.ap().rearrange("(a b) c -> a (b c)", a=128),
                        in_=zz[:, :]), waits=zz_done, inc=GD, by=16)
                    slotz_done = Plan.at(GD)
                    for i in range(NT):
                        ps = [PS_A, PS_B][i % 2]
                        wl = list(gdv)
                        if i >= 2:
                            wl += list(Plan.at(DV))
                        for kc in range(DC):
                            p.add("pe", lambda t, ps=ps, kc=kc, i=i: t.matmul(
                                ps[:, 0:E],
                                xn[:, kc * T + i * 128: kc * T + (i + 1) * 128],
                                wg_sb[:, kc * E:(kc + 1) * E],
                                start=(kc == 0), stop=(kc == DC - 1)),
                                waits=tuple(wl) if kc == 0 else (),
                                inc=PE if kc == DC - 1 else None)
                        pev = Plan.at(PE)

                        def pd(fn, extra=()):
                            p.add("dve", fn,
                                  waits=tuple(list(Plan.at(DV, AC)) + list(extra)),
                                  inc=DV)

                        pd(lambda v, ps=ps: v.tensor_copy(lg[:, :], ps[:, 0:E]),
                           extra=pev)
                        dvv = Plan.at(DV)
                        p.add("act", lambda s: s.activation(
                            pexp[:, :], lg[:, :], AF.Exp,
                            accum_out=rcol[:, 0:1]), waits=dvv, inc=AC)
                        pd(lambda v: v.reduce_max(rcol[:, 1:2], pexp[:, :], axis=AX))
                        pd(lambda v: v.reciprocal(rcol[:, 2:3], rcol[:, 0:1]))
                        pd(lambda v, i=i: v.tensor_mul(
                            gate_s[:, i:i + 1], rcol[:, 1:2], rcol[:, 2:3]))
                        pd(lambda v: v.reduce_max(rcol[:, 3:4], lg[:, :], axis=AX))
                        pd(lambda v: v.tensor_scalar(
                            t8[:, :], lg[:, :], rcol[:, 3:4], None, ALU.subtract))
                        pd(lambda v, i=i: v.tensor_single_scalar(
                            oh_s[:, i * E:(i + 1) * E], t8[:, :], 0.0, ALU.is_ge))
                        if DBG and i == 0:
                            p.add("pool", lambda g: g.dma_start(
                                out=dbg2_lg.ap(), in_=lg[:, :]),
                                waits=Plan.at(DV), inc=GD, by=16)
                            p.add("pool", lambda g: g.dma_start(
                                out=dbg2_pexp.ap(), in_=pexp[:, :]), inc=GD, by=16)
                            p.add("pool", lambda g: g.dma_start(
                                out=dbg2_rcol.ap(), in_=rcol[:, :]), inc=GD, by=16)
                            gddbg = Plan.at(GD)
                            p.add("dve", lambda v: v.engine_nop(), waits=gddbg)
                        p.add("dve", lambda v: v.engine_nop(), inc=DV)
                        dvo = Plan.at(DV)
                        p.add("pe", lambda t, i=i: t.transpose(
                            PS_C[0:E, 0:128], oh_s[:, i * E:(i + 1) * E],
                            c_ident_f[:, :]), waits=dvo, inc=PE)
                        pev2 = Plan.at(PE)
                        p.add("dve", lambda v, i=i: v.tensor_copy(
                            oh_T[:, i * 128:(i + 1) * 128], PS_C[0:E, 0:128]),
                            waits=pev2, inc=DV)
                    dva = Plan.at(DV)
                    p.add("dve", lambda v: v.reduce_sum(cnt_sb[0:E, 0:1], oh_T[:, :],
                                                        axis=AX), waits=dva, inc=DV)
                    dvc = Plan.at(DV)
                    p.add("pool", lambda g: g.dma_start(out=cnt_in.ap(),
                                                        in_=cnt_sb[0:E, 0:1]),
                          waits=dvc, inc=GD, by=16)
                    cnt_w = Plan.at(GD)
                    p.add("pool", lambda g: g.collective_compute(
                        "AllGather", ALU.bypass,
                        replica_groups=[list(range(NCORE))],
                        ins=[cnt_in.ap().opt()], outs=[cnt_all.ap().opt()]),
                        waits=cnt_w, inc=CC)
                    ccv = Plan.at(CC)
                    p.add("pool", lambda g: g.dma_start(out=cnt_sb[:, :],
                                                        in_=cnt_all.ap()),
                          waits=ccv, inc=GD, by=16)
                    cnt_l = Plan.at(GD)
                    p.add("pe", lambda t: t.matmul(PS_D[0:E, 0:1], c_rankmsk[:, :],
                                                   cnt_sb[:, :], start=True,
                                                   stop=True), waits=cnt_l, inc=PE)
                    pev = Plan.at(PE)
                    p.add("dve", lambda v: v.tensor_copy(offs_sb[:, :],
                                                         PS_D[0:E, 0:1]),
                          waits=pev, inc=DV)
                    dvo = Plan.at(DV)
                    p.add("dve", lambda v: v.tensor_tensor_scan(
                        incl_T[:, :], oh_T[:, :], oh_T[:, :], offs_sb[:, :],
                        ALU.add, ALU.bypass), waits=dvo, inc=DV)
                    scan_done = Plan.at(DV)
                    evac_marks = []
                    for i in range(NT):
                        wtr = list(scan_done) if i == 0 else []
                        if i >= 1:
                            wtr += list(evac_marks[i - 1])
                        p.add("pe", lambda t, i=i: t.transpose(
                            PS_C[:, 0:E], incl_T[:, i * 128:(i + 1) * 128],
                            c_ident_f[0:E, 0:E]),
                            waits=tuple(wtr), inc=PE)
                        pev = Plan.at(PE)

                        def pd(fn, extra=()):
                            p.add("dve", fn,
                                  waits=tuple(list(Plan.at(DV)) + list(extra)),
                                  inc=DV)

                        pd(lambda v: v.tensor_copy(inc_tm[:, :], PS_C[:, 0:E]),
                           extra=pev)
                        evac_marks.append(Plan.at(DV))
                        pd(lambda v, i=i: v.tensor_mul(
                            t8[:, :], oh_s[:, i * E:(i + 1) * E], inc_tm[:, :]))
                        pd(lambda v: v.reduce_sum(rcol[:, 0:1], t8[:, :], axis=AX))
                        pd(lambda v, i=i: v.tensor_mul(
                            t8[:, :], oh_s[:, i * E:(i + 1) * E], c_iota8[:, :]))
                        pd(lambda v: v.reduce_sum(rcol[:, 1:2], t8[:, :], axis=AX))
                        pd(lambda v: v.tensor_single_scalar(
                            rcol[:, 2:3], rcol[:, 0:1], float(CAP), ALU.is_le))
                        pd(lambda v, i=i: v.tensor_mul(
                            gk_s[:, i:i + 1], gate_s[:, i:i + 1], rcol[:, 2:3]))
                        pd(lambda v: v.scalar_tensor_tensor(
                            rcol[:, 3:4], rcol[:, 1:2], float(CAP), rcol[:, 0:1],
                            ALU.mult, ALU.add))
                        pd(lambda v: v.tensor_mul(
                            rcol[:, 4:5], rcol[:, 3:4], rcol[:, 2:3]))
                        pd(lambda v: v.tensor_scalar(
                            rcol[:, 5:6], rcol[:, 2:3], -1.0, 1.0, ALU.mult,
                            ALU.add))
                        pd(lambda v, i=i: v.tensor_scalar_add(
                            rcol[:, 6:7], c_tokid[:, i:i + 1], 50000.0))
                        pd(lambda v: v.tensor_mul(
                            rcol[:, 7:8], rcol[:, 5:6], rcol[:, 6:7]))
                        pd(lambda v, i=i: v.tensor_add(
                            sidx_u[:, i:i + 1], rcol[:, 4:5], rcol[:, 7:8]))
                        pd(lambda v: v.tensor_single_scalar(
                            rcol[:, 5:6], rcol[:, 4:5], 0.0, ALU.is_equal))
                        pd(lambda v: v.scalar_tensor_tensor(
                            rcol[:, 6:7], rcol[:, 5:6], 1.0e6, rcol[:, 4:5],
                            ALU.mult, ALU.add))
                        pd(lambda v, i=i: v.tensor_scalar(
                            retg_u[:, i:i + 1], rcol[:, 6:7], -1.0, None,
                            ALU.add))
                        dvs = Plan.at(DV)
                        p.add("pool", lambda g, i=i: g.indirect_dma_start(
                            out=slot_loc.ap(),
                            out_offset=IndirectOffsetOnAxis(sidx_u[:, i:i + 1], 0),
                            in_=c_tokid[:, i:i + 1], in_offset=None,
                            bounds_check=RB_SLOT, oob_is_err=False),
                            waits=tuple(list(dvs) + (list(slotz_done) if i == 0
                                                     else [])), inc=GD, by=16)
                    sc_done = Plan.at(GD)
                    if DBG:
                        for dst_t, src_ap in [
                                (dbg2_oh, oh_s[:, :]), (dbg2_ohT, oh_T[:, :]),
                                (dbg2_incl, incl_T[:, :]), (dbg2_cnt, cnt_sb[:, :]),
                                (dbg2_offs, offs_sb[:, :]),
                                (dbg2_sidx, sidx_u[:, :]),
                                (dbg2_retg, retg_u[:, :]), (dbg2_gk, gk_s[:, :])]:
                            p.add("pool", lambda g, a=dst_t, b=src_ap: g.dma_start(
                                out=a.ap(), in_=b), waits=sc_done, inc=GD, by=16)
                        sc_done = Plan.at(GD)
                    p.add("pool", lambda g: g.collective_compute(
                        "AllReduce", ALU.add,
                        replica_groups=[list(range(NCORE))],
                        ins=[slot_loc.ap().opt()], outs=[slot_all.ap().opt()]),
                        waits=sc_done, inc=CC)
                    _flush(nc, p)
                ar3_done = (CC, CC.n)

                h1s = contextlib.ExitStack()
                h_sl = h1s.enter_context(sb("h_sl", [128, HC * T], BF16))
                ffn_h_production(xn, sw1, sw3, h_sl)

            ffn_w2(h_sl, sw2, src_dram, None, True)
            h1s.close()

            with contextlib.ExitStack() as f6:
                xd = f6.enter_context(sb("xd_f", [128, DC * T], BF16, side="right"))
                with contextlib.ExitStack() as f6a:
                    sv = f6a.enter_context(sb("sv", [128, NT], F32))
                    svt = f6a.enter_context(sb("svt", [128, NT], F32))
                    svt2 = f6a.enter_context(sb("svt2", [128, NT], F32))
                    xgi = f6a.enter_context(sb("xgi", [128, NT], U32))
                    gt = [f6a.enter_context(sb(f"gt{i}", [128, D], BF16))
                          for i in range(2)]
                    p = Plan()
                    p.add("pool", lambda g: g.wait_ge(ar3_done[0].h, ar3_done[1])
                          or g.engine_nop())
                    for i in range(NT):
                        p.add("pool", lambda g, i=i: g.indirect_dma_start(
                            out=sv[:, i:i + 1], out_offset=None,
                            in_=slot_all.ap(),
                            in_offset=IndirectOffsetOnAxis(c_eslot[:, i:i + 1], 0),
                            bounds_check=RB_SLOTALL, oob_is_err=False),
                            inc=GD, by=16)
                    gdv = Plan.at(GD)
                    p.add("dve", lambda v: v.tensor_single_scalar(
                        svt[:, :], sv[:, :], 0.0, ALU.is_equal),
                        waits=gdv, inc=DV)
                    p.add("dve", lambda v: v.scalar_tensor_tensor(
                        svt2[:, :], svt[:, :], 1.0e6, sv[:, :],
                        ALU.mult, ALU.add), waits=Plan.at(DV), inc=DV)
                    p.add("dve", lambda v: v.tensor_scalar(
                        xgi[:, :], svt2[:, :], -1.0, None, ALU.add),
                        waits=Plan.at(DV), inc=DV)
                    xgi_done = Plan.at(DV)
                    p.add("pool", lambda g: g.wait_ge(ag1_done[0].h, ag1_done[1])
                          or g.engine_nop())
                    ld_marks = []
                    for i in range(NT):
                        wl = list(xgi_done)
                        if i >= 2:
                            wl += list(ld_marks[i - 2])
                        p.add("pool", lambda g, i=i: g.memset(gt[i % 2][:, :], 0.0),
                              waits=tuple(wl), inc=GD)
                        p.add("pool", lambda g, i=i: g.indirect_dma_start(
                            out=gt[i % 2][:, :], out_offset=None,
                            in_=xall.ap(),
                            in_offset=IndirectOffsetOnAxis(xgi[:, i:i + 1], 0),
                            bounds_check=RB_TOK, oob_is_err=False),
                            inc=GD, by=16)
                        gdi = Plan.at(GD)
                        p.add("sync", lambda s, i=i: s.dma_start(
                            out=xd_tm_d.ap()[i * 128:(i + 1) * 128, :],
                            in_=gt[i % 2][:, :]), waits=gdi, inc=LD, by=16)
                        ld_marks.append(Plan.at(LD))
                    all_rows = Plan.at(LD)
                    tr_marks = []
                    for c in range(DC):
                        wl = list(all_rows)
                        p.add("sync", lambda s, c=c: s.dma_start(
                            out=xd[:, c * T:(c + 1) * T],
                            in_=xd_tm_d.ap()[:, c * 128:(c + 1) * 128],
                            transpose=True), waits=tuple(wl), inc=LD, by=16)
                    p.add("sync", lambda s: s.wait_ge(LD.h, LD.n) or s.engine_nop())
                    _flush(nc, p)

                h2s = contextlib.ExitStack()
                h_sl2 = h2s.enter_context(sb("h_sl2", [128, HC * T], BF16))
                ffn_h_production(xd, w1, w3, h_sl2)

            ffn_w2(h_sl2, w2, None, ag_oe_in, False)
            h2s.close()

            with contextlib.ExitStack() as f9:
                rt = [f9.enter_context(sb(f"rt{i}", [128, D], BF16))
                      for i in range(2)]
                pcb = [f9.enter_context(sb(f"pcb{i}", [128, T], F32))
                      for i in range(2)]
                rstrip = [f9.enter_context(sb(f"rstrip{i}", [128, T], BF16))
                          for i in range(2)]
                xbf_s = [f9.enter_context(sb(f"fxbf{i}", [128, T], BF16))
                         for i in range(2)]
                sqbf_s = [f9.enter_context(sb(f"fsq{i}", [128, T], BF16))
                          for i in range(2)]
                fstat_marks = []
                p = Plan()
                p.add("pool", lambda g: g.wait_ge(CC.h, CC.n) or g.engine_nop())
                dv_marks = []
                st_marks_r = []
                for i in range(NT):
                    wl = []
                    if i >= 2:
                        wl += list(dv_marks[i - 2]) + list(st_marks_r[i - 2])
                    p.add("pool", lambda g, i=i: g.memset(rt[i % 2][:, :], 0.0),
                          waits=tuple(wl), inc=GD)
                    p.add("pool", lambda g, i=i: g.indirect_dma_start(
                        out=rt[i % 2][:, :], out_offset=None,
                        in_=oe_all.ap(),
                        in_offset=IndirectOffsetOnAxis(retg_u[:, i:i + 1], 0),
                        bounds_check=RB_TOK, oob_is_err=False),
                        inc=GD, by=16)
                    gdi = Plan.at(GD)
                    p.add("dve", lambda v, i=i: v.tensor_scalar(
                        rt[i % 2][:, :], rt[i % 2][:, :], gk_s[:, i:i + 1], None,
                        ALU.mult), waits=gdi, inc=DV)
                    dv_marks.append(Plan.at(DV))
                    p.add("sync", lambda s, i=i: s.dma_start(
                        out=ret_tm_d.ap()[i * 128:(i + 1) * 128, :],
                        in_=rt[i % 2][:, :]), waits=dv_marks[-1], inc=LD, by=16)
                    st_marks_r.append(Plan.at(LD))
                rows_done = Plan.at(LD)
                st_marks = []
                for c in range(DC):
                    wl = list(rows_done)
                    if c >= 2:
                        wl += list(st_marks[c - 2])
                    p.add("sync", lambda s, c=c: s.dma_start(
                        out=rstrip[c % 2][:, :],
                        in_=ret_tm_d.ap()[:, c * 128:(c + 1) * 128],
                        transpose=True), waits=tuple(wl), inc=LD, by=16)
                    wpc = []
                    if (not is_last) and c >= 2:
                        wpc += list(fstat_marks[c - 2])
                    p.add("sync", lambda s, c=c: s.dma_start(
                        out=pcb[c % 2][:, :],
                        in_=partial_d.ap()[c * 128:(c + 1) * 128, :]),
                        waits=tuple(wpc), inc=LD, by=16)
                    ldv = Plan.at(LD)
                    p.add("dve", lambda v, c=c: v.tensor_add(
                        pcb[c % 2][:, :], pcb[c % 2][:, :], rstrip[c % 2][:, :]),
                        waits=ldv, inc=DV)
                    if not is_last:
                        emit_fused_stats(p, (lambda i=c % 2: pcb[i][:, :]),
                                         c, xbf_s, sqbf_s, fstat_marks)
                    dvv = Plan.at(DV)
                    p.add("sync", lambda s, c=c: s.dma_start(
                        out=dst_dram.ap()[c * 128:(c + 1) * 128, :],
                        in_=pcb[c % 2][:, :]), waits=dvv, inc=LD, by=16)
                    st_marks.append(Plan.at(LD))
                p.add("sync", lambda s: s.wait_ge(LD.h, LD.n) or s.engine_nop())
                _flush(nc, p)

    def ffn_h_production(act_slab, wu1, wu3, h_sl):
        with contextlib.ExitStack() as hs:
            w1b = [hs.enter_context(sb(f"w1b{i}", [128, DC * 128], BF16))
                   for i in range(2)]
            w3b = [hs.enter_context(sb(f"w3b{i}", [128, DC * 128], BF16))
                   for i in range(2)]
            sil = hs.enter_context(sb("sil", [128, 1024], BF16))
            p = Plan()
            wu1_r = wu1.ap().rearrange("(k p) m -> p k m", p=128)
            wu3_r = wu3.ap().rearrange("(k p) m -> p k m", p=128)
            gd_marks = []
            ev_marks = []
            pe_marks_h = []
            for hc in range(HC):
                wl = pe_marks_h[hc - 2] if hc >= 2 else ()
                load_w(p, w1b[hc % 2][:, :].rearrange("p (k m) -> p k m", k=DC),
                       wu1_r[:, :, hc * 128:(hc + 1) * 128], waits=wl)
                load_w(p, w3b[hc % 2][:, :].rearrange("p (k m) -> p k m", k=DC),
                       wu3_r[:, :, hc * 128:(hc + 1) * 128])
                gd_marks.append(Plan.at(GD))
                for tt in range(T4):
                    g_i = hc * T4 + tt
                    psA, psB = ([PS_A, PS_B], [PS_C, PS_D])[g_i % 2]
                    silb = sil[:, (g_i % 2) * 512:(g_i % 2 + 1) * 512]
                    wl = list(gd_marks[-1])
                    if g_i >= 2:
                        wl += list(ev_marks[g_i - 2])
                    for kc in range(DC):
                        p.add("pe", lambda t, kc=kc, hc=hc, tt=tt, ps=psA: t.matmul(
                            ps[:, :],
                            w1b[hc % 2][:, kc * 128:(kc + 1) * 128],
                            act_slab[:, kc * T + tt * 512: kc * T + (tt + 1) * 512],
                            start=(kc == 0), stop=(kc == DC - 1)),
                            waits=tuple(wl) if kc == 0 else (),
                            inc=PE if kc == DC - 1 else None)
                    for kc in range(DC):
                        p.add("pe", lambda t, kc=kc, hc=hc, tt=tt, ps=psB: t.matmul(
                            ps[:, :],
                            w3b[hc % 2][:, kc * 128:(kc + 1) * 128],
                            act_slab[:, kc * T + tt * 512: kc * T + (tt + 1) * 512],
                            start=(kc == 0), stop=(kc == DC - 1)),
                            inc=PE if kc == DC - 1 else None)
                    pev = Plan.at(PE)
                    p.add("act", lambda s, ps=psA, sb_=silb: s.activation(
                        sb_, ps[:, :], AF.Silu), waits=pev, inc=AC)
                    acv = Plan.at(AC)
                    p.add("dve", lambda v, hc=hc, tt=tt, ps=psB, sb_=silb:
                          v.tensor_mul(
                              h_sl[:, hc * T + tt * 512: hc * T + (tt + 1) * 512],
                              ps[:, :], sb_), waits=acv, inc=DV)
                    ev_marks.append(Plan.at(DV))
                pe_marks_h.append(Plan.at(PE))
            _flush(nc, p)

    def ffn_w2(h_sl, wu2, resid_src, oe_ag_in, is_shared):
        h_done = Plan.at(DV)
        with contextlib.ExitStack() as w2s:
            w2b = [w2s.enter_context(sb(f"w2b{i}", [128, HC * 256], BF16))
                   for i in range(2)]
            xo = [w2s.enter_context(sb(f"xo2_{i}", [128, T],
                                       F32 if is_shared else BF16))
                  for i in range(2)]
            p = Plan()
            wu2_r = wu2.ap().rearrange("(k p) m -> p k m", p=128)
            psg = [PS_BIG[:, 0:512], PS_BIG[:, 512:1024],
                   PS_BIG[:, 1024:1536], PS_BIG[:, 1536:2048],
                   PS_A[:, :], PS_B[:, :], PS_C[:, :], PS_D[:, :]]
            gd_marks = []
            ld_marks = []
            dv_marks = []
            pe_marks_w = []
            for mp in range(8):
                wl = list(pe_marks_w[mp - 2]) if mp >= 2 else []
                load_w(p, w2b[mp % 2][:, :].rearrange("p (k m) -> p k m", k=HC),
                       wu2_r[:, :, mp * 256:(mp + 1) * 256], waits=tuple(wl))
                gd_marks.append(Plan.at(GD))
                if is_shared:
                    for j in range(2):
                        p.add("sync", lambda s, j=j, mp=mp: s.dma_start(
                            out=xo[j][:, :],
                            in_=resid_src.ap()[(mp * 2 + j) * 128:
                                               (mp * 2 + j + 1) * 128, :]),
                            waits=Plan.at(DV) if mp >= 1 else (),
                            inc=LD, by=16)
                ld_marks.append(Plan.at(LD))
                for hc in range(HC):
                    wl = list(gd_marks[-1]) + (list(h_done) if hc == 0 else [])
                    if hc == 0 and mp >= 1:
                        wl += list(Plan.at(DV))
                    for j in range(2):
                        for tt in range(T4):
                            p.add("pe", lambda t, hc=hc, j=j, tt=tt, mp=mp:
                                  t.matmul(
                                      psg[j * 4 + tt],
                                      w2b[mp % 2][:, hc * 256 + j * 128:
                                                  hc * 256 + (j + 1) * 128],
                                      h_sl[:, hc * T + tt * 512:
                                           hc * T + (tt + 1) * 512],
                                      start=(hc == 0), stop=(hc == HC - 1)),
                                  waits=tuple(wl) if (hc == 0 and j == 0
                                                      and tt == 0) else (),
                                  inc=PE if (hc == HC - 1 and j == 1
                                             and tt == T4 - 1) else None)
                pev = Plan.at(PE)
                pe_marks_w.append(pev)
                for j in range(2):
                    for tt in range(T4):
                        if is_shared:
                            p.add("dve", lambda v, j=j, tt=tt: v.tensor_add(
                                xo[j][:, tt * 512:(tt + 1) * 512],
                                psg[j * 4 + tt],
                                xo[j][:, tt * 512:(tt + 1) * 512]),
                                waits=tuple(list(pev) + list(ld_marks[-1]))
                                if tt == 0 and j == 0 else (),
                                inc=DV if (j == 1 and tt == T4 - 1) else None)
                        else:
                            wle = (tuple(list(pev) + list(ld_marks[-1]))
                                   if tt == 0 and j == 0 else ())
                            p.add("dve", lambda v, j=j, tt=tt: v.tensor_copy(
                                xo[j][:, tt * 512:(tt + 1) * 512],
                                psg[j * 4 + tt]),
                                waits=wle,
                                inc=DV if (j == 1 and tt == T4 - 1) else None)
                dv_marks.append(Plan.at(DV))
                for j in range(2):
                    p.add("sync", lambda s, j=j, mp=mp: s.dma_start(
                        out=(partial_d if is_shared else oe_fm_d).ap()
                        [(mp * 2 + j) * 128:(mp * 2 + j + 1) * 128, :],
                        in_=xo[j][:, :]), waits=dv_marks[-1], inc=LD, by=16)
                ld_marks[-1] = Plan.at(LD)
            if not is_shared:
                all_st = Plan.at(LD)
                for i in range(NT):
                    p.add("sync", lambda s, i=i: s.dma_start(
                        out=xo[i % 2][:, :],
                        in_=oe_fm_d.ap()[:, i * 128:(i + 1) * 128],
                        transpose=True), waits=all_st, inc=LD, by=16)
                    ldv = Plan.at(LD)
                    p.add("sync", lambda s, i=i: s.dma_start(
                        out=oe_ag_in.ap()[i * 128:(i + 1) * 128, :],
                        in_=xo[i % 2][:, :]), waits=ldv, inc=LD, by=16)
                agw = Plan.at(LD)
                p.add("pool", lambda g: g.collective_compute(
                    "AllGather", ALU.bypass,
                    replica_groups=[list(range(NCORE))],
                    ins=[oe_ag_in.ap().opt()], outs=[oe_all.ap().opt()]),
                    waits=agw, inc=CC)
            _flush(nc, p)


    cur = xfm_in
    dst = yfm if n_sub == 1 else xres[0]
    attn_a(cur, dst, stats_from_psum=False, fuse_next_stats=(n_sub >= 2))
    cur = dst
    if n_sub >= 2:
        dst = yfm if n_sub == 2 else xres[1]
        ffn(cur, dst, n_sub == 2, stats_from_psum=True)
        cur = dst
    if n_sub >= 3:
        dst = yfm if n_sub == 3 else xres[0]
        attn_b(cur, dst, stats_from_psum=True, fuse_next_stats=(n_sub >= 4))
        cur = dst
    if n_sub >= 4:
        ffn(cur, yfm, True, stats_from_psum=True)

    if DBG and n_sub >= 2:
        with contextlib.ExitStack() as ds:
            tmpf = ds.enter_context(sb("dbg_tmp", [128, T], F32))
            tmpb = ds.enter_context(sb("dbg_tmpb", [128, T], BF16))
            p = Plan()
            pairs = [(dbg_partial, partial_d, F32), (dbg_xd, xd_tm_d, BF16),
                     (dbg_oe, oe_fm_d, BF16), (dbg_ret, ret_tm_d, BF16),
                     (dbg_xn, xnfm_d, BF16)]
            for dst_t, src_t, dt_ in pairs:
                buf = tmpb[:, :] if dt_ == BF16 else tmpf[:, :]
                for c in range(DC):
                    wl = Plan.at(LD)
                    p.add("sync", lambda s, a=src_t, b=buf, c=c: s.dma_start(
                        out=b, in_=a.ap()[c * 128:(c + 1) * 128, :]),
                        waits=wl, inc=LD, by=16)
                    ldv = Plan.at(LD)
                    p.add("sync", lambda s, a=dst_t, b=buf, c=c: s.dma_start(
                        out=a.ap()[c * 128:(c + 1) * 128, :], in_=b),
                        waits=ldv, inc=LD, by=16)
            wl = Plan.at(LD)
            p.add("sync", lambda s: s.dma_start(
                out=tmpf[:, 0:129], in_=slot_all.ap().rearrange(
                    "(a b) c -> a (b c)", a=128)), waits=wl, inc=LD, by=16)
            ldv = Plan.at(LD)
            p.add("sync", lambda s: s.dma_start(
                out=dbg_slot.ap().rearrange("(a b) c -> a (b c)", a=128),
                in_=tmpf[:, 0:129]), waits=ldv, inc=LD, by=16)
            p.add("sync", lambda s: s.wait_ge(LD.h, LD.n) or s.engine_nop())
            _flush(nc, p)

    p = Plan()
    p.add("sync", lambda s: s.wait_ge(LD.h, LD.n) or s.engine_nop())
    _flush(nc, p)
    st.close()
    return nc



def _aux_inputs(core):
    rank = np.zeros((64, E), np.float32)
    for r in range(NCORE):
        if r < core:
            for e in range(E):
                rank[r * E + e, e] = 1.0
    tok = np.zeros((128, NT), np.float32)
    slot = np.zeros((128, NT), np.uint32)
    for tt in range(NT):
        for pp in range(128):
            t_loc = tt * 128 + pp
            tok[pp, tt] = core * T + t_loc + 1
            slot[pp, tt] = core * CAP + t_loc + 1
    return rank, tok, slot


def kernel(**inputs):
    def f32(a):
        return np.ascontiguousarray(np.asarray(a), dtype=np.float32)

    x = f32(inputs["L0_attn_a_x"])
    wqa = f32(inputs["L0_attn_a_Wqa"])
    wqb = f32(inputs["L0_attn_a_Wqb"])
    wkva = f32(inputs["L0_attn_a_Wkva"])
    wkvb = f32(inputs["L0_attn_a_Wkvb"])
    woa = f32(inputs["L0_attn_a_Wo"])
    wq = f32(inputs["L0_attn_b_w_q"])
    wk = f32(inputs["L0_attn_b_w_k"])
    wv = f32(inputs["L0_attn_b_w_v"])
    wob = f32(inputs["L0_attn_b_w_o"])
    wg = f32(inputs["L0_ffn_Wg"])
    w1 = f32(inputs["L0_ffn_W1"])
    w3 = f32(inputs["L0_ffn_W3"])
    w2 = f32(inputs["L0_ffn_W2"])
    sw1 = f32(inputs["L0_ffn_sw1"])
    sw3 = f32(inputs["L0_ffn_sw3"])
    sw2 = f32(inputs["L0_ffn_sw2"])

    wq_n = np.ascontiguousarray(wq.T)
    wk_n = np.ascontiguousarray(wk.T)
    wv_n = np.ascontiguousarray(wv.T)
    wob_n = np.ascontiguousarray(wob.T)

    nc = build_nc(N_SUB)
    in_maps = []
    for c in range(NCORE):
        rank, tok, slot = _aux_inputs(c)
        in_maps.append(dict(
            xfm=np.ascontiguousarray(x[c].T),
            wqa=wqa, wqb=wqb, wkva=wkva, wkvb=wkvb, woa=woa,
            wq=wq_n, wk=wk_n, wv=wv_n, wob=wob_n,
            wg=wg, w1=np.ascontiguousarray(w1[c]),
            w3=np.ascontiguousarray(w3[c]), w2=np.ascontiguousarray(w2[c]),
            sw1=sw1, sw3=sw3, sw2=sw2,
            rankmsk=rank, tokid=tok, eslot=slot,
        ))
    res = run_bass_kernel_spmd(nc, in_maps, core_ids=list(range(NCORE)))
    out = np.stack([np.asarray(r["yfm"]).T for r in res.results], axis=0)
    if DBG:
        kernel.dbg = res.results
    return out
